# version 01
# baseline (speedup 1.0000x reference)
import numpy as np
import ml_dtypes
from contextlib import ExitStack
import concourse.bass as bass
import concourse.mybir as mybir
from concourse.bass_utils import run_bass_kernel_spmd

F32 = mybir.dt.float32
BF16 = mybir.dt.bfloat16
I32 = mybir.dt.int32
AF = mybir.ActivationFunctionType
ALU = mybir.AluOpType
AX = mybir.AxisListType

T = 4096
D = 1024
NTT = T // 128
KD = D // 128
TBL = 512
NTB = T // TBL
PI = float(np.pi)
INV2PI = float(1.0 / (2 * np.pi))
TWOPI = float(2 * np.pi)
GCONST = 0.7978845608028654

FL_NH = 16
FL_NQT = None
ALIBI_SLOPES = [float((2.0 ** (-0.5)) ** (i + 1)) for i in range(16)]
EGH = [1, 1, 2] + [4] * 13



def _perm_reim():
    p = np.zeros(D, np.int64)
    for r in range(D):
        h, u = divmod(r, 64)
        p[r] = h * 64 + 2 * u if u < 32 else h * 64 + 2 * (u - 32) + 1
    return p


def _headdup(base):
    base = np.asarray(base, np.float32)
    out = np.zeros(1024, np.float32)
    for r in range(1024):
        h, u = divmod(r, 64)
        out[r] = base[h, u % 32]
    return out


def host_consts():
    c = {}
    c["identf"] = np.eye(128, dtype=np.float32)
    su = np.triu(np.ones((64, 64), np.float32))
    mg = np.zeros((128, 128), np.float32)
    for a in range(2):
        for b in range(2):
            mg[a * 64:(a + 1) * 64, b * 64:(b + 1) * 64] = su
    c["maskG"] = mg
    fm = np.where(np.arange(128)[None, :] <= np.arange(128)[:, None], 0.0, -1e30)
    c["maskF"] = fm.astype(np.float32)
    sr = np.ones((128, 2048), np.float32)
    sr[:, ::64] = 0.0
    sr[:, 0] = 1.0
    sr[:, ::64] = 0.0
    c["scanrst"] = sr
    uf = np.zeros((128, 64), np.float32)
    s = np.arange(128, dtype=np.float64)
    for h in range(16):
        sl, eg = ALIBI_SLOPES[h], EGH[h]
        for r in range(4):
            fix = 0.0 if eg == 1 else (128.0 * (r % 2) - 64.0 if eg == 2 else 128.0 * r - 192.0)
            uf[:, h * 4 + r] = np.exp(sl * (s - 64.0 + fix)).astype(np.float32)
    c["uflash"] = uf
    sh = np.zeros((128, 128), np.float32)
    for h in range(16):
        for r in range(8):
            sh[:, h * 8 + r] = ALIBI_SLOPES[h] * (128.0 * r + 64.0)
    c["shflash"] = sh
    return c


def pack_pcol(inp):
    cols = {}
    buf = []

    def add(name, vec):
        vec = np.asarray(vec, np.float32).reshape(-1)
        ntile = (len(vec) + 127) // 128
        cols[name] = len(buf)
        for i in range(ntile):
            v = np.zeros(128, np.float32)
            v[: min(128, len(vec) - i * 128)] = vec[i * 128:(i + 1) * 128]
            buf.append(v)

    perm = _perm_reim()
    add("bq", np.asarray(inp["L0_attn_bq"], np.float32)[perm])
    add("bk", np.asarray(inp["L0_attn_bk"], np.float32)[perm])
    add("bv", inp["L0_attn_bv"])
    add("bo", inp["L0_attn_bo"])
    add("bf1", inp["L0_attn_bf1"])
    add("bf2", inp["L0_attn_bf2"])
    add("n1g", inp["L0_attn_n1g"])
    add("n1b", inp["L0_attn_n1b"])
    add("n2g", inp["L0_attn_n2g"])
    add("n2b", inp["L0_attn_n2b"])
    add("batt", inp["L1_attn_b_attn"])
    add("battqs", np.zeros(1024, np.float32))
    add("bproj", inp["L1_attn_b_proj"])
    add("thb", _headdup(inp["L0_attn_theta_base"]))
    add("lamb", _headdup(inp["L0_attn_lambda_base"]))
    add("eta", _headdup(inp["L0_attn_eta"]))
    sg = np.ones(128, np.float32)
    sg[32:64] = -1.0
    sg[96:128] = -1.0
    add("sgnri", sg)
    add("c_i127sq", np.full(128, 1.0 / (127.0 * 127.0), np.float32))
    add("c_neg1", np.full(128, -1.0, np.float32))
    add("c_i127", np.full(128, 1.0 / 127.0, np.float32))
    add("c_gc", np.full(128, 0.7978845608028654, np.float32))
    add("c_eps5", np.full(128, 1e-5, np.float32))
    add("c_eps6", np.full(128, 1e-6, np.float32))
    add("c_one", np.ones(128, np.float32))
    arr = np.stack(buf, axis=1)
    assert arr.shape[1] <= 192, arr.shape
    pad = np.zeros((128, 192), np.float32)
    pad[:, :arr.shape[1]] = arr
    return pad, cols


def prep_inputs(inp):
    perm = _perm_reim()
    g = lambda k: np.ascontiguousarray(np.asarray(inp[k], np.float32))
    com = {}
    com["Wq"] = np.ascontiguousarray(g("L0_attn_Wq")[:, perm])
    com["Wk"] = np.ascontiguousarray(g("L0_attn_Wk")[:, perm])
    com["Wv"] = g("L0_attn_Wv")
    com["Wo"] = g("L0_attn_Wo")
    com["Wf1"] = g("L0_attn_Wf1")
    com["Wf2"] = g("L0_attn_Wf2")
    com["thw1"] = g("L0_attn_theta_w1")
    w2 = g("L0_attn_theta_w2")
    w2d = np.zeros((48, 1024), np.float32)
    for r in range(1024):
        h, u = divmod(r, 64)
        w2d[:, r] = w2[:, h * 32 + (u % 32)]
    com["thw2"] = np.ascontiguousarray(w2d)
    com["wgt"] = np.ascontiguousarray(np.asarray(inp["L0_ffn_w_gate"], np.int32).T)
    com["wut"] = np.ascontiguousarray(np.asarray(inp["L0_ffn_w_up"], np.int32).T)
    com["wdt"] = np.ascontiguousarray(np.asarray(inp["L0_ffn_w_down"], np.int32).T)
    com["Watt"] = g("L1_attn_W_attn")
    com["Wproj"] = g("L1_attn_W_proj")
    com["temp"] = g("L1_attn_temperature").reshape(1, 1)
    com["gwt"] = np.ascontiguousarray(g("L1_ffn_gate_w").T)
    com["gwt_lo"] = (com["gwt"] -
                     com["gwt"].astype(ml_dtypes.bfloat16).astype(np.float32))
    com["gu"] = g("L1_ffn_gate_up_proj")
    com["dn"] = g("L1_ffn_down_proj")
    com.update(host_consts())
    pc, pcols = pack_pcol(inp)
    com["pcol"] = pc
    x = np.ascontiguousarray(np.asarray(inp["L0_attn_x"], np.float32))
    return com, pcols, [np.ascontiguousarray(x[i]) for i in range(8)]


DT_MAP = {"wgt": I32, "wut": I32, "wdt": I32}

SCRATCH = ["sc_xF", "sc_xnF", "sc_xn2F", "sc_qF", "sc_kF", "sc_vF", "sc_attnF",
           "sc_x1F", "sc_h2F", "sc_A0F", "sc_F0F", "sc_xl1F", "sc_xn1F",
           "sc_accF", "sc_F1F"]
SCRATCH_SHAPES = {"sc_loraF": (8, 128, T), "sc_q1F": (8, 128, T),
                  "sc_k1F": (8, 128, T), "sc_v1F": (8, 128, T),
                  "sc_y1F": (8, 128, T), "sc_r": (8, T)}



class Sem:
    __slots__ = ("h", "n", "sw")

    def __init__(self, h):
        self.h = h
        self.n = 0

    def inc(self, k=1):
        self.n += k
        return self.n


class Ctx:
    def __init__(self, nc, es, dev_external=False):
        self.nc = nc
        self.es = es
        self.dev_external = dev_external
        self.d = {}
        self.sb = {}
        self.ph = 0
        self.sems = {}
        self.gsems = {}
        self.sem_pool = [es.enter_context(nc.semaphore(f"sp{i}")) for i in range(55)]
        self.pool_free = list(self.sem_pool)
        self.wait_base = {}
        self.MEGA = 43008
        self.mega = es.enter_context(nc.sbuf_tensor("mega", [128, self.MEGA], F32))
        self.mega_off = 0
        self.mega_stack = []
        self.psum = [es.enter_context(nc.psum_tensor(f"pb{i}", [128, 512], F32))
                     for i in range(8)]
        self.block = None
        self.PH = None
    def vmark(self, eng, semh, k=1):
        eng.drain().then_inc(semh, k)

    def smark(self, eng, semh, k=1):
        eng.drain().then_inc(semh, k)

    def vedge(self, eng):
        eng.drain()

    def sedge(self, eng):
        eng.drain()

    def gsem(self, name):
        if name not in self.gsems:
            self.gsems[name] = Sem(self.es.enter_context(self.nc.semaphore(name)))
        return self.gsems[name]

    def sem(self, name, sw=False):
        if name in ("PH", "PH2", "vchain", "schain"):
            return self.gsem(name)
        if name not in self.sems:
            if sw:
                h = self.es.enter_context(self.nc.semaphore(name))
                s = Sem(h)
                s.sw = True
                self.sems[name] = s
            else:
                if not self.pool_free:
                    raise RuntimeError("sem pool exhausted")
                h = self.pool_free.pop()
                s = Sem(h)
                s.sw = False
                self.sems[name] = s
        return self.sems[name]

    def phase_reset_sems(self):
        used = [s for s in self.sems.values() if not getattr(s, "sw", False)]
        for s in used:
            self.wait_base[s.h.num] = self.wait_base.get(s.h.num, 0) + s.n
        self.pool_free.extend(s.h for s in used)
        self.sems = {}
        return used

    def alloc(self, words, name=""):
        off = self.mega_off
        words = (words + 1) & ~1
        assert off + words <= self.MEGA, f"SBUF overflow at {name}: {off}+{words}"
        self.mega_off += words
        return off

    def region(self, rows, cols, dt=F32, name=""):
        sz = mybir.dt.size(dt)
        w = (cols * sz + 3) // 4
        off = self.alloc(w, name)
        ap = self.mega[0:rows, off:off + w]
        if dt != F32:
            ap = ap.bitcast(dt)[0:rows, 0:cols]
        return ap

    def push(self):
        self.mega_stack.append(self.mega_off)

    def pop(self):
        self.mega_off = self.mega_stack.pop()

    def pcol(self, name, tile=0):
        i = self.pcols[name] + tile
        return self.sb["pcol"][:, i:i + 1]



_GUARD_OPS = {
    "tensor_tensor": ([0], [1, 2]),
    "tensor_scalar": ([0], [1, 2, 3]),
    "scalar_tensor_tensor": ([0], [1, 2, 3]),
    "tensor_copy": ([0], [1]),
    "tensor_tensor_scan": ([0], [1, 2, 3]),
    "reduce_max": ([0], [1]),
    "reduce_sum": ([0], [1]),
    "tensor_reduce": ([0], [1]),
    "reciprocal": ([0], [1]),
    "memset": ([0], []),
    "memzero": ([0], []),
    "select": ([0], [1, 2, 3]),
    "activation": ([0], [1]),
    "square": ([0], [1]),
    "sqrt": ([0], [1]),
    "copy": ([0], [1]),
    "add": ([0], [1, 2]),
    "mul": ([0], [1, 2]),
    "sign": ([0], [1]),
}


def _ap_span(x):
    try:
        t = x.tensor
        name = t.name
        dt_sz = mybir.dt.size(x.dtype)
        off = x.offset * dt_sz
        nb = x.free_nbytes if hasattr(x, "free_nbytes") else x.nbytes
        bp = x.base_partition() if callable(getattr(x, "base_partition", None)) else 0
        np_ = x.shape[0] if len(x.shape) else 128
        return (name, bp, bp + np_, off, off + max(int(nb), dt_sz))
    except Exception:
        return ("?", 0, 128, 0, 1 << 40)


def _overlap(a, b):
    return (a[0] == b[0] and a[1] < b[2] and b[1] < a[2]
            and a[3] < b[4] and b[3] < a[4])


class EngGuard:
    def __init__(self, eng):
        self._e = eng
        self._hist = []

    def __getattr__(self, name):
        fn = getattr(self._e, name)
        spec = _GUARD_OPS.get(name)
        if spec is None:
            return fn
        outs_i, ins_i = spec

        def wrapped(*args, **kwargs):
            aps = []
            for i in ins_i:
                if i < len(args) and hasattr(args[i], "tensor"):
                    aps.append((0, _ap_span(args[i])))
            for kw in ("bias", "scale"):
                if hasattr(kwargs.get(kw), "tensor"):
                    aps.append((0, _ap_span(kwargs[kw])))
            for i in outs_i:
                if i < len(args) and hasattr(args[i], "tensor"):
                    aps.append((1, _ap_span(args[i])))
            if hasattr(kwargs.get("accum_out"), "tensor"):
                aps.append((1, _ap_span(kwargs["accum_out"])))
            need = False
            for is_w, sp in aps:
                for h_w, h_sp in self._hist:
                    if (is_w or h_w) and _overlap(sp, h_sp):
                        need = True
                        break
                if need:
                    break
            if need or len(self._hist) > 256:
                self._e.drain()
                self._hist = []
            self._hist.extend(aps)
            return fn(*args, **kwargs)
        return wrapped

    def drain(self, *a, **k):
        self._hist = []
        return self._e.drain(*a, **k)


class WaitShim:

    def __init__(self, eng, c):
        self._e = eng
        self._c = c

    def __getattr__(self, name):
        return getattr(self._e, name)

    def wait_ge(self, h, v):
        return self._e.wait_ge(h, v + self._c.wait_base.get(h.num, 0))


ENGS = ["sync", "gpsimd", "vector", "scalar", "tensor"]


def phase(c, fns):
    PH = c.gsem("PH")
    PH2 = c.gsem("PH2")
    for e in ENGS:
        fn = fns.get(e)
        sec = getattr(c.block, e)

        def body(eng, fn=fn, e=e):
            eng = WaitShim(eng, c)
            if e in ("vector", "scalar"):
                eng = EngGuard(eng)
            if c.ph > 0:
                eng.wait_ge(PH.h, 5 * c.ph)
            if fn is not None:
                fn(eng)
            if e == "vector":
                c.vmark(eng, PH.h, 1)
            elif e == "scalar":
                c.smark(eng, PH.h, 1)
            elif e == "tensor":
                eng.drain().then_inc(PH.h, 1)
            else:
                eng.nop().then_inc(PH.h, 1)
        sec(body)
    c.ph += 1
    PH.n = 5 * c.ph
    c.phase_reset_sems()


def declare_tensors(c, com, dev_external=False):
    nc = c.nc
    c.d["x"] = nc.dram_tensor("x", [T, D], F32, kind="ExternalInput").ap()
    for k, v in com.items():
        dt = DT_MAP.get(k, F32)
        c.d[k] = nc.dram_tensor(k, list(v.shape), dt, kind="ExternalInput").ap()
    kind = "ExternalOutput" if dev_external else "Internal"
    for s in SCRATCH:
        c.d[s] = nc.dram_tensor(s, [KD, 128, T], F32, kind=kind).ap()
    for s, shp in SCRATCH_SHAPES.items():
        c.d[s] = nc.dram_tensor(s, list(shp), F32, kind=kind).ap()
    c.d["out"] = nc.dram_tensor("out", [T, D], F32, kind="ExternalOutput").ap()



def setup_regions(c):
    c.sb["pcol"] = c.region(128, 192, F32, "pcol")
    c.sb["identf"] = c.region(128, 128, F32, "identf")
    c.sb["identb"] = c.region(128, 128, BF16, "identb")
    c.sb["maskG"] = c.region(128, 128, F32, "maskG")
    c.sb["maskF"] = c.region(128, 128, F32, "maskF")
    c.sb["scanrst"] = c.region(128, 2048, F32, "scanrst")
    c.sb["uflash"] = c.region(128, 64, F32, "uflash")
    c.sb["shflash"] = c.region(128, 128, F32, "shflash")
    c.sb["temp"] = c.region(128, 4, F32, "temp")
    c.sb["onesb"] = c.region(128, 2, BF16, "onesb")
    c.sb["onesrow"] = c.region(1, 128, F32, "onesrow")
    for val, col in [(-1.0, "c_neg1"), (1.0 / 127.0, "c_i127"),
                     (0.7978845608028654, "c_gc")]:
        c.nc.const_aps.register(F32, val, c.pcol(col))


def ph_setup(c):
    ld = c.sem("s0ld")
    vs = c.sem("s0v")

    def fsync(sync):
        for nm, dst in [("pcol", c.sb["pcol"]), ("identf", c.sb["identf"]),
                        ("maskG", c.sb["maskG"]), ("maskF", c.sb["maskF"]),
                        ("scanrst", c.sb["scanrst"]), ("uflash", c.sb["uflash"]),
                        ("shflash", c.sb["shflash"])]:
            sync.dma_start(out=dst, in_=c.d[nm]).then_inc(ld.h, 16)
            ld.inc(16)
        sync.dma_start(out=c.sb["temp"][:, 0:1],
                       in_=c.d["temp"].broadcast_to([128, 1])).then_inc(ld.h, 16)
        ld.inc(16)
        sync.wait_ge(vs.h, 1)

    ldg = c.sem("s0ldg", sw=True)

    def fgps(g):
        g.dma_start(out=c.sb["identb"], in_=c.d["identf"]).then_inc(ldg.h, 16)
        ldg.inc(16)

    def fvec(v):
        v.wait_ge(ld.h, ld.n)
        v.wait_ge(ldg.h, ldg.n)
        v.tensor_scalar(c.sb["temp"][:, 1:2], c.sb["temp"][:, 0:1], 8.0, 0.0,
                        ALU.mult, ALU.add)
        c.vedge(v)
        v.reciprocal(c.sb["temp"][:, 2:3], c.sb["temp"][:, 1:2])
        c.vedge(v)
        for i in range(8):
            v.tensor_scalar(c.pcol("battqs", i), c.pcol("batt", i),
                            c.sb["temp"][:, 2:3], 0.0, ALU.mult, ALU.add)
        v.memset(c.sb["onesb"], 1.0)
        v.memset(c.sb["onesrow"], 1.0)
        c.vmark(v, vs.h, 1)
        vs.inc()

    phase(c, {"sync": fsync, "gpsimd": fgps, "vector": fvec})



def ph_ln0(c):
    c.push()
    xin = [c.region(128, 1024, F32, "xin") for _ in range(2)]
    xnsb = [c.region(128, 1024, F32, "xn") for _ in range(2)]
    xnnsb = [c.region(128, 1024, F32, "xnn") for _ in range(2)]
    stage = [c.region(128, 1024, F32, "stg") for _ in range(6)]
    x2 = c.region(128, 1024, F32, "x2")
    st = c.region(128, 16, F32, "stats")
    ld2 = [c.sem("l0ld0"), c.sem("l0ld1")]
    vst = c.sem("l0v")
    sst = c.sem("l0s")
    tst = c.sem("l0t")
    est = c.sem("l0e")
    dst2 = [c.sem("l0d0"), c.sem("l0d1")]
    vv = c.sem("l0vv")

    xd = c.d["x"]

    def fsync(sync):
        sync.dma_start(out=xin[0], in_=xd[0:128, :]).then_inc(ld2[0].h, 16)
        ld2[0].inc(16)
        for tt in range(NTT):
            if tt + 1 < NTT:
                if tt >= 1:
                    sync.wait_ge(vst.h, 2 * (tt - 1) + 1)
                    sync.wait_ge(sst.h, 4 * (tt - 1) + 1)
                    sync.wait_ge(tst.h, 3 * (tt - 1) + 1)
                sync.dma_start(out=xin[(tt + 1) % 2],
                               in_=xd[(tt + 1) * 128:(tt + 2) * 128, :]).then_inc(ld2[(tt + 1) % 2].h, 16)
                ld2[(tt + 1) % 2].inc(16)
            sync.wait_ge(est.h, 3 * tt + 3)
            for k, scn in enumerate(["sc_xF", "sc_xnF", "sc_xn2F"]):
                sg = stage[k * 2 + tt % 2]
                dest = c.d[scn].rearrange("a p t -> p a t")[:, :, tt * 128:(tt + 1) * 128]
                sync.dma_start(out=dest, in_=sg.rearrange("p (a t) -> p a t", a=KD)).then_inc(dst2[tt % 2].h, 16)
                dst2[tt % 2].inc(16)
        sync.wait_ge(dst2[0].h, dst2[0].n)
        sync.wait_ge(dst2[1].h, dst2[1].n)

    def fvec(v):
        for tt in range(NTT):
            par = tt % 2
            v.wait_ge(ld2[par].h, 16 * (tt // 2 + 1))
            v.reduce_sum(st[:, 0:1], xin[par], axis=AX.X)
            c.vedge(v)
            v.tensor_scalar(st[:, 1:2], st[:, 0:1], 1.0 / D, 0.0, ALU.mult, ALU.add)
            v.wait_ge(sst.h, 4 * tt + 1)
            v.tensor_scalar(st[:, 3:4], st[:, 2:3], 1.0 / D, 0.0, ALU.mult, ALU.add)
            v.tensor_tensor(st[:, 4:5], st[:, 1:2], st[:, 1:2], ALU.mult)
            c.vedge(v)
            v.tensor_tensor(st[:, 5:6], st[:, 3:4], st[:, 4:5], ALU.subtract).then_inc(vv.h, 1)
            vv.inc()
            v.wait_ge(sst.h, 4 * tt + 2)
            v.reciprocal(st[:, 7:8], st[:, 6:7])
            c.vedge(v)
            v.tensor_scalar(xnsb[par], xin[par], st[:, 1:2], st[:, 7:8],
                            ALU.subtract, ALU.mult).then_inc(vst.h, 1)
            vst.inc()
            v.reduce_sum(st[:, 8:9], xnsb[par], axis=AX.X)
            c.vedge(v)
            v.tensor_scalar(st[:, 9:10], st[:, 8:9], 1.0 / D, 0.0, ALU.mult, ALU.add)
            v.wait_ge(sst.h, 4 * tt + 3)
            v.tensor_scalar(st[:, 11:12], st[:, 10:11], 1.0 / D, 0.0, ALU.mult, ALU.add)
            v.tensor_tensor(st[:, 12:13], st[:, 9:10], st[:, 9:10], ALU.mult)
            c.vedge(v)
            v.tensor_tensor(st[:, 13:14], st[:, 11:12], st[:, 12:13], ALU.subtract).then_inc(vv.h, 1)
            vv.inc()
            v.wait_ge(sst.h, 4 * tt + 4)
            v.reciprocal(st[:, 15:16], st[:, 14:15])
            c.vedge(v)
            v.tensor_scalar(xnnsb[par], xnsb[par], st[:, 9:10], st[:, 15:16],
                            ALU.subtract, ALU.mult).then_inc(vst.h, 1)
            vst.inc()
            if tt >= 2:
                v.wait_ge(dst2[par].h, 48 * (tt // 2))
            for k in range(3):
                v.wait_ge(tst.h, 3 * tt + k + 1)
                bank = c.psum[2 * k].bitcast(F32)
                bank2 = c.psum[2 * k + 1].bitcast(F32)
                sg = stage[k * 2 + par]
                if k == 2:
                    for ki in range(KD):
                        b = (bank if ki < 4 else bank2)
                        v.tensor_scalar(sg[:, ki * 128:(ki + 1) * 128],
                                        b[:, (ki % 4) * 128:(ki % 4 + 1) * 128],
                                        c.pcol("n1g", ki), c.pcol("n1b", ki),
                                        ALU.mult, ALU.add)
                else:
                    v.tensor_copy(sg[:, 0:512], bank[:, 0:512])
                    v.tensor_copy(sg[:, 512:1024], bank2[:, 0:512])
                c.vmark(v, est.h, 1)
                est.inc()

    def fscal(s):
        for tt in range(NTT):
            par = tt % 2
            s.wait_ge(ld2[par].h, 16 * (tt // 2 + 1))
            if tt >= 1:
                s.wait_ge(vv.h, 2 * tt - 1)
            s.activation(x2, xin[par], AF.Square, accum_out=st[:, 2:3]).then_inc(sst.h, 1)
            sst.inc()
            s.wait_ge(vv.h, 2 * tt + 1)
            s.activation(st[:, 6:7], st[:, 5:6], AF.Sqrt,
                         bias=c.pcol("c_eps5"), scale=1.0).then_inc(sst.h, 1)
            sst.inc()
            s.wait_ge(vst.h, 2 * tt + 1)
            if tt >= 1:
                s.wait_ge(vv.h, 2 * tt)
            s.activation(x2, xnsb[par], AF.Square, accum_out=st[:, 10:11]).then_inc(sst.h, 1)
            sst.inc()
            s.wait_ge(vv.h, 2 * tt + 2)
            s.activation(st[:, 14:15], st[:, 13:14], AF.Sqrt,
                         bias=c.pcol("c_eps6"), scale=1.0).then_inc(sst.h, 1)
            sst.inc()

    def ften(t):
        for tt in range(NTT):
            par = tt % 2
            srcs = [xin[par], xnsb[par], xnnsb[par]]
            waits = [(ld2[par], 16 * (tt // 2 + 1)), (vst, 2 * tt + 1), (vst, 2 * tt + 2)]
            for k in range(3):
                sem, val = waits[k]
                t.wait_ge(sem.h, val)
                if tt >= 1:
                    t.wait_ge(est.h, 3 * (tt - 1) + k + 1)
                bank = c.psum[2 * k].bitcast(F32)
                bank2 = c.psum[2 * k + 1].bitcast(F32)
                for ki in range(KD):
                    b = (bank if ki < 4 else bank2)
                    t.transpose(b[:, (ki % 4) * 128:(ki % 4 + 1) * 128],
                                srcs[k][:, ki * 128:(ki + 1) * 128], c.sb["identf"])
                t.drain().then_inc(tst.h, 1)
                tst.inc()

    phase(c, {"sync": fsync, "vector": fvec, "scalar": fscal, "tensor": ften})
    c.pop()



def emit_gemm(c, phname, xkey, xt, jobs):
    Kt = xt
    xreg = c.region(128, xt * T, BF16, phname + "x")
    wbuf = [c.region(128, Kt * 512, BF16, phname + "w") for _ in range(2)]
    stage = [c.region(128, 512, F32, phname + "s") for _ in range(8)]
    addb = [c.region(128, 512, F32, phname + "a") for _ in range(4)]
    xld = c.sem(phname + "xld", sw=True)
    wld2 = [c.sem(phname + "wld0", sw=True), c.sem(phname + "wld1", sw=True)]
    wfree = c.sem(phname + "wfree")
    mmS = c.sem(phname + "mm")
    ev = c.sem(phname + "ev")
    od2 = [c.sem(phname + "od0"), c.sem(phname + "od1")]
    ald = c.sem(phname + "ald")

    units = []
    for j, job in enumerate(jobs):
        for ns in range(job["N"] // 512):
            units.append((j, ns))
    nsteps_per_unit = 2 * 4
    has_add = any(job.get("add_from") for job in jobs)

    def fgps(g):
        for ki in range(xt):
            g.dma_start(out=xreg[:, ki * T:(ki + 1) * T],
                        in_=c.d[xkey][ki]).then_inc(xld.h, 16)
            xld.inc(16)
        for u, (j, ns) in enumerate(units):
            job = jobs[j]
            if u >= 2:
                g.wait_ge(wfree.h, u - 1)
            W = c.d[job["wkey"]]
            n0 = job.get("noff", 0) + ns * 512
            for ki in range(Kt):
                g.dma_start(out=wbuf[u % 2][:, ki * 512:(ki + 1) * 512],
                            in_=W[ki * 128:(ki + 1) * 128, n0:n0 + 512]).then_inc(wld2[u % 2].h, 16)
                wld2[u % 2].inc(16)

    def fsync(sync):
        s = 0
        for u, (j, ns) in enumerate(units):
            job = jobs[j]
            outk = job["out"]
            for tg in range(2):
                for nt4 in range(4):
                    nt_g = ns * 4 + nt4
                    if job.get("add_from"):
                        if s >= 1:
                            sync.wait_ge(ev.h, 4 * s)
                        for tb4 in range(4):
                            tb = tg * 4 + tb4
                            sync.dma_start(
                                out=addb[tb4],
                                in_=c.d[job["add_from"]][nt_g][:, tb * 512:(tb + 1) * 512]
                            ).then_inc(ald.h, 16)
                            ald.inc(16)
                    sync.wait_ge(ev.h, 4 * (s + 1))
                    for tb4 in range(4):
                        tb = tg * 4 + tb4
                        sync.dma_start(
                            out=c.d[outk][nt_g][:, tb * 512:(tb + 1) * 512],
                            in_=stage[(s % 2) * 4 + tb4]).then_inc(od2[s % 2].h, 16)
                        od2[s % 2].inc(16)
                    s += 1
        sync.wait_ge(od2[0].h, od2[0].n)
        sync.wait_ge(od2[1].h, od2[1].n)

    def ften(t):
        t.wait_ge(xld.h, 16 * xt)
        s = 0
        for u, (j, ns) in enumerate(units):
            job = jobs[j]
            t.wait_ge(wld2[u % 2].h, 16 * Kt * (u // 2 + 1))
            for tg in range(2):
                for nt4 in range(4):
                    if s >= 2:
                        t.wait_ge(ev.h, 4 * (s - 1))
                    banks = [c.psum[(s % 2) * 4 + i] for i in range(4)]
                    for ki in range(Kt):
                        lhs = wbuf[u % 2][:, ki * 512 + nt4 * 128: ki * 512 + (nt4 + 1) * 128]
                        for tb4 in range(4):
                            tb = tg * 4 + tb4
                            t.matmul(banks[tb4][:, 0:512],
                                     lhs,
                                     xreg[:, ki * T + tb * 512: ki * T + (tb + 1) * 512],
                                     start=(ki == 0), stop=(ki == Kt - 1),
                                     skip_group_check=True)
                    t.drain().then_inc(mmS.h, 1)
                    mmS.inc()
                    s += 1
            t.drain().then_inc(wfree.h, 1)
            wfree.inc()

    assert len({bool(job.get("add_from")) for job in jobs}) == 1

    def fvec(v):
        s = 0
        nald = 0
        for u, (j, ns) in enumerate(units):
            job = jobs[j]
            for tg in range(2):
                for nt4 in range(4):
                    nt_g = ns * 4 + nt4
                    v.wait_ge(mmS.h, s + 1)
                    if s >= 2:
                        v.wait_ge(od2[s % 2].h, 64 * (s // 2))
                    if job.get("add_from"):
                        nald += 64
                        v.wait_ge(ald.h, nald)
                    for tb4 in range(4):
                        sg = stage[(s % 2) * 4 + tb4]
                        ps = c.psum[(s % 2) * 4 + tb4][:, 0:512]
                        if job.get("scale") == "temp":
                            v.tensor_scalar(sg, ps, c.sb["temp"][:, 2:3],
                                            c.pcol(job["bias"], job.get("bias_off", 0) + nt_g),
                                            ALU.mult, ALU.add)
                        elif job.get("bias"):
                            v.tensor_scalar(sg, ps,
                                            c.pcol(job["bias"], job.get("bias_off", 0) + nt_g),
                                            0.0, ALU.add, ALU.add)
                        else:
                            v.tensor_copy(sg, ps)
                        if job.get("add_from"):
                            v.tensor_tensor(sg, sg, addb[tb4], ALU.add)
                    c.vmark(v, ev.h, 4)
                    ev.inc(4)
                    s += 1

    phase(c, {"sync": fsync, "gpsimd": fgps, "vector": fvec,
              "tensor": ften})


def ph_qkv0(c):
    c.push()
    emit_gemm(c, "qkv0", "sc_xn2F", KD, [
        dict(wkey="Wq", N=1024, out="sc_qF", bias="bq"),
        dict(wkey="Wk", N=1024, out="sc_kF", bias="bk"),
        dict(wkey="Wv", N=1024, out="sc_vF", bias="bv"),
    ])
    c.pop()


def ph_qkv1(c):
    c.push()
    emit_gemm(c, "qkv1", "sc_xn1F", KD, [
        dict(wkey="Watt", N=1024, out="sc_q1F", bias="battqs",
             scale="temp", noff=0),
        dict(wkey="Watt", N=1024, out="sc_k1F", bias="batt", bias_off=8, noff=1024),
        dict(wkey="Watt", N=1024, out="sc_v1F", bias="batt", bias_off=16, noff=2048),
    ])
    c.pop()


def ph_proj1(c):
    c.push()
    emit_gemm(c, "proj1", "sc_y1F", KD, [
        dict(wkey="Wproj", N=1024, out="sc_accF", bias="bproj",
             add_from="sc_xl1F"),
    ])
    c.pop()


def ph_lora(c):
    c.push()
    xreg = c.region(128, KD * T, BF16, "lorax")
    w1 = c.region(128, KD * 48, BF16, "w1")
    w2 = c.region(48, 1024, BF16, "w2")
    lt = c.region(48, T, BF16, "lt")
    stage = [c.region(128, 512, F32, "lstg") for _ in range(2)]
    xld = c.sem("lrxld", sw=True)
    mm1 = c.sem("lrmm1")
    tev = c.sem("lrtev")
    mm2 = c.sem("lrmm2")
    ev2 = c.sem("lrev2")
    od2 = [c.sem("lrod0"), c.sem("lrod1")]

    def fgps(g):
        for ki in range(KD):
            g.dma_start(out=xreg[:, ki * T:(ki + 1) * T],
                        in_=c.d["sc_xn2F"][ki]).then_inc(xld.h, 16)
            xld.inc(16)
            g.dma_start(out=w1[:, ki * 48:(ki + 1) * 48],
                        in_=c.d["thw1"][ki * 128:(ki + 1) * 128, :]).then_inc(xld.h, 16)
            xld.inc(16)
        g.dma_start(out=w2, in_=c.d["thw2"]).then_inc(xld.h, 16)
        xld.inc(16)

    def ften(t):
        t.wait_ge(xld.h, xld.n)
        for tb in range(NTB):
            for ki in range(KD):
                t.matmul(c.psum[tb][0:48, 0:512], w1[:, ki * 48:(ki + 1) * 48],
                         xreg[:, ki * T + tb * 512: ki * T + (tb + 1) * 512],
                         start=(ki == 0), stop=(ki == KD - 1))
            t.drain().then_inc(mm1.h, 1)
            mm1.inc()
        s = 0
        t.wait_ge(tev.h, NTB)
        for nt in range(8):
            for tb in range(NTB):
                if s >= 4:
                    t.wait_ge(ev2.h, s - 3)
                t.matmul(c.psum[s % 4][:, 0:512], w2[:, nt * 128:(nt + 1) * 128],
                         lt[:, tb * 512:(tb + 1) * 512],
                         start=True, stop=True).then_inc(mm2.h, 1)
                mm2.inc()
                s += 1

    def fscal(sc):
        for tb in range(NTB):
            sc.wait_ge(mm1.h, tb + 1)
            sc.activation(lt[:, tb * 512:(tb + 1) * 512], c.psum[tb][0:48, 0:512],
                          AF.Tanh).then_inc(tev.h, 1)
            tev.inc()
        s = 0
        for nt in range(8):
            for tb in range(NTB):
                sc.wait_ge(mm2.h, s + 1)
                if s >= 2:
                    sc.wait_ge(od2[s % 2].h, 16 * (s // 2))
                sc.activation(stage[s % 2], c.psum[s % 4][:, 0:512],
                              AF.Copy).then_inc(ev2.h, 1)
                ev2.inc()
                s += 1

    def fsync(sync):
        s = 0
        for nt in range(8):
            for tb in range(NTB):
                sync.wait_ge(ev2.h, s + 1)
                sync.dma_start(out=c.d["sc_loraF"][nt][:, tb * 512:(tb + 1) * 512],
                               in_=stage[s % 2]).then_inc(od2[s % 2].h, 16)
                od2[s % 2].inc(16)
                s += 1
        sync.wait_ge(od2[0].h, od2[0].n)
        sync.wait_ge(od2[1].h, od2[1].n)

    phase(c, {"sync": fsync, "gpsimd": fgps, "scalar": fscal, "tensor": ften})
    c.pop()



def ph_wo(c):
    c.push()
    ab2 = [c.region(128, KD * 512, BF16, "wox") for _ in range(2)]
    wo = c.region(128, KD * 1024, BF16, "wow")
    x1sb = [c.region(128, KD * 512, F32, "wox1") for _ in range(2)]
    h2sb = [c.region(128, KD * 512, F32, "woh2") for _ in range(2)]
    xnb = [c.region(128, KD * 512, F32, "woxn") for _ in range(2)]
    x1c = c.region(128, KD * 512, BF16, "wox1c")
    x2c = c.region(128, KD * 512, BF16, "wox2c")
    srow = c.region(1, 16, F32, "worow")
    r1 = c.region(1, 512, F32, "wor1")
    r2 = c.region(1, 512, F32, "wor2")
    rm = c.region(1, 512, F32, "worm")
    rv = c.region(1, 512, F32, "worv")
    tmp = c.region(128, 512, F32, "wotmp")
    ld = c.sem("wold", sw=True)
    al = c.sem("woal")
    mmA = c.sem("womA")
    evA = c.sem("woevA")
    vad = c.sem("wovad")
    vcst = c.sem("wovcst")
    ssq = c.sem("wossq")
    smm = c.sem("wosmm")
    stv = c.sem("wostv")
    bc = c.sem("wobc")
    nrm = c.sem("wonrm")
    od2 = [c.sem("wood0"), c.sem("wood1")]

    xl2 = [c.sem("woxl0", sw=True), c.sem("woxl1", sw=True)]

    def fgps(g):
        for ki in range(KD):
            g.dma_start(out=wo[:, ki * 1024:(ki + 1) * 1024],
                        in_=c.d["Wo"][ki * 128:(ki + 1) * 128, :]).then_inc(ld.h, 16)
            ld.inc(16)
        for tb in range(NTB):
            if tb >= 2:
                g.wait_ge(mmA.h, 2 * (tb - 1))
            for ki in range(KD):
                g.dma_start(out=ab2[tb % 2][:, ki * 512:(ki + 1) * 512],
                            in_=c.d["sc_attnF"][ki][:, tb * 512:(tb + 1) * 512]).then_inc(xl2[tb % 2].h, 16)
                xl2[tb % 2].inc(16)

    def fsync(sync):
        for tb in range(NTB):
            par = tb % 2
            if tb >= 2:
                sync.wait_ge(vad.h, tb - 1)
            for nt in range(KD):
                sync.dma_start(out=xnb[par][:, nt * 512:(nt + 1) * 512],
                               in_=c.d["sc_xnF"][nt][:, tb * 512:(tb + 1) * 512]).then_inc(al.h, 16)
                al.inc(16)
            sync.wait_ge(vad.h, tb + 1)
            for nt in range(KD):
                sync.dma_start(out=c.d["sc_x1F"][nt][:, tb * 512:(tb + 1) * 512],
                               in_=x1sb[par][:, nt * 512:(nt + 1) * 512]).then_inc(od2[par].h, 16)
                od2[par].inc(16)
            sync.wait_ge(nrm.h, tb + 1)
            for nt in range(KD):
                sync.dma_start(out=c.d["sc_h2F"][nt][:, tb * 512:(tb + 1) * 512],
                               in_=h2sb[par][:, nt * 512:(nt + 1) * 512]).then_inc(od2[par].h, 16)
                od2[par].inc(16)
        sync.wait_ge(od2[0].h, od2[0].n)
        sync.wait_ge(od2[1].h, od2[1].n)

    def ften(t):
        t.wait_ge(ld.h, 16 * KD)
        for tb in range(NTB):
            t.wait_ge(xl2[tb % 2].h, 16 * KD * (tb // 2 + 1))
            for grp in range(2):
                if tb > 0 or grp > 0:
                    t.wait_ge(evA.h, 8 * tb + 4 * grp)
                for nt4 in range(4):
                    nt = grp * 4 + nt4
                    for ki in range(KD):
                        t.matmul(c.psum[nt4][:, 0:512],
                                 wo[:, ki * 1024 + nt * 128: ki * 1024 + (nt + 1) * 128],
                                 ab2[tb % 2][:, ki * 512:(ki + 1) * 512],
                                 start=(ki == 0), stop=(ki == KD - 1))
                t.drain().then_inc(mmA.h, 1)
                mmA.inc()
            t.wait_ge(vcst.h, tb + 1)
            t.wait_ge(ssq.h, tb + 1)
            if tb > 0:
                t.wait_ge(stv.h, tb)
            for ki in range(KD):
                t.matmul(c.psum[4][0:1, 0:512], c.sb["onesb"][:, 0:1],
                         x1c[:, ki * 512:(ki + 1) * 512],
                         start=(ki == 0), stop=(ki == KD - 1))
            for ki in range(KD):
                t.matmul(c.psum[5][0:1, 0:512], c.sb["onesb"][:, 0:1],
                         x2c[:, ki * 512:(ki + 1) * 512],
                         start=(ki == 0), stop=(ki == KD - 1))
            t.drain().then_inc(smm.h, 1)
            smm.inc()
            t.wait_ge(stv.h, tb + 1)
            if tb > 0:
                t.wait_ge(nrm.h, tb)
            t.matmul(c.psum[6][:, 0:512], c.sb["onesrow"][0:1, 0:128], r1[0:1, 0:512],
                     start=True, stop=True)
            t.matmul(c.psum[7][:, 0:512], c.sb["onesrow"][0:1, 0:128], r2[0:1, 0:512],
                     start=True, stop=True).then_inc(bc.h, 1)
            bc.inc()

    def fscal(s):
        for tb in range(NTB):
            par = tb % 2
            s.wait_ge(vad.h, tb + 1)
            for nt in range(KD):
                s.activation(x2c[:, nt * 512:(nt + 1) * 512],
                             x1sb[par][:, nt * 512:(nt + 1) * 512], AF.Square)
            c.smark(s, ssq.h, 1)
            ssq.inc()
            s.wait_ge(c.sem("wovv").h, tb + 1)
            s.activation(rv[0:1, 0:512], rv[0:1, 0:512], AF.Sqrt,
                         bias=c.pcol("c_eps6")[0:1, 0:1], scale=1.0).then_inc(c.sem("wosq2").h, 1)
            c.sem("wosq2").inc()

    vv = c.sem("wovv")
    sq2 = c.sem("wosq2")

    def fvec(v):
        for tb in range(NTB):
            par = tb % 2
            for grp in range(2):
                v.wait_ge(mmA.h, 2 * tb + grp + 1)
                if tb >= 2 and grp == 0:
                    v.wait_ge(od2[par].h, 256 * (tb // 2))
                    v.wait_ge(nrm.h, tb - 1)
                for nt4 in range(4):
                    nt = grp * 4 + nt4
                    v.tensor_scalar(x1sb[par][:, nt * 512:(nt + 1) * 512],
                                    c.psum[nt4][:, 0:512], c.pcol("bo", nt), 0.0,
                                    ALU.add, ALU.add)
                c.vmark(v, evA.h, 4)
                evA.inc(4)
            v.wait_ge(al.h, 128 * (tb + 1))
            for nt in range(KD):
                v.tensor_tensor(x1sb[par][:, nt * 512:(nt + 1) * 512],
                                x1sb[par][:, nt * 512:(nt + 1) * 512],
                                xnb[par][:, nt * 512:(nt + 1) * 512], ALU.add)
            c.vmark(v, vad.h, 1)
            vad.inc()
            for nt in range(KD):
                v.tensor_copy(x1c[:, nt * 512:(nt + 1) * 512],
                              x1sb[par][:, nt * 512:(nt + 1) * 512])
            c.vmark(v, vcst.h, 1)
            vcst.inc()
            v.wait_ge(smm.h, tb + 1)
            v.tensor_scalar(rm[0:1, 0:512], c.psum[4][0:1, 0:512], 1.0 / D, 0.0,
                            ALU.mult, ALU.add)
            v.tensor_scalar(rv[0:1, 0:512], c.psum[5][0:1, 0:512], 1.0 / D, 0.0,
                            ALU.mult, ALU.add)
            v.tensor_tensor(tmp[0:1, 0:512], rm[0:1, 0:512], rm[0:1, 0:512], ALU.mult)
            v.tensor_tensor(rv[0:1, 0:512], rv[0:1, 0:512], tmp[0:1, 0:512],
                            ALU.subtract).then_inc(vv.h, 1)
            vv.inc()
            v.wait_ge(sq2.h, tb + 1)
            v.reciprocal(r1[0:1, 0:512], rv[0:1, 0:512])
            v.tensor_tensor(r2[0:1, 0:512], rm[0:1, 0:512], r1[0:1, 0:512], ALU.mult)
            v.tensor_scalar(r2[0:1, 0:512], r2[0:1, 0:512], -1.0, 0.0,
                            ALU.mult, ALU.add).then_inc(stv.h, 1)
            stv.inc()
            v.wait_ge(bc.h, tb + 1)
            for nt in range(KD):
                v.tensor_tensor(tmp, x1sb[par][:, nt * 512:(nt + 1) * 512],
                                c.psum[6][:, 0:512], ALU.mult)
                v.tensor_tensor(tmp, tmp, c.psum[7][:, 0:512], ALU.add)
                v.tensor_scalar(h2sb[par][:, nt * 512:(nt + 1) * 512], tmp,
                                c.pcol("n2g", nt), c.pcol("n2b", nt), ALU.mult, ALU.add)
            c.vmark(v, nrm.h, 1)
            nrm.inc()

    phase(c, {"sync": fsync, "gpsimd": fgps, "vector": fvec, "scalar": fscal,
              "tensor": ften})
    c.pop()



def ph_comb1(c):
    c.push()
    xl = [c.region(128, KD * 512, F32, "cb_xl") for _ in range(2)]
    xn1 = [c.region(128, KD * 512, F32, "cb_xn") for _ in range(2)]
    ina = [c.region(128, KD * 512, F32, "cb_a") for _ in range(2)]
    inb = [c.region(128, KD * 512, F32, "cb_b") for _ in range(2)]
    x1c = c.region(128, KD * 512, BF16, "cb_c")
    x2c = c.region(128, KD * 512, BF16, "cb_s")
    rm = c.region(1, 512, F32, "cb_rm")
    rv = c.region(1, 512, F32, "cb_rv")
    r1 = c.region(1, 512, F32, "cb_r1")
    r2 = c.region(1, 512, F32, "cb_r2")
    tmp = c.region(128, 512, F32, "cb_tmp")
    al = c.sem("cbal")
    vad = c.sem("cbvad")
    vcst = c.sem("cbvcst")
    ssq = c.sem("cbssq")
    smm = c.sem("cbsmm")
    vv = c.sem("cbvv")
    sq2 = c.sem("cbsq2")
    stv = c.sem("cbstv")
    bc = c.sem("cbbc")
    nrm = c.sem("cbnrm")
    od2 = [c.sem("cbod0"), c.sem("cbod1")]

    def fsync(sync):
        for tb in range(NTB):
            par = tb % 2
            if tb >= 2:
                sync.wait_ge(nrm.h, tb - 1)
                sync.wait_ge(od2[par].h, 256 * (tb // 2))
            for nt in range(KD):
                sl = slice(nt * 512, (nt + 1) * 512)
                tsl = slice(tb * 512, (tb + 1) * 512)
                sync.dma_start(out=xl[par][:, sl], in_=c.d["sc_xF"][nt][:, tsl]).then_inc(al.h, 16)
                sync.dma_start(out=ina[par][:, sl], in_=c.d["sc_A0F"][nt][:, tsl]).then_inc(al.h, 16)
                sync.dma_start(out=inb[par][:, sl], in_=c.d["sc_F0F"][nt][:, tsl]).then_inc(al.h, 16)
                al.inc(48)
            sync.wait_ge(vad.h, tb + 1)
            for nt in range(KD):
                sync.dma_start(out=c.d["sc_xl1F"][nt][:, tb * 512:(tb + 1) * 512],
                               in_=xl[par][:, nt * 512:(nt + 1) * 512]).then_inc(od2[par].h, 16)
                od2[par].inc(16)
            sync.wait_ge(nrm.h, tb + 1)
            for nt in range(KD):
                sync.dma_start(out=c.d["sc_xn1F"][nt][:, tb * 512:(tb + 1) * 512],
                               in_=xn1[par][:, nt * 512:(nt + 1) * 512]).then_inc(od2[par].h, 16)
                od2[par].inc(16)
        sync.wait_ge(od2[0].h, od2[0].n)
        sync.wait_ge(od2[1].h, od2[1].n)

    def ften(t):
        for tb in range(NTB):
            t.wait_ge(vcst.h, tb + 1)
            t.wait_ge(ssq.h, tb + 1)
            if tb > 0:
                t.wait_ge(stv.h, tb)
            for ki in range(KD):
                t.matmul(c.psum[4][0:1, 0:512], c.sb["onesb"][:, 0:1],
                         x1c[:, ki * 512:(ki + 1) * 512],
                         start=(ki == 0), stop=(ki == KD - 1))
            for ki in range(KD):
                t.matmul(c.psum[5][0:1, 0:512], c.sb["onesb"][:, 0:1],
                         x2c[:, ki * 512:(ki + 1) * 512],
                         start=(ki == 0), stop=(ki == KD - 1))
            t.drain().then_inc(smm.h, 1)
            smm.inc()
            t.wait_ge(stv.h, tb + 1)
            if tb > 0:
                t.wait_ge(nrm.h, tb)
            t.matmul(c.psum[6][:, 0:512], c.sb["onesrow"][0:1, 0:128], r1[0:1, 0:512],
                     start=True, stop=True)
            t.matmul(c.psum[7][:, 0:512], c.sb["onesrow"][0:1, 0:128], r2[0:1, 0:512],
                     start=True, stop=True).then_inc(bc.h, 1)
            bc.inc()

    def fscal(s):
        for tb in range(NTB):
            par = tb % 2
            s.wait_ge(vad.h, tb + 1)
            for nt in range(KD):
                s.activation(x2c[:, nt * 512:(nt + 1) * 512],
                             xl[par][:, nt * 512:(nt + 1) * 512], AF.Square)
            c.smark(s, ssq.h, 1)
            ssq.inc()
            s.wait_ge(vv.h, tb + 1)
            s.activation(rv[0:1, 0:512], rv[0:1, 0:512], AF.Sqrt,
                         bias=c.pcol("c_eps5")[0:1, 0:1], scale=1.0).then_inc(sq2.h, 1)
            sq2.inc()

    def fvec(v):
        for tb in range(NTB):
            par = tb % 2
            v.wait_ge(al.h, 384 * (tb + 1))
            if tb >= 2:
                v.wait_ge(od2[par].h, 256 * (tb // 2))
                v.wait_ge(nrm.h, tb - 1)
            for nt in range(KD):
                sl = slice(nt * 512, (nt + 1) * 512)
                v.tensor_tensor(xl[par][:, sl], xl[par][:, sl], ina[par][:, sl], ALU.add)
                v.tensor_tensor(xl[par][:, sl], xl[par][:, sl], inb[par][:, sl], ALU.add)
            c.vmark(v, vad.h, 1)
            vad.inc()
            for nt in range(KD):
                v.tensor_copy(x1c[:, nt * 512:(nt + 1) * 512],
                              xl[par][:, nt * 512:(nt + 1) * 512])
            c.vmark(v, vcst.h, 1)
            vcst.inc()
            v.wait_ge(smm.h, tb + 1)
            v.tensor_scalar(rm[0:1, 0:512], c.psum[4][0:1, 0:512], 1.0 / D, 0.0,
                            ALU.mult, ALU.add)
            v.tensor_scalar(rv[0:1, 0:512], c.psum[5][0:1, 0:512], 1.0 / D, 0.0,
                            ALU.mult, ALU.add)
            v.tensor_tensor(tmp[0:1, 0:512], rm[0:1, 0:512], rm[0:1, 0:512], ALU.mult)
            v.tensor_tensor(rv[0:1, 0:512], rv[0:1, 0:512], tmp[0:1, 0:512],
                            ALU.subtract).then_inc(vv.h, 1)
            vv.inc()
            v.wait_ge(sq2.h, tb + 1)
            v.reciprocal(r1[0:1, 0:512], rv[0:1, 0:512])
            v.tensor_tensor(r2[0:1, 0:512], rm[0:1, 0:512], r1[0:1, 0:512], ALU.mult)
            v.tensor_scalar(r2[0:1, 0:512], r2[0:1, 0:512], -1.0, 0.0,
                            ALU.mult, ALU.add).then_inc(stv.h, 1)
            stv.inc()
            v.wait_ge(bc.h, tb + 1)
            for nt in range(KD):
                sl = slice(nt * 512, (nt + 1) * 512)
                v.tensor_tensor(tmp, xl[par][:, sl], c.psum[6][:, 0:512], ALU.mult)
                v.tensor_tensor(xn1[par][:, sl], tmp, c.psum[7][:, 0:512], ALU.add)
            c.vmark(v, nrm.h, 1)
            nrm.inc()

    phase(c, {"sync": fsync, "vector": fvec, "scalar": fscal, "tensor": ften})
    c.pop()



def ph_ffn0b(c):
    c.push()
    xreg = c.region(128, KD * T, BF16, "f0x")
    w1b = [c.region(128, KD * 512, BF16, "f0w1") for _ in range(2)]
    w2b = [c.region(128, 1024, BF16, "f0w2") for _ in range(2)]
    gsb = c.region(128, 32 * 512, BF16, "f0g")
    gx = [c.region(128, 512, F32, "f0gx") for _ in range(2)]
    gt = [c.region(128, 512, F32, "f0gt") for _ in range(2)]
    stage = [c.region(128, 512, F32, "f0s") for _ in range(8)]
    xld = c.sem("f0xld", sw=True)
    w1ld2 = [c.sem("f0w1ld0", sw=True), c.sem("f0w1ld1", sw=True)]
    w1fr = c.sem("f0w1fr")
    w2ld2 = [c.sem("f0w2ld0", sw=True), c.sem("f0w2ld1", sw=True)]
    w2fr = c.sem("f0w2fr")
    mmA = c.sem("f0mmA")
    gdn = c.sem("f0gdn")
    sgd = c.sem("f0sgd")
    mmB = c.sem("f0mmB")
    evB = c.sem("f0evB")
    vad = c.sem("f0vad")
    ald = c.sem("f0ald")
    od2 = [c.sem("f0od0"), c.sem("f0od1")]

    def fgps(g):
        for ki in range(KD):
            g.dma_start(out=xreg[:, ki * T:(ki + 1) * T],
                        in_=c.d["sc_h2F"][ki]).then_inc(xld.h, 16)
            xld.inc(16)
        for tb in range(NTB):
            for ns in range(8):
                u = tb * 8 + ns
                if u >= 2:
                    g.wait_ge(w1fr.h, u - 1)
                for ki in range(KD):
                    g.dma_start(out=w1b[u % 2][:, ki * 512:(ki + 1) * 512],
                                in_=c.d["Wf1"][ki * 128:(ki + 1) * 128,
                                               ns * 512:(ns + 1) * 512]).then_inc(w1ld2[u % 2].h, 16)
                    w1ld2[u % 2].inc(16)
            for ki in range(32):
                u = tb * 32 + ki
                if u >= 2:
                    g.wait_ge(w2fr.h, u - 1)
                g.dma_start(out=w2b[u % 2],
                            in_=c.d["Wf2"][ki * 128:(ki + 1) * 128, :]).then_inc(w2ld2[u % 2].h, 16)
                w2ld2[u % 2].inc(16)

    def ften(t):
        t.wait_ge(xld.h, 16 * KD)
        sA = 0
        for tb in range(NTB):
            if tb > 0:
                t.wait_ge(evB.h, 8 * tb)
            for ns in range(8):
                u = tb * 8 + ns
                t.wait_ge(w1ld2[u % 2].h, 16 * KD * (u // 2 + 1))
                for nt4 in range(4):
                    if sA >= 4:
                        t.wait_ge(gdn.h, sA - 3)
                    bank = c.psum[sA % 4]
                    for ki in range(KD):
                        t.matmul(bank[:, 0:512],
                                 w1b[u % 2][:, ki * 512 + nt4 * 128: ki * 512 + (nt4 + 1) * 128],
                                 xreg[:, ki * T + tb * 512: ki * T + (tb + 1) * 512],
                                 start=(ki == 0), stop=(ki == KD - 1))
                    t.drain().then_inc(mmA.h, 1)
                    mmA.inc()
                    sA += 1
                t.drain().then_inc(w1fr.h, 1)
                w1fr.inc()
            t.wait_ge(gdn.h, 32 * (tb + 1))
            if tb > 0:
                t.wait_ge(evB.h, 8 * tb)
            for ki in range(32):
                u = tb * 32 + ki
                t.wait_ge(w2ld2[u % 2].h, 16 * (u // 2 + 1))
                for mt in range(KD):
                    t.matmul(c.psum[mt][:, 0:512],
                             w2b[u % 2][:, mt * 128:(mt + 1) * 128],
                             gsb[:, ki * 512:(ki + 1) * 512],
                             start=(ki == 0), stop=(ki == 31), skip_group_check=True)
                t.drain().then_inc(w2fr.h, 1)
                w2fr.inc()
            t.drain().then_inc(mmB.h, 1)
            mmB.inc()

    vgx = c.sem("f0vgx")

    def fscal(s):
        sA = 0
        for tb in range(NTB):
            for ns in range(8):
                for nt4 in range(4):
                    s.wait_ge(vgx.h, sA + 1)
                    gxp, gtp = gx[sA % 2], gt[sA % 2]
                    s.activation(gtp, gxp, AF.Square).then_inc(sgd.h, 1)
                    sgd.inc()
                    s.wait_ge(c.sem("f0vu").h, sA + 1)
                    s.activation(gtp, gtp, AF.Tanh, scale=GCONST).then_inc(c.sem("f0st").h, 1)
                    c.sem("f0st").inc()
                    sA += 1

    vu = c.sem("f0vu")
    st_ = c.sem("f0st")

    addb4 = [c.region(128, 512, F32, "f0a4") for _ in range(8)]

    def fvec(v):
        sA = 0
        for tb in range(NTB):
            if tb > 0:
                v.wait_ge(mmB.h, tb)
            for ns in range(8):
                for nt4 in range(4):
                    nt_g = ns * 4 + nt4
                    gxp, gtp = gx[sA % 2], gt[sA % 2]
                    v.wait_ge(mmA.h, sA + 1)
                    if sA >= 2:
                        v.wait_ge(gdn.h, sA - 1)
                    v.tensor_scalar(gxp, c.psum[sA % 4][:, 0:512],
                                    c.pcol("bf1", nt_g), 0.0, ALU.add,
                                    ALU.add).then_inc(vgx.h, 1)
                    vgx.inc()
                    v.wait_ge(sgd.h, sA + 1)
                    v.tensor_tensor(gtp, gtp, gxp, ALU.mult)
                    v.scalar_tensor_tensor(gtp, gtp, 0.044715, gxp, ALU.mult,
                                           ALU.add).then_inc(vu.h, 1)
                    vu.inc()
                    v.wait_ge(st_.h, sA + 1)
                    v.tensor_scalar(gtp, gtp, 1.0, 0.5, ALU.add, ALU.mult)
                    v.tensor_tensor(gsb[:, nt_g * 512:(nt_g + 1) * 512], gtp, gxp,
                                    ALU.mult).then_inc(gdn.h, 1)
                    gdn.inc()
                    sA += 1
            v.wait_ge(mmB.h, tb + 1)
            if tb >= 1:
                v.wait_ge(od2[(tb - 1) % 2].h, 128 * ((tb - 1) // 2 + 1))
            v.wait_ge(ald.h, 128 * (tb + 1))
            for mt in range(KD):
                v.tensor_scalar(stage[mt], c.psum[mt][:, 0:512],
                                c.pcol("bf2", mt), 0.0, ALU.add, ALU.add)
                v.tensor_tensor(stage[mt], stage[mt], addb4[mt], ALU.add)
            c.vmark(v, evB.h, 8)
            evB.inc(8)
            c.vmark(v, vad.h, 1)
            vad.inc()

    def fsync(sync):
        for tb in range(NTB):
            if tb >= 1:
                sync.wait_ge(vad.h, tb)
            for mt in range(KD):
                sync.dma_start(out=addb4[mt],
                               in_=c.d["sc_x1F"][mt][:, tb * 512:(tb + 1) * 512]).then_inc(ald.h, 16)
                ald.inc(16)
            sync.wait_ge(vad.h, tb + 1)
            for mt in range(KD):
                sync.dma_start(out=c.d["sc_A0F"][mt][:, tb * 512:(tb + 1) * 512],
                               in_=stage[mt]).then_inc(od2[tb % 2].h, 16)
                od2[tb % 2].inc(16)
        sync.wait_ge(od2[0].h, od2[0].n)
        sync.wait_ge(od2[1].h, od2[1].n)

    phase(c, {"sync": fsync, "gpsimd": fgps, "vector": fvec, "scalar": fscal,
              "tensor": ften})
    c.pop()



def ph_ffnint(c):
    c.push()
    xreg = c.region(128, KD * T, BF16, "fix")
    wub = [c.region(128, KD * 512, BF16, "fiu") for _ in range(2)]
    wdb = [c.region(128, 1024, BF16, "fid") for _ in range(2)]
    hsb = c.region(128, 24 * 512, BF16, "fih")
    sgb = [c.region(128, 512, BF16, "fisg") for _ in range(2)]
    stage = [c.region(128, 512, F32, "fis") for _ in range(8)]
    xld = c.sem("fixld", sw=True)
    uld2 = [c.sem("fiuld0", sw=True), c.sem("fiuld1", sw=True)]
    ufr = c.sem("fiufr")
    dld2 = [c.sem("fidld0", sw=True), c.sem("fidld1", sw=True)]
    dfr = c.sem("fidfr")
    mmG = c.sem("fimmG")
    sil = c.sem("fisil")
    hdn = c.sem("fihdn")
    mmD = c.sem("fimmD")
    evD = c.sem("fievD")
    od2 = [c.sem("fiod0"), c.sem("fiod1")]

    def fgps(g):
        for ki in range(KD):
            g.dma_start(out=xreg[:, ki * T:(ki + 1) * T],
                        in_=c.d["sc_xnF"][ki]).then_inc(xld.h, 16)
            xld.inc(16)
        for tb in range(NTB):
            for ng in range(6):
                for kind in range(2):
                    u = tb * 12 + ng * 2 + kind
                    if u >= 2:
                        g.wait_ge(ufr.h, u - 1)
                    W = c.d["wgt"] if kind == 0 else c.d["wut"]
                    for ki in range(KD):
                        g.dma_start(out=wub[u % 2][:, ki * 512:(ki + 1) * 512],
                                    in_=W[ki * 128:(ki + 1) * 128,
                                          ng * 512:(ng + 1) * 512]).then_inc(uld2[u % 2].h, 16)
                        uld2[u % 2].inc(16)
            for ki in range(24):
                u = tb * 24 + ki
                if u >= 2:
                    g.wait_ge(dfr.h, u - 1)
                g.dma_start(out=wdb[u % 2],
                            in_=c.d["wdt"][ki * 128:(ki + 1) * 128, :]).then_inc(dld2[u % 2].h, 16)
                dld2[u % 2].inc(16)

    def ften(t):
        t.wait_ge(xld.h, 16 * KD)
        for tb in range(NTB):
            if tb > 0:
                t.wait_ge(evD.h, 8 * tb)
            for ng in range(6):
                ug = tb * 12 + ng * 2
                if ng > 0 or tb > 0:
                    t.wait_ge(hdn.h, 24 * tb + 4 * ng)
                t.wait_ge(uld2[ug % 2].h, 16 * KD * (ug // 2 + 1))
                for nt4 in range(4):
                    for ki in range(KD):
                        t.matmul(c.psum[nt4][:, 0:512],
                                 wub[ug % 2][:, ki * 512 + nt4 * 128: ki * 512 + (nt4 + 1) * 128],
                                 xreg[:, ki * T + tb * 512: ki * T + (tb + 1) * 512],
                                 start=(ki == 0), stop=(ki == KD - 1))
                t.drain().then_inc(ufr.h, 1)
                ufr.inc()
                t.wait_ge(uld2[(ug + 1) % 2].h, 16 * KD * ((ug + 1) // 2 + 1))
                for nt4 in range(4):
                    for ki in range(KD):
                        t.matmul(c.psum[4 + nt4][:, 0:512],
                                 wub[(ug + 1) % 2][:, ki * 512 + nt4 * 128: ki * 512 + (nt4 + 1) * 128],
                                 xreg[:, ki * T + tb * 512: ki * T + (tb + 1) * 512],
                                 start=(ki == 0), stop=(ki == KD - 1))
                t.drain().then_inc(ufr.h, 1)
                ufr.inc()
                t.drain().then_inc(mmG.h, 1)
                mmG.inc()
            t.wait_ge(hdn.h, 24 * (tb + 1))
            t.wait_ge(evD.h, 8 * tb)
            for ki in range(24):
                u = tb * 24 + ki
                t.wait_ge(dld2[u % 2].h, 16 * (u // 2 + 1))
                for mt in range(KD):
                    t.matmul(c.psum[mt][:, 0:512],
                             wdb[u % 2][:, mt * 128:(mt + 1) * 128],
                             hsb[:, ki * 512:(ki + 1) * 512],
                             start=(ki == 0), stop=(ki == 23), skip_group_check=True)
                t.drain().then_inc(dfr.h, 1)
                dfr.inc()
            t.drain().then_inc(mmD.h, 1)
            mmD.inc()

    def fscal(s):
        sg = 0
        for tb in range(NTB):
            for ng in range(6):
                s.wait_ge(mmG.h, tb * 6 + ng + 1)
                for nt4 in range(4):
                    if sg >= 2:
                        s.wait_ge(hdn.h, sg - 1)
                    s.activation(sgb[sg % 2], c.psum[nt4][:, 0:512], AF.Silu,
                                 scale=1.0 / 127.0).then_inc(sil.h, 1)
                    sil.inc()
                    sg += 1


    def fvec(v):
        sg = 0
        for tb in range(NTB):
            if tb > 0:
                v.wait_ge(mmD.h, tb)
            for ng in range(6):
                for nt4 in range(4):
                    v.wait_ge(sil.h, sg + 1)
                    slot = ng * 4 + nt4
                    v.tensor_tensor(hsb[:, slot * 512:(slot + 1) * 512],
                                    sgb[sg % 2], c.psum[4 + nt4][:, 0:512],
                                    ALU.mult).then_inc(hdn.h, 1)
                    hdn.inc()
                    sg += 1
            v.wait_ge(mmD.h, tb + 1)
            if tb >= 1:
                v.wait_ge(od2[(tb - 1) % 2].h, 128 * ((tb - 1) // 2 + 1))
            for mt in range(KD):
                v.tensor_scalar(stage[mt], c.psum[mt][:, 0:512],
                                c.pcol("c_i127sq"), 0.0, ALU.mult,
                                ALU.add).then_inc(evD.h, 1)
                evD.inc()

    def fsync(sync):
        for tb in range(NTB):
            sync.wait_ge(evD.h, 8 * (tb + 1))
            for mt in range(KD):
                sync.dma_start(out=c.d["sc_F0F"][mt][:, tb * 512:(tb + 1) * 512],
                               in_=stage[mt]).then_inc(od2[tb % 2].h, 16)
                od2[tb % 2].inc(16)
        sync.wait_ge(od2[0].h, od2[0].n)
        sync.wait_ge(od2[1].h, od2[1].n)

    phase(c, {"sync": fsync, "gpsimd": fgps, "vector": fvec, "scalar": fscal,
              "tensor": ften})
    c.pop()



def ph_router(c):
    c.push()
    xf = [c.region(128, KD * 512, F32, "roxf") for _ in range(2)]
    xhb = [c.region(128, KD * 512, BF16, "roxh") for _ in range(2)]
    xhf = [c.region(128, KD * 512, F32, "roxhf") for _ in range(2)]
    xlb = [c.region(128, KD * 512, BF16, "roxl") for _ in range(2)]
    gw = c.region(128, 2 * KD * 8, BF16, "rog")
    lg = c.region(8, T, F32, "rolg")
    rtm = c.region(128, 8, F32, "rotm")
    eq1 = c.region(128, 8, F32, "roe1")
    eq2 = c.region(128, 8, F32, "roe2")
    r2m = c.region(128, 8, F32, "ror2")
    rwk = c.region(128, 16, F32, "rowk")
    rst = [c.region(8, 128, F32, "rost") for _ in range(2)]
    ld = c.sem("rold", sw=True)
    xl2 = [c.sem("roxl0", sw=True), c.sem("roxl1", sw=True)]
    mm = c.sem("romm")
    ev = c.sem("roev")
    tr1 = c.sem("rotr1")
    cpd = c.sem("rocpd")
    vm = c.sem("rovm")
    se = c.sem("rose")
    sf = c.sem("rosf")
    tr2 = c.sem("rotr2")
    ev2 = c.sem("roev2")
    od2 = [c.sem("rood0"), c.sem("rood1")]

    def fgps(g):
        for ki in range(KD):
            g.dma_start(out=gw[:, ki * 8:(ki + 1) * 8],
                        in_=c.d["gwt"][ki * 128:(ki + 1) * 128, :]).then_inc(ld.h, 16)
            ld.inc(16)
            g.dma_start(out=gw[:, KD * 8 + ki * 8:KD * 8 + (ki + 1) * 8],
                        in_=c.d["gwt_lo"][ki * 128:(ki + 1) * 128, :]).then_inc(ld.h, 16)
            ld.inc(16)
        for tb in range(NTB):
            if tb >= 2:
                g.wait_ge(c.sem("rovxd").h, tb - 1)
            for ki in range(KD):
                g.dma_start(out=xf[tb % 2][:, ki * 512:(ki + 1) * 512],
                            in_=c.d["sc_xn1F"][ki][:, tb * 512:(tb + 1) * 512]).then_inc(xl2[tb % 2].h, 16)
                xl2[tb % 2].inc(16)

    def ften(t):
        vxd = c.sem("rovxd")
        t.wait_ge(ld.h, 16 * 2 * KD)
        for tb in range(NTB):
            t.wait_ge(vxd.h, tb + 1)
            if tb >= 2:
                t.wait_ge(ev.h, tb - 1)
            for ki in range(KD):
                for gsl, xsl in ((0, xhb), (0, xlb), (KD * 8, xhb)):
                    t.matmul(c.psum[tb % 2][0:8, 0:512],
                             gw[:, gsl + ki * 8:gsl + (ki + 1) * 8],
                             xsl[tb % 2][:, ki * 512:(ki + 1) * 512],
                             start=(ki == 0 and gsl == 0 and xsl is xhb),
                             stop=(ki == KD - 1 and gsl == KD * 8))
            t.drain().then_inc(mm.h, 1)
            mm.inc()
        for tt in range(NTT):
            t.wait_ge(ev.h, tt // 4 + 1)
            if tt >= 2:
                t.wait_ge(cpd.h, tt - 1)
            t.transpose(c.psum[2 + tt % 2][:, 0:8], lg[0:8, tt * 128:(tt + 1) * 128],
                        c.sb["identf"][0:8, 0:8])
            t.drain().then_inc(tr1.h, 1)
            tr1.inc()
            t.wait_ge(sf.h, tt + 1)
            if tt >= 2:
                t.wait_ge(ev2.h, tt - 1)
            t.transpose(c.psum[4 + tt % 2][0:8, 0:128], rtm, c.sb["identf"])
            t.drain().then_inc(tr2.h, 1)
            tr2.inc()

    def fscal(s):
        for tb in range(NTB):
            s.wait_ge(mm.h, tb + 1)
            s.activation(lg[0:8, tb * 512:(tb + 1) * 512], c.psum[tb % 2][0:8, 0:512],
                         AF.Copy).then_inc(ev.h, 1)
            ev.inc()
        for tt in range(NTT):
            s.wait_ge(vm.h, tt + 1)
            s.activation(rtm, rtm, AF.Exp, bias=rwk[:, 1:2],
                         accum_out=rwk[:, 2:3]).then_inc(se.h, 1)
            se.inc()
            s.wait_ge(tr2.h, tt + 1)
            if tt >= 2:
                s.wait_ge(od2[tt % 2].h, 16 * (tt // 2))
            s.activation(rst[tt % 2], c.psum[4 + tt % 2][0:8, 0:128],
                         AF.Copy).then_inc(ev2.h, 1)
            ev2.inc()

    def fvec(v):
        vxd = c.sem("rovxd")
        for tb in range(NTB):
            v.wait_ge(xl2[tb % 2].h, 16 * KD * (tb // 2 + 1))
            if tb >= 2:
                v.wait_ge(mm.h, tb - 1)
            v.tensor_copy(xhb[tb % 2], xf[tb % 2])
            v.tensor_copy(xhf[tb % 2], xhb[tb % 2])
            v.tensor_tensor(xlb[tb % 2], xf[tb % 2], xhf[tb % 2], ALU.subtract)
            c.vmark(v, vxd.h, 1)
            vxd.inc()
        for tt in range(NTT):
            v.wait_ge(tr1.h, tt + 1)
            if tt >= 1:
                v.wait_ge(tr2.h, tt)
            v.tensor_copy(rtm, c.psum[2 + tt % 2][:, 0:8]).then_inc(cpd.h, 1)
            cpd.inc()
            v.reduce_max(rwk[:, 0:1], rtm, axis=AX.X)
            v.tensor_scalar(rwk[:, 1:2], rwk[:, 0:1], -1.0, 0.0, ALU.mult,
                            ALU.add).then_inc(vm.h, 1)
            vm.inc()
            v.wait_ge(se.h, tt + 1)
            v.reciprocal(rwk[:, 3:4], rwk[:, 2:3])
            v.tensor_scalar(rtm, rtm, rwk[:, 3:4], 0.0, ALU.mult, ALU.add)
            v.reduce_max(rwk[:, 4:5], rtm, axis=AX.X)
            v.tensor_scalar(eq1, rtm, rwk[:, 4:5], 0.0, ALU.is_equal, ALU.add)
            v.scalar_tensor_tensor(r2m, eq1, -1e30, rtm, ALU.mult, ALU.add)
            v.reduce_max(rwk[:, 6:7], r2m, axis=AX.X)
            v.tensor_scalar(eq2, r2m, rwk[:, 6:7], 0.0, ALU.is_equal, ALU.add)
            v.tensor_tensor(rwk[:, 7:8], rwk[:, 4:5], rwk[:, 6:7], ALU.add)
            v.tensor_scalar(rwk[:, 7:8], rwk[:, 7:8], 1e-9, 0.0, ALU.add, ALU.add)
            v.reciprocal(rwk[:, 8:9], rwk[:, 7:8])
            v.tensor_tensor(rwk[:, 9:10], rwk[:, 4:5], rwk[:, 8:9], ALU.mult)
            v.tensor_tensor(rwk[:, 10:11], rwk[:, 6:7], rwk[:, 8:9], ALU.mult)
            v.tensor_tensor(rwk[:, 11:12], rwk[:, 9:10], rwk[:, 10:11], ALU.add)
            v.tensor_scalar(rwk[:, 11:12], rwk[:, 11:12], 1e-9, 0.0, ALU.add, ALU.add)
            v.reciprocal(rwk[:, 12:13], rwk[:, 11:12])
            v.tensor_tensor(rwk[:, 13:14], rwk[:, 9:10], rwk[:, 12:13], ALU.mult)
            v.tensor_tensor(rwk[:, 14:15], rwk[:, 10:11], rwk[:, 12:13], ALU.mult)
            v.tensor_scalar(rtm, eq1, rwk[:, 13:14], 0.0, ALU.mult, ALU.add)
            v.scalar_tensor_tensor(rtm, eq2, rwk[:, 14:15], rtm, ALU.mult,
                                   ALU.add).then_inc(sf.h, 1)
            sf.inc()

    def fsync(sync):
        for tt in range(NTT):
            sync.wait_ge(ev2.h, tt + 1)
            sync.dma_start(out=c.d["sc_r"][:, tt * 128:(tt + 1) * 128],
                           in_=rst[tt % 2]).then_inc(od2[tt % 2].h, 16)
            od2[tt % 2].inc(16)
        sync.wait_ge(od2[0].h, od2[0].n)
        sync.wait_ge(od2[1].h, od2[1].n)

    phase(c, {"sync": fsync, "gpsimd": fgps, "vector": fvec, "scalar": fscal,
              "tensor": ften})
    c.pop()



def ph_moe(c):
    c.push()
    gureg = c.region(128, KD * 2048, BF16, "mogu")
    dnreg = c.region(128, KD * 1024, BF16, "modn")
    xb = [c.region(128, KD * 512, BF16, "mox") for _ in range(2)]
    hsb = c.region(128, KD * 512, BF16, "moh")
    sgb = [c.region(128, 512, BF16, "mosg") for _ in range(2)]
    rb = [c.region(128, 512, F32, "morb") for _ in range(2)]
    fprev = [c.region(128, 512, F32, "mofp") for _ in range(2)]
    stage = [c.region(128, 512, F32, "mos") for _ in range(8)]
    wld = c.sem("mowld", sw=True)
    xl2 = [c.sem("moxl0", sw=True), c.sem("moxl1", sw=True)]
    rbl2 = [c.sem("morbl0"), c.sem("morbl1")]
    fpl = c.sem("mofpl")
    mmG = c.sem("mommG")
    sil = c.sem("mosil")
    hdn = c.sem("mohdn")
    mmD = c.sem("mommD")
    acc = c.sem("moacc")
    od2 = [c.sem("mood0"), c.sem("mood1")]

    NE = 8

    def fgps(g):
        for e in range(NE):
            if e >= 1:
                g.wait_ge(mmD.h, e * NTB)
            for ki in range(KD):
                g.dma_start(out=gureg[:, ki * 2048:(ki + 1) * 2048],
                            in_=c.d["gu"][e, ki * 128:(ki + 1) * 128, :]).then_inc(wld.h, 16)
                wld.inc(16)
                g.dma_start(out=dnreg[:, ki * 1024:(ki + 1) * 1024],
                            in_=c.d["dn"][e, ki * 128:(ki + 1) * 128, :]).then_inc(wld.h, 16)
                wld.inc(16)
            for tb in range(NTB):
                et = e * NTB + tb
                if et >= 2:
                    g.wait_ge(mmD.h, et - 1)
                for ki in range(KD):
                    g.dma_start(out=xb[tb % 2][:, ki * 512:(ki + 1) * 512],
                                in_=c.d["sc_xn1F"][ki][:, tb * 512:(tb + 1) * 512]).then_inc(xl2[tb % 2].h, 16)
                    xl2[tb % 2].inc(16)

    def fsync(sync):
        for e in range(NE):
            for tb in range(NTB):
                et = e * NTB + tb
                if et >= 2:
                    sync.wait_ge(acc.h, 8 * (et - 1))
                sync.dma_start(out=rb[tb % 2],
                               in_=c.d["sc_r"][e:e + 1, tb * 512:(tb + 1) * 512]
                               .broadcast_to([128, 512])).then_inc(rbl2[tb % 2].h, 16)
                rbl2[tb % 2].inc(16)
                for mt in range(KD):
                    if e > 0:
                        if et * 8 + mt >= 2:
                            sync.wait_ge(acc.h, 8 * et + mt - 1)
                        sync.dma_start(out=fprev[mt % 2],
                                       in_=c.d["sc_F1F"][mt][:, tb * 512:(tb + 1) * 512]).then_inc(fpl.h, 16)
                        fpl.inc(16)
                    sync.wait_ge(acc.h, 8 * et + mt + 1)
                    sync.dma_start(out=c.d["sc_F1F"][mt][:, tb * 512:(tb + 1) * 512],
                                   in_=stage[mt]).then_inc(od2[et % 2].h, 16)
                    od2[et % 2].inc(16)
        sync.wait_ge(od2[0].h, od2[0].n)
        sync.wait_ge(od2[1].h, od2[1].n)

    def ften(t):
        for e in range(NE):
            t.wait_ge(wld.h, 16 * 2 * KD * (e + 1))
            for tb in range(NTB):
                et = e * NTB + tb
                t.wait_ge(xl2[tb % 2].h, 16 * KD * (et // 2 + 1))
                for grp in range(2):
                    if et > 0 or grp > 0:
                        t.wait_ge(hdn.h, 8 * et + 4 * grp)
                    if grp == 0 and et > 0:
                        t.wait_ge(acc.h, 8 * et)
                    for nt4 in range(4):
                        nt = grp * 4 + nt4
                        for ki in range(KD):
                            t.matmul(c.psum[nt4][:, 0:512],
                                     gureg[:, ki * 2048 + nt * 128: ki * 2048 + (nt + 1) * 128],
                                     xb[tb % 2][:, ki * 512:(ki + 1) * 512],
                                     start=(ki == 0), stop=(ki == KD - 1))
                        for ki in range(KD):
                            t.matmul(c.psum[4 + nt4][:, 0:512],
                                     gureg[:, ki * 2048 + 1024 + nt * 128: ki * 2048 + 1024 + (nt + 1) * 128],
                                     xb[tb % 2][:, ki * 512:(ki + 1) * 512],
                                     start=(ki == 0), stop=(ki == KD - 1))
                    t.drain().then_inc(mmG.h, 1)
                    mmG.inc()
                t.wait_ge(hdn.h, 8 * (et + 1))
                for ki in range(KD):
                    for mt in range(KD):
                        t.matmul(c.psum[mt][:, 0:512],
                                 dnreg[:, ki * 1024 + mt * 128:(ki * 1024 + (mt + 1) * 128)],
                                 hsb[:, ki * 512:(ki + 1) * 512],
                                 start=(ki == 0), stop=(ki == KD - 1),
                                 skip_group_check=True)
                t.drain().then_inc(mmD.h, 1)
                mmD.inc()

    def fscal(s):
        sg = 0
        for e in range(NE):
            for tb in range(NTB):
                et = e * NTB + tb
                for grp in range(2):
                    s.wait_ge(mmG.h, 2 * et + grp + 1)
                    for nt4 in range(4):
                        if sg >= 2:
                            s.wait_ge(hdn.h, sg - 1)
                        s.activation(sgb[sg % 2], c.psum[nt4][:, 0:512],
                                     AF.Silu).then_inc(sil.h, 1)
                        sil.inc()
                        sg += 1

    def fvec(v):
        sg = 0
        for e in range(NE):
            for tb in range(NTB):
                et = e * NTB + tb
                if et > 0:
                    v.wait_ge(mmD.h, et)
                for grp in range(2):
                    for nt4 in range(4):
                        nt = grp * 4 + nt4
                        v.wait_ge(sil.h, sg + 1)
                        v.tensor_tensor(hsb[:, nt * 512:(nt + 1) * 512],
                                        sgb[sg % 2], c.psum[4 + nt4][:, 0:512],
                                        ALU.mult).then_inc(hdn.h, 1)
                        hdn.inc()
                        sg += 1
                v.wait_ge(mmD.h, et + 1)
                v.wait_ge(rbl2[tb % 2].h, 16 * (et // 2 + 1))
                if et >= 1:
                    v.wait_ge(od2[(et - 1) % 2].h, 128 * ((et - 1) // 2 + 1))
                for mt in range(KD):
                    v.tensor_tensor(stage[mt], c.psum[mt][:, 0:512], rb[tb % 2],
                                    ALU.mult)
                    if e > 0:
                        v.wait_ge(fpl.h, 16 * ((e - 1) * NTB * KD + tb * KD + mt + 1))
                        v.tensor_tensor(stage[mt], stage[mt], fprev[mt % 2],
                                        ALU.add)
                    c.vmark(v, acc.h, 1)
                    acc.inc()

    phase(c, {"sync": fsync, "gpsimd": fgps, "vector": fvec, "scalar": fscal,
              "tensor": ften})
    c.pop()



def ph_out(c):
    c.push()
    osb = [c.region(128, KD * 512, F32, "oo") for _ in range(2)]
    inb = [c.region(128, KD * 512, F32, "oi") for _ in range(2)]
    ost = [c.region(128, 4 * 1024, F32, "os") for _ in range(2)]
    al = c.sem("ooal")
    vad = c.sem("oovad")
    trd = c.sem("ootrd")
    evt = c.sem("ooevt")
    od2 = [c.sem("oood0"), c.sem("oood1")]

    def fsync(sync):
        for tb in range(NTB):
            par = tb % 2
            if tb >= 2:
                sync.wait_ge(vad.h, tb - 1)
            for nt in range(KD):
                sl = slice(nt * 512, (nt + 1) * 512)
                tsl = slice(tb * 512, (tb + 1) * 512)
                sync.dma_start(out=osb[par][:, sl], in_=c.d["sc_accF"][nt][:, tsl]).then_inc(al.h, 16)
                sync.dma_start(out=inb[par][:, sl], in_=c.d["sc_F1F"][nt][:, tsl]).then_inc(al.h, 16)
                al.inc(32)
            sync.wait_ge(evt.h, 8 * (tb + 1))
            for tsub in range(4):
                sync.dma_start(out=c.d["out"][tb * 512 + tsub * 128: tb * 512 + (tsub + 1) * 128, :],
                               in_=ost[par][:, tsub * 1024:(tsub + 1) * 1024]).then_inc(od2[par].h, 16)
                od2[par].inc(16)
        sync.wait_ge(od2[0].h, od2[0].n)
        sync.wait_ge(od2[1].h, od2[1].n)

    def fvec(v):
        for tb in range(NTB):
            par = tb % 2
            v.wait_ge(al.h, 256 * (tb + 1))
            for nt in range(KD):
                sl = slice(nt * 512, (nt + 1) * 512)
                v.tensor_tensor(osb[par][:, sl], osb[par][:, sl], inb[par][:, sl], ALU.add)
            c.vmark(v, vad.h, 1)
            vad.inc()
            if tb >= 2:
                v.wait_ge(od2[par].h, 64 * (tb // 2))
            for mt in range(KD):
                v.wait_ge(trd.h, 8 * tb + mt + 1)
                bank = c.psum[mt % 2]
                for tsub in range(4):
                    v.tensor_copy(ost[par][:, tsub * 1024 + mt * 128: tsub * 1024 + (mt + 1) * 128],
                                  bank[:, tsub * 128:(tsub + 1) * 128])
                c.vmark(v, evt.h, 1)
                evt.inc()

    def ften(t):
        for tb in range(NTB):
            par = tb % 2
            t.wait_ge(vad.h, tb + 1)
            for mt in range(KD):
                if 8 * tb + mt >= 2:
                    t.wait_ge(evt.h, 8 * tb + mt - 1)
                bank = c.psum[mt % 2]
                for tsub in range(4):
                    t.transpose(bank[:, tsub * 128:(tsub + 1) * 128],
                                osb[par][:, mt * 512 + tsub * 128: mt * 512 + (tsub + 1) * 128],
                                c.sb["identf"])
                t.drain().then_inc(trd.h, 1)
                trd.inc()

    phase(c, {"sync": fsync, "vector": fvec, "tensor": ften})
    c.pop()



def ph_lin(c):
    BL = 256
    NBL = T // BL
    NC = BL // 64
    HALFPI = float(np.pi / 2)
    c.push()
    P = [c.region(128, KD * BL, F32, f"liP{i}") for i in range(9)]
    Pint = c.region(128, KD * BL, I32, "liPi")
    PintF = Pint.bitcast(F32)
    QH = c.region(128, KD * BL, BF16, "liQH")
    QC = c.region(128, KD * BL, BF16, "liQC")
    KH = [c.region(128, KD * BL, BF16, f"liKH{t}") for t in range(4)]
    vb = c.region(128, KD * BL, BF16, "livb")
    VT = c.region(128, 8 * NC * 64, BF16, "liVT")
    KTs = c.region(128, 64, BF16, "liKT")
    KTsw = c.region(128, 64, BF16, "liKTw")
    GM = [c.region(128, NC * 64, BF16, f"liGM{i}") for i in range(8)]
    SS = [[c.region(128, 64, F32, "liS") for _ in range(2)] for _ in range(8)]
    SSb = [[c.region(128, 64, BF16, "liSb") for _ in range(2)] for _ in range(8)]
    SWr = [[c.region(128, 64, F32, "liSW") for _ in range(2)] for _ in range(8)]
    ABe = c.region(128, 32, F32, "liABe")
    ABw = c.region(128, 64, F32, "liABw")
    AB = c.region(128, 192, F32, "liAB")
    Tw = c.region(128, 64, F32, "liTw")
    yac = [[c.region(128, 256, F32, "liyac") for _ in range(8)] for _ in range(2)]
    lil = c.sem("lil")
    ve = c.sem("live")
    se = c.sem("lise")
    vready = c.sem("livr")
    vbc = c.sem("livbc")
    vte = c.sem("livte")
    tvt = c.sem("litvt")
    tG = c.sem("litG")
    gev = c.sem("ligev")
    ktt = c.sem("liktt")
    ktv = c.sem("liktv")
    tpe = c.sem("litpe")
    vtr = c.sem("livtr")
    yev = c.sem("liyev")
    lod2 = [c.sem("lilod0"), c.sem("lilod1")]
    sinit = c.sem("lisinit")
    SWMAP = [1, 0, 3, 2]
    rst = c.sb["scanrst"][:, 0:KD * BL]

    def tau_sub(v, tau):
        off = 7 + 16 * tau
        v.tensor_tensor(P[2].rearrange("p (a b) -> p a b", b=64),
                        P[3].rearrange("p (a b) -> p a b", b=64),
                        P[3][:, off::64].unsqueeze(2).broadcast_to([128, KD * BL // 64, 64]),
                        ALU.subtract)
        v.tensor_scalar(P[2], P[2], 80.0, 0.0, ALU.min, ALU.add)

    def build_script():
        sc = []
        A = sc.append
        A(("v", lambda v: v.tensor_copy(vb, P[0])))

        def th_clip(v):
            for ki in range(KD):
                sl = slice(ki * BL, (ki + 1) * BL)
                v.tensor_scalar(P[1][:, sl], P[1][:, sl], c.pcol("thb", ki), HALFPI,
                                ALU.add, ALU.min)
            v.tensor_scalar(P[1], P[1], -HALFPI, 0.0, ALU.max, ALU.add)
        A(("v", th_clip))
        A(("s", lambda s: s.activation(P[2], P[1], AF.Square)))

        def lam_scans(v):
            for ki in range(KD):
                sl = slice(ki * BL, (ki + 1) * BL)
                v.tensor_scalar(P[2][:, sl], P[2][:, sl], c.pcol("eta", ki),
                                c.pcol("lamb", ki), ALU.mult, ALU.add)
            v.tensor_tensor_scan(P[3], rst, P[2], 0.0, ALU.mult, ALU.add)
            v.tensor_tensor_scan(P[0], rst, P[1], 0.0, ALU.mult, ALU.add)
            v.tensor_copy(ABe.rearrange("p (a b) -> p a b", b=4),
                          P[0][:, 63::64].rearrange("p (a b) -> p a b", b=4))
            v.tensor_scalar(P[2], P[0], INV2PI, 0.5, ALU.mult, ALU.add)
            v.tensor_copy(Pint, P[2])
            v.tensor_copy(P[4], Pint)
            v.tensor_tensor(PintF, P[4], P[2], ALU.is_gt)
            v.tensor_tensor(P[4], P[4], PintF, ALU.subtract)
            v.scalar_tensor_tensor(P[4], P[4], -TWOPI, P[0], ALU.mult, ALU.add)
        A(("v", lam_scans))
        A(("s", lambda s: s.activation(P[1], P[4], AF.Sin)))

        def trig2(v):
            v.tensor_scalar(P[4], P[0], 1.0, HALFPI, ALU.mult, ALU.add)
            v.tensor_scalar(P[2], P[4], INV2PI, 0.5, ALU.mult, ALU.add)
            v.tensor_copy(Pint, P[2])
            v.tensor_copy(P[0], Pint)
            v.tensor_tensor(PintF, P[0], P[2], ALU.is_gt)
            v.tensor_tensor(P[0], P[0], PintF, ALU.subtract)
            v.scalar_tensor_tensor(P[0], P[0], -TWOPI, P[4], ALU.mult, ALU.add)
        A(("v", trig2))
        A(("s", lambda s: s.activation(P[4], P[0], AF.Sin)))

        def sins_phik(v):
            v.tensor_scalar(P[0], P[1], c.pcol("sgnri"), 0.0, ALU.mult, ALU.add)
            v.tensor_scalar(P[2], P[7], 0.0, 0.0, ALU.min, ALU.add)
        A(("v", sins_phik))
        A(("s", lambda s: s.activation(P[2], P[2], AF.Exp)))

        def phik_fin(v):
            v.scalar_tensor_tensor(P[1], P[7], 0.0, P[2], ALU.max, ALU.add)
            v.tensor_scalar(P[2], P[8], 0.0, 0.0, ALU.min, ALU.add)
        A(("v", phik_fin))
        A(("s", lambda s: s.activation(P[2], P[2], AF.Exp)))

        def rot(v):
            v.scalar_tensor_tensor(P[7], P[8], 0.0, P[2], ALU.max, ALU.add)
            v.tensor_tensor(P[2], P[5], P[4], ALU.mult)
            v.tensor_tensor(P[6], P[6], P[0], ALU.mult)
            v.tensor_tensor(P[5], P[2], P[6], ALU.add)
            v.tensor_tensor(P[2], P[1], P[4], ALU.mult)
            v.tensor_tensor(P[7], P[7], P[0], ALU.mult)
            v.tensor_tensor(P[1], P[2], P[7], ALU.add)
            v.tensor_tensor(P[2].rearrange("p (a b) -> p a b", b=16),
                            P[3].rearrange("p (a b) -> p a b", b=16),
                            P[3][:, 7::16].unsqueeze(2).broadcast_to([128, KD * BL // 16, 16]),
                            ALU.subtract)
        A(("v", rot))
        A(("s", lambda s: s.activation(P[4], P[2], AF.Exp, scale=-1.0)))
        A(("v", lambda v: (v.tensor_tensor(QH, P[5], P[4], ALU.mult),
                           tau_sub(v, 0))))
        for tau in range(4):
            A(("s", lambda s: s.activation(P[4], P[2], AF.Exp)))
            if tau < 3:
                A(("v", lambda v, tau=tau: (v.tensor_tensor(KH[tau], P[1], P[4], ALU.mult),
                                            tau_sub(v, tau + 1))))
            else:
                A(("v", lambda v: v.tensor_tensor(KH[3], P[1], P[4], ALU.mult)))
        A(("s", lambda s: s.activation(P[4], P[3], AF.Exp, scale=-1.0)))

        def qc_ends(v):
            v.tensor_tensor(QC, P[5], P[4], ALU.mult)
            v.tensor_scalar(ABw[:, 32:64], ABe, INV2PI, 0.5, ALU.mult, ALU.add)
            v.tensor_copy(Pint[:, 0:32], ABw[:, 32:64])
            v.tensor_copy(ABw[:, 0:32], Pint[:, 0:32])
            v.tensor_tensor(AB[:, 0:32], ABw[:, 0:32], ABw[:, 32:64], ALU.is_gt)
            v.tensor_tensor(ABw[:, 0:32], ABw[:, 0:32], AB[:, 0:32], ALU.subtract)
            v.scalar_tensor_tensor(ABw[:, 0:32], ABw[:, 0:32], -TWOPI, ABe,
                                   ALU.mult, ALU.add)
        A(("v", qc_ends))
        A(("s", lambda s: s.activation(AB[:, 160:192], ABw[:, 0:32], AF.Sin)))

        def cose_in(v):
            v.tensor_scalar(ABw[:, 32:64], ABw[:, 0:32], 1.0, HALFPI, ALU.mult, ALU.add)
            v.tensor_scalar(AB[:, 0:32], ABw[:, 32:64], INV2PI, 0.5, ALU.mult, ALU.add)
            v.tensor_copy(Pint[:, 0:32], AB[:, 0:32])
            v.tensor_copy(AB[:, 96:128], Pint[:, 0:32])
            v.tensor_tensor(AB[:, 64:96], AB[:, 96:128], AB[:, 0:32], ALU.is_gt)
            v.tensor_tensor(AB[:, 96:128], AB[:, 96:128], AB[:, 64:96], ALU.subtract)
            v.scalar_tensor_tensor(ABw[:, 0:32], AB[:, 96:128], -TWOPI, ABw[:, 32:64],
                                   ALU.mult, ALU.add)
        A(("v", cose_in))
        A(("s", lambda s: s.activation(AB[:, 128:160], ABw[:, 0:32], AF.Sin)))

        def lend(v):
            v.tensor_copy(ABw[:, 0:32].rearrange("p (a b) -> p a b", b=4),
                          P[3][:, 63::64].rearrange("p (a b) -> p a b", b=4))
            v.tensor_copy(ABw[:, 32:64].rearrange("p (a b) -> p a b", b=4),
                          P[3][:, 55::64].rearrange("p (a b) -> p a b", b=4))
            v.tensor_tensor(ABw[:, 32:64], ABw[:, 0:32], ABw[:, 32:64], ALU.subtract)
        A(("v", lend))
        A(("s", lambda s: (s.activation(ABw[:, 0:32], ABw[:, 0:32], AF.Exp, scale=-1.0),
                           s.activation(ABw[:, 32:64], ABw[:, 32:64], AF.Exp, scale=-1.0))))

        def abfin(v):
            sinE, cosE = AB[:, 160:192], AB[:, 128:160]
            v.tensor_tensor(AB[:, 0:32], ABw[:, 0:32], cosE, ALU.mult)
            v.tensor_tensor(AB[:, 96:128], ABw[:, 32:64], cosE, ALU.mult)
            v.tensor_tensor(ABw[:, 0:32], ABw[:, 0:32], sinE, ALU.mult)
            v.tensor_tensor(ABw[:, 32:64], ABw[:, 32:64], sinE, ALU.mult)
            v.tensor_scalar(AB[:, 32:64], ABw[:, 0:32], c.pcol("sgnri"), -1.0,
                            ALU.mult, ALU.mult)
            v.tensor_scalar(AB[:, 64:96], ABw[:, 0:32], c.pcol("sgnri"), 1.0,
                            ALU.mult, ALU.mult)
            v.tensor_scalar(AB[:, 160:192], ABw[:, 32:64], c.pcol("sgnri"), -1.0,
                            ALU.mult, ALU.mult)
            v.tensor_scalar(AB[:, 128:160], ABw[:, 32:64], c.pcol("sgnri"), 1.0,
                            ALU.mult, ALU.mult)
        A(("v", abfin))
        return sc

    script = build_script()
    NV = sum(1 for k, _ in script if k == "v")
    NS = sum(1 for k, _ in script if k == "s")

    def fsync(sync):
        for bi in range(NBL):
            bc = slice(bi * BL, (bi + 1) * BL)
            if bi >= 1:
                sync.wait_ge(vready.h, bi)
            for (reg, key) in [(P[0], "sc_vF"), (P[1], "sc_loraF"),
                               (P[5], "sc_qF"), (P[7], "sc_kF")]:
                sync.dma_start(out=reg.rearrange("p (a t) -> p a t", a=KD),
                               in_=c.d[key].rearrange("a p t -> p a t")[:, :, bc]).then_inc(lil.h, 16)
                lil.inc(16)
            for (reg, key) in [(P[6], "sc_qF"), (P[8], "sc_kF")]:
                for w in range(4):
                    sw = SWMAP[w]
                    sync.dma_start(
                        out=reg[32 * w:32 * (w + 1), :].rearrange("p (a t) -> p a t", a=KD),
                        in_=c.d[key].rearrange("a p t -> p a t")[32 * sw:32 * (sw + 1), :, bc]
                    ).then_inc(lil.h, 16)
                    lil.inc(16)
            for hp in range(8):
                sync.wait_ge(yev.h, 8 * bi + hp + 1)
                sync.dma_start(out=c.d["sc_attnF"][hp][:, bc],
                               in_=yac[bi % 2][hp]).then_inc(lod2[bi % 2].h, 16)
                lod2[bi % 2].inc(16)
        sync.wait_ge(lod2[0].h, lod2[0].n)
        sync.wait_ge(lod2[1].h, lod2[1].n)

    def fvec(v):
        v.memset(c.psum[0][:, 0:512], 0.0)
        v.memset(c.psum[1][:, 0:512], 0.0)
        for hp in range(8):
            for par in range(2):
                v.memset(SS[hp][par], 0.0)
                v.memset(SSb[hp][par], 0.0)
                v.memset(SWr[hp][par], 0.0)
        c.vmark(v, sinit.h, 1)
        sinit.inc()
        for bi in range(NBL):
            ut0 = bi * 32
            if bi >= 1:
                v.wait_ge(tpe.h, 32 * bi)
            v.wait_ge(lil.h, 192 * (bi + 1))
            ns_done = 0
            first = True
            for kind, fn in script:
                if kind == "v":
                    if ns_done:
                        v.wait_ge(se.h, bi * NS + ns_done)
                    fn(v)
                    c.vmark(v, ve.h, 1)
                    ve.inc()
                    if first:
                        c.vmark(v, vbc.h, 1)
                        vbc.inc()
                        first = False
                else:
                    ns_done += 1
            c.vmark(v, vready.h, 1)
            vready.inc()
            v.wait_ge(tvt.h, bi + 1)
            for i in range(8):
                bank = c.psum[6 + i // 4]
                v.tensor_copy(VT.bitcast(I32)[:, i * 128:(i + 1) * 128],
                              bank.bitcast(I32)[:, (i % 4) * 128:((i % 4) + 1) * 128])
            c.vmark(v, vte.h, 1)
            vte.inc()
            for hp in range(8):
                v.wait_ge(tG.h, 8 * bi + hp + 1)
                v.tensor_tensor(GM[hp].rearrange("p (a b) -> p a b", b=64),
                                c.psum[hp % 2][:, 0:NC * 64].rearrange("p (a b) -> p a b", b=64),
                                c.sb["maskG"][:, 0:64].unsqueeze(1).broadcast_to([128, NC, 64]),
                                ALU.mult)
                c.vmark(v, gev.h, 1)
                gev.inc()
            for cci in range(NC):
                gcc = bi * NC + cci
                par = gcc % 2
                for hp in range(8):
                    ib = [2, 6][hp % 2]
                    yb = c.psum[4 + hp % 2][:, (hp // 2) * 64:(hp // 2 + 1) * 64]
                    ut = ut0 + cci * 8 + hp
                    v.wait_ge(ktt.h, ut + 1)
                    v.tensor_copy(KTs.bitcast(I32), c.psum[hp % 2].bitcast(I32)[:, 384:416])
                    v.tensor_copy(KTsw.bitcast(I32)[:, 0:16],
                                  c.psum[hp % 2].bitcast(I32)[:, 400:416])
                    v.tensor_copy(KTsw.bitcast(I32)[:, 16:32],
                                  c.psum[hp % 2].bitcast(I32)[:, 384:400])
                    c.vmark(v, ktv.h, 1)
                    ktv.inc()
                    v.wait_ge(tpe.h, ut + 1)
                    inc_ap = c.psum[ib][:, (hp // 2) * 64:(hp // 2 + 1) * 64]
                    incw_ap = c.psum[ib + 1][:, (hp // 2) * 64:(hp // 2 + 1) * 64]
                    col = hp * 4 + cci
                    a2 = AB[:, col:col + 1]
                    b2n = AB[:, 32 + col:32 + col + 1]
                    b2p = AB[:, 64 + col:64 + col + 1]
                    a3 = AB[:, 96 + col:96 + col + 1]
                    b3p = AB[:, 128 + col:128 + col + 1]
                    b3n = AB[:, 160 + col:160 + col + 1]
                    nxt = 1 - par
                    v.tensor_scalar(Tw, SWr[hp][par], b2n, 0.0, ALU.mult, ALU.add)
                    v.scalar_tensor_tensor(Tw, SS[hp][par], a2, Tw, ALU.mult, ALU.add)
                    v.scalar_tensor_tensor(Tw, inc_ap, a3, Tw, ALU.mult, ALU.add)
                    v.scalar_tensor_tensor(SS[hp][nxt], incw_ap, b3n, Tw, ALU.mult, ALU.add)
                    v.tensor_copy(SSb[hp][nxt], SS[hp][nxt])
                    v.tensor_scalar(Tw, SS[hp][par], b2p, 0.0, ALU.mult, ALU.add)
                    v.scalar_tensor_tensor(Tw, SWr[hp][par], a2, Tw, ALU.mult, ALU.add)
                    v.scalar_tensor_tensor(Tw, incw_ap, a3, Tw, ALU.mult, ALU.add)
                    v.scalar_tensor_tensor(SWr[hp][nxt], inc_ap, b3p, Tw, ALU.mult, ALU.add)
                    if cci == 0 and bi >= 2:
                        v.wait_ge(lod2[bi % 2].h, 128 * (bi // 2))
                    v.tensor_copy(yac[bi % 2][hp][:, cci * 64:(cci + 1) * 64], yb)
                    if cci == NC - 1:
                        c.vmark(v, yev.h, 1)
                        yev.inc()
                    c.vmark(v, vtr.h, 1)
                    vtr.inc()

    def fscal(s):
        for bi in range(NBL):
            nv_done = 0
            for kind, fn in script:
                if kind == "v":
                    nv_done += 1
                else:
                    s.wait_ge(ve.h, bi * NV + nv_done)
                    fn(s)
                    c.smark(s, se.h, 1)
                    se.inc()

    def ften(t):
        t.wait_ge(sinit.h, 1)
        for bi in range(NBL):
            ut0 = bi * 32
            t.wait_ge(vbc.h, bi + 1)
            if bi >= 1:
                t.wait_ge(vtr.h, 32 * bi)
                t.wait_ge(vte.h, bi)
            for hp in range(8):
                for cci in range(NC):
                    i = hp * NC + cci
                    bank = c.psum[6 + hp // 4].bitcast(BF16)
                    scol = (i % 16) * 64
                    for h2 in range(2):
                        h = 2 * hp + h2
                        src = vb[64 * (h % 2):64 * (h % 2) + 64,
                                 (h // 2) * BL + cci * 64:(h // 2) * BL + (cci + 1) * 64]
                        idt = c.sb["identb"][64 * (h % 2):64 * (h % 2) + 64,
                                             64 * (h % 2):64 * (h % 2) + 64]
                        t.transpose(bank[64 * h2:64 * (h2 + 1), scol:scol + 64], src, idt)
            t.drain().then_inc(tvt.h, 1)
            tvt.inc()
            t.wait_ge(vready.h, bi + 1)
            if bi >= 1:
                t.wait_ge(ktv.h, 32 * bi)
            for hp in range(8):
                if 8 * bi + hp >= 2:
                    t.wait_ge(gev.h, 8 * bi + hp - 1)
                bank = c.psum[hp % 2]
                for cci in range(NC):
                    for h2 in range(2):
                        rq = slice(64 * h2, 64 * (h2 + 1))
                        c0 = hp * BL + cci * 64
                        for tau in range(4):
                            t.matmul(bank[64 * h2:64 * h2 + 16 * tau + 8,
                                          cci * 64 + 16 * tau: cci * 64 + 16 * tau + 8],
                                     KH[tau][rq, c0:c0 + 16 * tau + 8],
                                     QH[rq, c0 + 16 * tau:c0 + 16 * tau + 8],
                                     start=True, stop=True, skip_group_check=True)
                            t.matmul(bank[64 * h2:64 * h2 + 16 * (tau + 1),
                                          cci * 64 + 16 * tau + 8: cci * 64 + 16 * (tau + 1)],
                                     KH[tau][rq, c0:c0 + 16 * (tau + 1)],
                                     QH[rq, c0 + 16 * tau + 8:c0 + 16 * (tau + 1)],
                                     start=True, stop=True, skip_group_check=True)
                t.drain().then_inc(tG.h, 1)
                tG.inc()
            t.wait_ge(vte.h, bi + 1)
            t.wait_ge(gev.h, 8 * (bi + 1))
            for cci in range(NC):
                gcc = bi * NC + cci
                par = gcc % 2
                for hp in range(8):
                    ib = [2, 6][hp % 2]
                    ys = slice((hp // 2) * 64, (hp // 2 + 1) * 64)
                    yb = c.psum[4 + hp % 2]
                    ut = ut0 + cci * 8 + hp
                    if ut >= 2:
                        t.wait_ge(vtr.h, ut - 1)
                        t.wait_ge(ktv.h, ut - 1)
                    vslot = VT[:, (hp * NC + cci) * 64:(hp * NC + cci + 1) * 64]
                    for h2 in range(2):
                        rq = slice(64 * h2, 64 * (h2 + 1))
                        ccols = slice(hp * BL + cci * 64, hp * BL + (cci + 1) * 64)
                        idt = c.sb["identb"][rq, rq]
                        t.transpose(c.psum[hp % 2].bitcast(BF16)[rq, 768:832],
                                    KH[3][rq, ccols], idt)
                        t.matmul(yb[64 * h2:64 * (h2 + 1), ys],
                                 vslot[rq, :], GM[hp][rq, cci * 64:(cci + 1) * 64],
                                 start=True, stop=False, skip_group_check=True)
                        t.matmul(yb[64 * h2:64 * (h2 + 1), ys],
                                 SSb[hp][par][rq, :], QC[rq, ccols],
                                 start=False, stop=True, skip_group_check=True)
                    t.drain().then_inc(ktt.h, 1)
                    ktt.inc()
                    t.wait_ge(ktv.h, ut + 1)
                    for h2 in range(2):
                        rq = slice(64 * h2, 64 * (h2 + 1))
                        t.matmul(c.psum[ib][rq, ys],
                                 KTs[rq, :], vslot[rq, :],
                                 start=True, stop=True, skip_group_check=True)
                        t.matmul(c.psum[ib + 1][rq, ys],
                                 KTsw[rq, :], vslot[rq, :],
                                 start=True, stop=True, skip_group_check=True)
                    t.drain().then_inc(tpe.h, 1)
                    tpe.inc()

    phase(c, {"sync": fsync, "vector": fvec, "scalar": fscal, "tensor": ften})
    c.pop()



def ph_flash(c):
    c.push()
    GW = 512
    KBG = GW // 128
    qh = [c.region(64, T, BF16, "flq") for _ in range(2)]
    kh = [c.region(64, T, BF16, "flk") for _ in range(2)]
    vh = [c.region(64, T, F32, "flv") for _ in range(2)]
    vhb = [c.region(64, T, BF16, "flvb") for _ in range(2)]
    va = [c.region(128, 32 * 128, BF16, "flva") for _ in range(2)]
    Pb = [c.region(128, GW, BF16, "flPb") for _ in range(2)]
    Pt = [c.region(128, GW, BF16, "flPt") for _ in range(2)]
    c.sem("flvax")
    rw = c.region(128, 52, F32, "flrw")
    accY = c.region(128, 66, F32, "flacc")
    ysb = c.region(128, 64, F32, "flysb")
    yts = [c.region(64, 128, F32, "flyts") for _ in range(2)]
    hl2 = [c.sem("flhl0", sw=True), c.sem("flhl1", sw=True)]
    vtp = c.sem("flvtp")
    vae = c.sem("flvae")
    zs = c.sem("flzs")
    dv = c.sem("fldv")
    ex = c.sem("flex")
    trs = c.sem("fltrs")
    pte = c.sem("flpte")
    av = c.sem("flav")
    fy = c.sem("flfy")
    fyt = c.sem("flfyt")
    fye = c.sem("flfye")
    od2 = [c.sem("flod0"), c.sem("flod1")]

    sched = []
    for h in range(16):
        for qt in range(FL_NQT or NTT):
            nkb = qt + 1
            ngrp = (nkb + KBG - 1) // KBG
            sched.append((h, qt, nkb, ngrp))
    GTOT = sum(s[3] for s in sched)

    def fgps(g):
        for h in range(FL_NH):
            if h >= 2:
                g.wait_ge(av.h, sum(s[3] for s in sched if s[0] < h - 1))
            src_q = c.d["sc_q1F"][h // 2][64 * (h % 2):64 * (h % 2) + 64, :]
            src_k = c.d["sc_k1F"][h // 2][64 * (h % 2):64 * (h % 2) + 64, :]
            src_v = c.d["sc_v1F"][h // 2][64 * (h % 2):64 * (h % 2) + 64, :]
            g.dma_start(out=qh[h % 2][:], in_=src_q).then_inc(hl2[h % 2].h, 16)
            g.dma_start(out=kh[h % 2][:], in_=src_k).then_inc(hl2[h % 2].h, 16)
            g.dma_start(out=vh[h % 2][:], in_=src_v).then_inc(hl2[h % 2].h, 16)
            hl2[h % 2].inc(48)

    def ften(t):
        gi = 0
        hq = 0
        for h in range(FL_NH):
            par = h % 2
            eg = EGH[h]
            t.wait_ge(hl2[h % 2].h, 48 * (h // 2 + 1))
            t.wait_ge(c.sem("flvc").h, h + 1)
            if h >= 1:
                t.wait_ge(pte.h, sum(s[3] for s in sched if s[0] < h))
            for tt in range(NTT):
                if tt % 8 == 0:
                    W = h * 4 + tt // 8
                    if W >= 1:
                        t.wait_ge(c.sem("flvax").h, W)
                slot = tt % 16
                bank = c.psum[2 + slot % 2]
                t.matmul(bank[:, (slot // 2) * 64:(slot // 2 + 1) * 64],
                         vhb[par][:, tt * 128:(tt + 1) * 128],
                         c.sb["identb"][0:64, 0:64],
                         start=True, stop=True, skip_group_check=True)
                if tt % 8 == 7 or tt == NTT - 1:
                    t.drain().then_inc(vtp.h, 1)
                    vtp.inc()
            for qt in range(FL_NQT or NTT):
                nkb = qt + 1
                ngrp = (nkb + KBG - 1) // KBG
                t.wait_ge(vae.h, h + 1)
                for gg in range(ngrp):
                    w = min(KBG, nkb - gg * KBG) * 128
                    if gi >= 2:
                        t.wait_ge(ex.h, gi - 1)
                    for half in range((w + 511) // 512):
                        hw = min(512, w - half * 512)
                        t.matmul(c.psum[gi % 2][:, half * 512:half * 512 + hw],
                                 qh[par][:, qt * 128:(qt + 1) * 128],
                                 kh[par][:, gg * GW + half * 512: gg * GW + half * 512 + hw],
                                 start=True, stop=True, skip_group_check=True)
                    t.drain().then_inc(zs.h, 1)
                    zs.inc()
                    t.wait_ge(ex.h, gi + 1)
                    if gi >= 2:
                        t.wait_ge(pte.h, gi - 1)
                    ptb = c.psum[2 + gi % 2].bitcast(BF16)
                    for kb in range(w // 128):
                        t.transpose(ptb[:, kb * 128:(kb + 1) * 128],
                                    Pb[gi % 2][:, kb * 128:(kb + 1) * 128],
                                    c.sb["identb"])
                    t.drain().then_inc(trs.h, 1)
                    trs.inc()
                    t.wait_ge(pte.h, gi + 1)
                    if gi >= 2:
                        t.wait_ge(c.sem("flavc").h, gi - 1)
                    if gi % 2 == 1 and hq >= 1:
                        t.wait_ge(fye.h, hq)
                    avcs = slice((gi % 2) * 256, (gi % 2) * 256 + 66)
                    for kb in range(w // 128):
                        kbg = gg * KBG + kb
                        t.matmul(c.psum[4 + gi % 2][:, avcs],
                                 Pt[gi % 2][:, kb * 128:(kb + 1) * 128],
                                 va[par][:, kbg * 128: kbg * 128 + 66],
                                 start=(kb == 0), stop=(kb == w // 128 - 1),
                                 skip_group_check=True)
                    t.drain().then_inc(av.h, 1)
                    av.inc()
                    gi += 1
                t.wait_ge(fy.h, hq + 1)
                if hq >= 1:
                    t.wait_ge(fye.h, hq)
                t.transpose(c.psum[5][0:64, 0:128], ysb, c.sb["identf"])
                t.drain().then_inc(fyt.h, 1)
                fyt.inc()
                hq += 1

    def fvec(v):
        gi = 0
        hq = 0
        vc = c.sem("flvc")
        for h in range(FL_NH):
            par = h % 2
            eg = EGH[h]
            sl = ALIBI_SLOPES[h]
            v.wait_ge(hl2[h % 2].h, 48 * (h // 2 + 1))
            if h >= 2:
                v.wait_ge(av.h, sum(s[3] for s in sched if s[0] < h - 1))
            v.tensor_copy(vhb[par][:], vh[par][:])
            c.vmark(v, vc.h, 1)
            vc.inc()
            vax = c.sem("flvax")
            for tt in range(NTT):
                v.wait_ge(vtp.h, h * 4 + tt // 8 + 1)
                slot = tt % 16
                bank = c.psum[2 + slot % 2]
                rr = tt % eg if eg > 1 else 0
                ucol = c.sb["uflash"][:, h * 4 + rr: h * 4 + rr + 1]
                v.tensor_scalar(va[par][:, tt * 128:tt * 128 + 64],
                                bank[:, (slot // 2) * 64:(slot // 2 + 1) * 64],
                                ucol, 0.0, ALU.mult, ALU.add)
                v.tensor_scalar(va[par][:, tt * 128 + 64:tt * 128 + 65],
                                ucol, 1.0, 0.0, ALU.mult, ALU.add)
                if tt % 8 == 7 or tt == NTT - 1:
                    c.vmark(v, vax.h, 1)
                    vax.inc()
            c.vmark(v, vae.h, 1)
            vae.inc()
            for qt in range(FL_NQT or NTT):
                nkb = qt + 1
                ngrp = (nkb + KBG - 1) // KBG
                for gg in range(ngrp):
                    w = min(KBG, nkb - gg * KBG) * 128
                    nsub = w // 128
                    zb = c.psum[gi % 2]
                    v.wait_ge(zs.h, gi + 1)
                    if gg == ngrp - 1:
                        v.tensor_tensor(zb[:, w - 128:w], zb[:, w - 128:w],
                                        c.sb["maskF"], ALU.add)
                    rb = 24 * (gi % 2)
                    v.reduce_max(rw[:, rb:rb + nsub].unsqueeze(2),
                                 zb[:, 0:w].rearrange("p (a b) -> p a b", b=128),
                                 axis=AX.X)
                    v.tensor_tensor(rw[:, rb:rb + nsub], rw[:, rb:rb + nsub],
                                    c.sb["shflash"][:, h * 8:h * 8 + nsub], ALU.add)
                    v.tensor_scalar(rw[:, rb:rb + nsub], rw[:, rb:rb + nsub],
                                    float(sl * gg * GW), 0.0, ALU.add, ALU.add)
                    v.reduce_max(rw[:, rb + 8:rb + 9], rw[:, rb:rb + nsub], axis=AX.X)
                    if gg == 0:
                        v.tensor_copy(rw[:, 49:50], rw[:, rb + 8:rb + 9])
                        v.tensor_copy(rw[:, 48:49], rw[:, rb + 8:rb + 9])
                    else:
                        v.tensor_tensor(rw[:, 49:50], rw[:, 48:49], rw[:, rb + 8:rb + 9],
                                        ALU.max)
                        v.tensor_tensor(rw[:, rb + 11:rb + 12], rw[:, 48:49],
                                        rw[:, 49:50], ALU.subtract)
                        v.tensor_copy(rw[:, 48:49], rw[:, 49:50])
                    nch = (nsub + eg - 1) // eg
                    for e in range(nch):
                        rho = gg * GW + e * eg * 128 + eg * 64
                        v.tensor_scalar(rw[:, rb + 12 + e:rb + 13 + e], rw[:, 49:50],
                                        -1.0, float(sl * rho), ALU.mult, ALU.add)
                    c.vmark(v, dv.h, 1)
                    dv.inc()
                    v.wait_ge(trs.h, gi + 1)
                    v.tensor_copy(Pt[gi % 2].bitcast(I32)[:, 0:w // 2],
                                  c.psum[2 + gi % 2].bitcast(I32)[:, 0:w // 2])
                    c.vmark(v, pte.h, 1)
                    pte.inc()
                    avc = c.sem("flavc")
                    avcs = slice((gi % 2) * 256, (gi % 2) * 256 + 66)
                    v.wait_ge(av.h, gi + 1)
                    if gg == 0:
                        v.tensor_copy(accY, c.psum[4 + gi % 2][:, avcs])
                    else:
                        v.wait_ge(c.sem("flre").h, gi + 1)
                        v.scalar_tensor_tensor(accY, accY, rw[:, rb + 22:rb + 23],
                                               c.psum[4 + gi % 2][:, avcs],
                                               ALU.mult, ALU.add)
                    c.vmark(v, avc.h, 1)
                    avc.inc()
                    gi += 1
                v.reciprocal(rw[:, 50:51], accY[:, 64:65])
                v.tensor_scalar(ysb, accY[:, 0:64], rw[:, 50:51], 0.0,
                                ALU.mult, ALU.add)
                c.vmark(v, fy.h, 1)
                fy.inc()
                v.wait_ge(fyt.h, hq + 1)
                if hq >= 2:
                    v.wait_ge(od2[hq % 2].h, 16 * (hq // 2))
                v.tensor_copy(yts[hq % 2][:], c.psum[5][0:64, 0:128])
                c.vmark(v, fye.h, 1)
                fye.inc()
                hq += 1

    def fscal(s):
        gi = 0
        re = c.sem("flre")
        for h in range(FL_NH):
            eg = EGH[h]
            for qt in range(FL_NQT or NTT):
                nkb = qt + 1
                ngrp = (nkb + KBG - 1) // KBG
                for gg in range(ngrp):
                    w = min(KBG, nkb - gg * KBG) * 128
                    nsub = w // 128
                    rb = 24 * (gi % 2)
                    s.wait_ge(dv.h, gi + 1)
                    if gg >= 1:
                        s.activation(rw[:, rb + 22:rb + 23], rw[:, rb + 11:rb + 12],
                                     AF.Exp)
                    c.smark(s, re.h, 1)
                    re.inc()
                    nch = (nsub + eg - 1) // eg
                    zb = c.psum[gi % 2]
                    for e in range(nch):
                        wch = min(eg * 128, w - e * eg * 128)
                        s.activation(Pb[gi % 2][:, e * eg * 128:e * eg * 128 + wch],
                                     zb[:, e * eg * 128:e * eg * 128 + wch],
                                     AF.Exp, bias=rw[:, rb + 12 + e:rb + 13 + e])
                    c.smark(s, ex.h, 1)
                    ex.inc()
                    gi += 1

    def fsync(sync):
        hq = 0
        for h in range(FL_NH):
            for qt in range(FL_NQT or NTT):
                sync.wait_ge(fye.h, hq + 1)
                dest = c.d["sc_y1F"][h // 2][64 * (h % 2):64 * (h % 2) + 64,
                                             qt * 128:(qt + 1) * 128]
                sync.dma_start(out=dest, in_=yts[hq % 2][:]).then_inc(od2[hq % 2].h, 16)
                od2[hq % 2].inc(16)
                hq += 1
        sync.wait_ge(od2[0].h, od2[0].n)
        sync.wait_ge(od2[1].h, od2[1].n)

    phase(c, {"sync": fsync, "gpsimd": fgps, "vector": fvec, "scalar": fscal,
              "tensor": ften})
    c.pop()



PHASES = None


def build(com, pcols, dev_until=None, dev_external=False):
    nc = bass.Bass("TRN2", target_bir_lowering=False)
    es = ExitStack()
    phases = [ph_setup, ph_ln0, ph_qkv0, ph_lora, ph_lin, ph_wo, ph_ffn0b,
              ph_ffnint, ph_comb1, ph_qkv1, ph_flash, ph_proj1, ph_router,
              ph_moe, ph_out]
    with es:
        c = Ctx(nc, es, dev_external)
        c.pcols = pcols
        declare_tensors(c, com, dev_external)
        setup_regions(c)
        c.block = es.enter_context(nc.Block())
        for ph in phases:
            ph(c)
            if dev_until is not None and ph.__name__ == dev_until:
                break
    return nc


def kernel(**inputs):
    com, pcols, xs = prep_inputs(inputs)
    nc = build(com, pcols)
    in_maps = [dict(com, x=xs[i]) for i in range(8)]
    res = run_bass_kernel_spmd(nc, in_maps, core_ids=list(range(8)))
    return np.stack([np.asarray(r["out"]) for r in res.results], axis=0)


# version 02 (final)
# speedup vs baseline: 1.1260x; 1.1260x over previous
import numpy as np
import ml_dtypes
from contextlib import ExitStack
import concourse.bass as bass
import concourse.mybir as mybir
from concourse.bass_utils import run_bass_kernel_spmd

F32 = mybir.dt.float32
BF16 = mybir.dt.bfloat16
I32 = mybir.dt.int32
AF = mybir.ActivationFunctionType
ALU = mybir.AluOpType
AX = mybir.AxisListType

T = 4096
D = 1024
NTT = T // 128
KD = D // 128
TBL = 512
NTB = T // TBL
PI = float(np.pi)
INV2PI = float(1.0 / (2 * np.pi))
TWOPI = float(2 * np.pi)
GCONST = 0.7978845608028654

FL_NH = 16
FL_NQT = None
ALIBI_SLOPES = [float((2.0 ** (-0.5)) ** (i + 1)) for i in range(16)]
EGH = [1, 1, 2] + [4] * 13



def _perm_reim():
    p = np.zeros(D, np.int64)
    for r in range(D):
        h, u = divmod(r, 64)
        p[r] = h * 64 + 2 * u if u < 32 else h * 64 + 2 * (u - 32) + 1
    return p


def _headdup(base):
    base = np.asarray(base, np.float32)
    out = np.zeros(1024, np.float32)
    for r in range(1024):
        h, u = divmod(r, 64)
        out[r] = base[h, u % 32]
    return out


def host_consts():
    c = {}
    c["identf"] = np.eye(128, dtype=np.float32)
    su = np.triu(np.ones((64, 64), np.float32))
    mg = np.zeros((128, 128), np.float32)
    for a in range(2):
        for b in range(2):
            mg[a * 64:(a + 1) * 64, b * 64:(b + 1) * 64] = su
    c["maskG"] = mg
    fm = np.where(np.arange(128)[None, :] <= np.arange(128)[:, None], 0.0, -1e30)
    c["maskF"] = fm.astype(np.float32)
    sr = np.ones((128, 2048), np.float32)
    sr[:, ::64] = 0.0
    sr[:, 0] = 1.0
    sr[:, ::64] = 0.0
    c["scanrst"] = sr
    uf = np.zeros((128, 64), np.float32)
    s = np.arange(128, dtype=np.float64)
    for h in range(16):
        sl, eg = ALIBI_SLOPES[h], EGH[h]
        for r in range(4):
            fix = 0.0 if eg == 1 else (128.0 * (r % 2) - 64.0 if eg == 2 else 128.0 * r - 192.0)
            uf[:, h * 4 + r] = np.exp(sl * (s - 64.0 + fix)).astype(np.float32)
    c["uflash"] = uf
    sh = np.zeros((128, 128), np.float32)
    for h in range(16):
        for r in range(8):
            sh[:, h * 8 + r] = ALIBI_SLOPES[h] * (128.0 * r + 64.0)
    c["shflash"] = sh
    return c


def pack_pcol(inp):
    cols = {}
    buf = []

    def add(name, vec):
        vec = np.asarray(vec, np.float32).reshape(-1)
        ntile = (len(vec) + 127) // 128
        cols[name] = len(buf)
        for i in range(ntile):
            v = np.zeros(128, np.float32)
            v[: min(128, len(vec) - i * 128)] = vec[i * 128:(i + 1) * 128]
            buf.append(v)

    perm = _perm_reim()
    add("bq", np.asarray(inp["L0_attn_bq"], np.float32)[perm])
    add("bk", np.asarray(inp["L0_attn_bk"], np.float32)[perm])
    add("bv", inp["L0_attn_bv"])
    add("bo", inp["L0_attn_bo"])
    add("bf1", inp["L0_attn_bf1"])
    add("bf2", inp["L0_attn_bf2"])
    add("n1g", inp["L0_attn_n1g"])
    add("n1b", inp["L0_attn_n1b"])
    add("n2g", inp["L0_attn_n2g"])
    add("n2b", inp["L0_attn_n2b"])
    add("batt", inp["L1_attn_b_attn"])
    add("battqs", np.zeros(1024, np.float32))
    add("bproj", inp["L1_attn_b_proj"])
    add("thb", _headdup(inp["L0_attn_theta_base"]))
    add("lamb", _headdup(inp["L0_attn_lambda_base"]))
    add("eta", _headdup(inp["L0_attn_eta"]))
    sg = np.ones(128, np.float32)
    sg[32:64] = -1.0
    sg[96:128] = -1.0
    add("sgnri", sg)
    add("c_i127sq", np.full(128, 1.0 / (127.0 * 127.0), np.float32))
    add("c_neg1", np.full(128, -1.0, np.float32))
    add("c_i127", np.full(128, 1.0 / 127.0, np.float32))
    add("c_gc", np.full(128, 0.7978845608028654, np.float32))
    add("c_eps5", np.full(128, 1e-5, np.float32))
    add("c_eps6", np.full(128, 1e-6, np.float32))
    add("c_one", np.ones(128, np.float32))
    arr = np.stack(buf, axis=1)
    assert arr.shape[1] <= 192, arr.shape
    pad = np.zeros((128, 192), np.float32)
    pad[:, :arr.shape[1]] = arr
    return pad, cols


def prep_inputs(inp):
    perm = _perm_reim()
    g = lambda k: np.ascontiguousarray(np.asarray(inp[k], np.float32))
    com = {}
    com["Wq"] = np.ascontiguousarray(g("L0_attn_Wq")[:, perm])
    com["Wk"] = np.ascontiguousarray(g("L0_attn_Wk")[:, perm])
    com["Wv"] = g("L0_attn_Wv")
    com["Wo"] = g("L0_attn_Wo")
    com["Wf1"] = g("L0_attn_Wf1")
    com["Wf2"] = g("L0_attn_Wf2")
    com["thw1"] = g("L0_attn_theta_w1")
    w2 = g("L0_attn_theta_w2")
    w2d = np.zeros((48, 1024), np.float32)
    for r in range(1024):
        h, u = divmod(r, 64)
        w2d[:, r] = w2[:, h * 32 + (u % 32)]
    com["thw2"] = np.ascontiguousarray(w2d)
    com["wgt"] = np.ascontiguousarray(np.asarray(inp["L0_ffn_w_gate"], np.int32).T)
    com["wut"] = np.ascontiguousarray(np.asarray(inp["L0_ffn_w_up"], np.int32).T)
    com["wdt"] = np.ascontiguousarray(np.asarray(inp["L0_ffn_w_down"], np.int32).T)
    com["Watt"] = g("L1_attn_W_attn")
    com["Wproj"] = g("L1_attn_W_proj")
    com["temp"] = g("L1_attn_temperature").reshape(1, 1)
    com["gwt"] = np.ascontiguousarray(g("L1_ffn_gate_w").T)
    com["gwt_lo"] = (com["gwt"] -
                     com["gwt"].astype(ml_dtypes.bfloat16).astype(np.float32))
    com["gu"] = g("L1_ffn_gate_up_proj")
    com["dn"] = g("L1_ffn_down_proj")
    com.update(host_consts())
    pc, pcols = pack_pcol(inp)
    com["pcol"] = pc
    x = np.ascontiguousarray(np.asarray(inp["L0_attn_x"], np.float32))
    return com, pcols, [np.ascontiguousarray(x[i]) for i in range(8)]


DT_MAP = {"wgt": I32, "wut": I32, "wdt": I32}

SCRATCH = ["sc_xF", "sc_xnF", "sc_xn2F", "sc_qF", "sc_kF", "sc_vF", "sc_attnF",
           "sc_x1F", "sc_h2F", "sc_A0F", "sc_F0F", "sc_xl1F", "sc_xn1F",
           "sc_accF", "sc_F1F"]
SCRATCH_SHAPES = {"sc_loraF": (8, 128, T), "sc_q1F": (8, 128, T),
                  "sc_k1F": (8, 128, T), "sc_v1F": (8, 128, T),
                  "sc_y1F": (8, 128, T), "sc_r": (8, T)}



class Sem:
    __slots__ = ("h", "n", "sw")

    def __init__(self, h):
        self.h = h
        self.n = 0

    def inc(self, k=1):
        self.n += k
        return self.n


class Ctx:
    def __init__(self, nc, es, dev_external=False):
        self.nc = nc
        self.es = es
        self.dev_external = dev_external
        self.d = {}
        self.sb = {}
        self.ph = 0
        self.sems = {}
        self.gsems = {}
        self.sem_pool = [es.enter_context(nc.semaphore(f"sp{i}")) for i in range(55)]
        self.pool_free = list(self.sem_pool)
        self.wait_base = {}
        self.MEGA = 43008
        self.mega = es.enter_context(nc.sbuf_tensor("mega", [128, self.MEGA], F32))
        self.mega_off = 0
        self.mega_stack = []
        self.psum = [es.enter_context(nc.psum_tensor(f"pb{i}", [128, 512], F32))
                     for i in range(8)]
        self.block = None
        self.PH = None
    def vmark(self, eng, semh, k=1):
        eng.drain().then_inc(semh, k)

    def smark(self, eng, semh, k=1):
        eng.drain().then_inc(semh, k)

    def vedge(self, eng):
        eng.drain()

    def sedge(self, eng):
        eng.drain()

    def gsem(self, name):
        if name not in self.gsems:
            self.gsems[name] = Sem(self.es.enter_context(self.nc.semaphore(name)))
        return self.gsems[name]

    def sem(self, name, sw=False):
        if name in ("PH", "PH2", "vchain", "schain"):
            return self.gsem(name)
        if name not in self.sems:
            if sw:
                h = self.es.enter_context(self.nc.semaphore(name))
                s = Sem(h)
                s.sw = True
                self.sems[name] = s
            else:
                if not self.pool_free:
                    raise RuntimeError("sem pool exhausted")
                h = self.pool_free.pop()
                s = Sem(h)
                s.sw = False
                self.sems[name] = s
        return self.sems[name]

    def phase_reset_sems(self):
        used = [s for s in self.sems.values() if not getattr(s, "sw", False)]
        for s in used:
            self.wait_base[s.h.num] = self.wait_base.get(s.h.num, 0) + s.n
        self.pool_free.extend(s.h for s in used)
        self.sems = {}
        return used

    def alloc(self, words, name=""):
        off = self.mega_off
        words = (words + 1) & ~1
        assert off + words <= self.MEGA, f"SBUF overflow at {name}: {off}+{words}"
        self.mega_off += words
        return off

    def region(self, rows, cols, dt=F32, name=""):
        sz = mybir.dt.size(dt)
        w = (cols * sz + 3) // 4
        off = self.alloc(w, name)
        ap = self.mega[0:rows, off:off + w]
        if dt != F32:
            ap = ap.bitcast(dt)[0:rows, 0:cols]
        return ap

    def push(self):
        self.mega_stack.append(self.mega_off)

    def pop(self):
        self.mega_off = self.mega_stack.pop()

    def pcol(self, name, tile=0):
        i = self.pcols[name] + tile
        return self.sb["pcol"][:, i:i + 1]



_GUARD_OPS = {
    "tensor_tensor": ([0], [1, 2]),
    "tensor_scalar": ([0], [1, 2, 3]),
    "scalar_tensor_tensor": ([0], [1, 2, 3]),
    "tensor_copy": ([0], [1]),
    "tensor_tensor_scan": ([0], [1, 2, 3]),
    "reduce_max": ([0], [1]),
    "reduce_sum": ([0], [1]),
    "tensor_reduce": ([0], [1]),
    "reciprocal": ([0], [1]),
    "memset": ([0], []),
    "memzero": ([0], []),
    "select": ([0], [1, 2, 3]),
    "activation": ([0], [1]),
    "square": ([0], [1]),
    "sqrt": ([0], [1]),
    "copy": ([0], [1]),
    "add": ([0], [1, 2]),
    "mul": ([0], [1, 2]),
    "sign": ([0], [1]),
}


def _ap_span(x):
    try:
        t = x.tensor
        name = t.name
        dt_sz = mybir.dt.size(x.dtype)
        off = x.offset * dt_sz
        nb = x.free_nbytes if hasattr(x, "free_nbytes") else x.nbytes
        bp = x.base_partition() if callable(getattr(x, "base_partition", None)) else 0
        np_ = x.shape[0] if len(x.shape) else 128
        return (name, bp, bp + np_, off, off + max(int(nb), dt_sz))
    except Exception:
        return ("?", 0, 128, 0, 1 << 40)


def _overlap(a, b):
    return (a[0] == b[0] and a[1] < b[2] and b[1] < a[2]
            and a[3] < b[4] and b[3] < a[4])


class EngGuard:
    def __init__(self, eng):
        self._e = eng
        self._hist = []

    def __getattr__(self, name):
        fn = getattr(self._e, name)
        spec = _GUARD_OPS.get(name)
        if spec is None:
            return fn
        outs_i, ins_i = spec

        def wrapped(*args, **kwargs):
            aps = []
            for i in ins_i:
                if i < len(args) and hasattr(args[i], "tensor"):
                    aps.append((0, _ap_span(args[i])))
            for kw in ("bias", "scale"):
                if hasattr(kwargs.get(kw), "tensor"):
                    aps.append((0, _ap_span(kwargs[kw])))
            for i in outs_i:
                if i < len(args) and hasattr(args[i], "tensor"):
                    aps.append((1, _ap_span(args[i])))
            if hasattr(kwargs.get("accum_out"), "tensor"):
                aps.append((1, _ap_span(kwargs["accum_out"])))
            need = False
            for is_w, sp in aps:
                for h_w, h_sp in self._hist:
                    if (is_w or h_w) and _overlap(sp, h_sp):
                        need = True
                        break
                if need:
                    break
            if need or len(self._hist) > 256:
                self._e.drain()
                self._hist = []
            self._hist.extend(aps)
            return fn(*args, **kwargs)
        return wrapped

    def drain(self, *a, **k):
        self._hist = []
        return self._e.drain(*a, **k)


class WaitShim:

    def __init__(self, eng, c):
        self._e = eng
        self._c = c

    def __getattr__(self, name):
        return getattr(self._e, name)

    def wait_ge(self, h, v):
        return self._e.wait_ge(h, v + self._c.wait_base.get(h.num, 0))


ENGS = ["sync", "gpsimd", "vector", "scalar", "tensor"]


def phase(c, fns):
    PH = c.gsem("PH")
    PH2 = c.gsem("PH2")
    for e in ENGS:
        fn = fns.get(e)
        sec = getattr(c.block, e)

        def body(eng, fn=fn, e=e):
            eng = WaitShim(eng, c)
            if e in ("vector", "scalar"):
                eng = EngGuard(eng)
            if c.ph > 0:
                eng.wait_ge(PH.h, 5 * c.ph)
            if fn is not None:
                fn(eng)
            if e == "vector":
                c.vmark(eng, PH.h, 1)
            elif e == "scalar":
                c.smark(eng, PH.h, 1)
            elif e == "tensor":
                eng.drain().then_inc(PH.h, 1)
            else:
                eng.nop().then_inc(PH.h, 1)
        sec(body)
    c.ph += 1
    PH.n = 5 * c.ph
    c.phase_reset_sems()


def declare_tensors(c, com, dev_external=False):
    nc = c.nc
    c.d["x"] = nc.dram_tensor("x", [T, D], F32, kind="ExternalInput").ap()
    for k, v in com.items():
        dt = DT_MAP.get(k, F32)
        c.d[k] = nc.dram_tensor(k, list(v.shape), dt, kind="ExternalInput").ap()
    kind = "ExternalOutput" if dev_external else "Internal"
    for s in SCRATCH:
        c.d[s] = nc.dram_tensor(s, [KD, 128, T], F32, kind=kind).ap()
    for s, shp in SCRATCH_SHAPES.items():
        c.d[s] = nc.dram_tensor(s, list(shp), F32, kind=kind).ap()
    c.d["out"] = nc.dram_tensor("out", [T, D], F32, kind="ExternalOutput").ap()



def setup_regions(c):
    c.sb["pcol"] = c.region(128, 192, F32, "pcol")
    c.sb["identf"] = c.region(128, 128, F32, "identf")
    c.sb["identb"] = c.region(128, 128, BF16, "identb")
    c.sb["maskG"] = c.region(128, 128, F32, "maskG")
    c.sb["maskF"] = c.region(128, 128, F32, "maskF")
    c.sb["scanrst"] = c.region(128, 2048, F32, "scanrst")
    c.sb["uflash"] = c.region(128, 64, F32, "uflash")
    c.sb["shflash"] = c.region(128, 128, F32, "shflash")
    c.sb["temp"] = c.region(128, 4, F32, "temp")
    c.sb["onesb"] = c.region(128, 2, BF16, "onesb")
    c.sb["onesrow"] = c.region(1, 128, F32, "onesrow")
    for val, col in [(-1.0, "c_neg1"), (1.0 / 127.0, "c_i127"),
                     (0.7978845608028654, "c_gc")]:
        c.nc.const_aps.register(F32, val, c.pcol(col))


def ph_setup(c):
    ld = c.sem("s0ld")
    vs = c.sem("s0v")

    def fsync(sync):
        for nm, dst in [("pcol", c.sb["pcol"]), ("identf", c.sb["identf"]),
                        ("maskG", c.sb["maskG"]), ("maskF", c.sb["maskF"]),
                        ("scanrst", c.sb["scanrst"]), ("uflash", c.sb["uflash"]),
                        ("shflash", c.sb["shflash"])]:
            sync.dma_start(out=dst, in_=c.d[nm]).then_inc(ld.h, 16)
            ld.inc(16)
        sync.dma_start(out=c.sb["temp"][:, 0:1],
                       in_=c.d["temp"].broadcast_to([128, 1])).then_inc(ld.h, 16)
        ld.inc(16)
        sync.wait_ge(vs.h, 1)

    ldg = c.sem("s0ldg", sw=True)

    def fgps(g):
        g.dma_start(out=c.sb["identb"], in_=c.d["identf"]).then_inc(ldg.h, 16)
        ldg.inc(16)

    def fvec(v):
        v.wait_ge(ld.h, ld.n)
        v.wait_ge(ldg.h, ldg.n)
        v.tensor_scalar(c.sb["temp"][:, 1:2], c.sb["temp"][:, 0:1], 8.0, 0.0,
                        ALU.mult, ALU.add)
        c.vedge(v)
        v.reciprocal(c.sb["temp"][:, 2:3], c.sb["temp"][:, 1:2])
        c.vedge(v)
        for i in range(8):
            v.tensor_scalar(c.pcol("battqs", i), c.pcol("batt", i),
                            c.sb["temp"][:, 2:3], 0.0, ALU.mult, ALU.add)
        v.memset(c.sb["onesb"], 1.0)
        v.memset(c.sb["onesrow"], 1.0)
        c.vmark(v, vs.h, 1)
        vs.inc()

    phase(c, {"sync": fsync, "gpsimd": fgps, "vector": fvec})



def ph_ln0(c):
    c.push()
    xin = [c.region(128, 1024, F32, "xin") for _ in range(2)]
    xnsb = [c.region(128, 1024, F32, "xn") for _ in range(2)]
    xnnsb = [c.region(128, 1024, F32, "xnn") for _ in range(2)]
    stage = [c.region(128, 1024, F32, "stg") for _ in range(6)]
    x2 = c.region(128, 1024, F32, "x2")
    st = c.region(128, 16, F32, "stats")
    ld2 = [c.sem("l0ld0"), c.sem("l0ld1")]
    vst = c.sem("l0v")
    sst = c.sem("l0s")
    tst = c.sem("l0t")
    est = c.sem("l0e")
    dst2 = [c.sem("l0d0"), c.sem("l0d1")]
    vv = c.sem("l0vv")

    xd = c.d["x"]

    def fsync(sync):
        sync.dma_start(out=xin[0], in_=xd[0:128, :]).then_inc(ld2[0].h, 16)
        ld2[0].inc(16)
        for tt in range(NTT):
            if tt + 1 < NTT:
                if tt >= 1:
                    sync.wait_ge(vst.h, 2 * (tt - 1) + 1)
                    sync.wait_ge(sst.h, 4 * (tt - 1) + 1)
                    sync.wait_ge(tst.h, 3 * (tt - 1) + 1)
                sync.dma_start(out=xin[(tt + 1) % 2],
                               in_=xd[(tt + 1) * 128:(tt + 2) * 128, :]).then_inc(ld2[(tt + 1) % 2].h, 16)
                ld2[(tt + 1) % 2].inc(16)
            sync.wait_ge(est.h, 3 * tt + 3)
            for k, scn in enumerate(["sc_xF", "sc_xnF", "sc_xn2F"]):
                sg = stage[k * 2 + tt % 2]
                dest = c.d[scn].rearrange("a p t -> p a t")[:, :, tt * 128:(tt + 1) * 128]
                sync.dma_start(out=dest, in_=sg.rearrange("p (a t) -> p a t", a=KD)).then_inc(dst2[tt % 2].h, 16)
                dst2[tt % 2].inc(16)
        sync.wait_ge(dst2[0].h, dst2[0].n)
        sync.wait_ge(dst2[1].h, dst2[1].n)

    def fvec(v):
        for tt in range(NTT):
            par = tt % 2
            v.wait_ge(ld2[par].h, 16 * (tt // 2 + 1))
            v.reduce_sum(st[:, 0:1], xin[par], axis=AX.X)
            c.vedge(v)
            v.tensor_scalar(st[:, 1:2], st[:, 0:1], 1.0 / D, 0.0, ALU.mult, ALU.add)
            v.wait_ge(sst.h, 4 * tt + 1)
            v.tensor_scalar(st[:, 3:4], st[:, 2:3], 1.0 / D, 0.0, ALU.mult, ALU.add)
            v.tensor_tensor(st[:, 4:5], st[:, 1:2], st[:, 1:2], ALU.mult)
            c.vedge(v)
            v.tensor_tensor(st[:, 5:6], st[:, 3:4], st[:, 4:5], ALU.subtract).then_inc(vv.h, 1)
            vv.inc()
            v.wait_ge(sst.h, 4 * tt + 2)
            v.reciprocal(st[:, 7:8], st[:, 6:7])
            c.vedge(v)
            v.tensor_scalar(xnsb[par], xin[par], st[:, 1:2], st[:, 7:8],
                            ALU.subtract, ALU.mult).then_inc(vst.h, 1)
            vst.inc()
            v.reduce_sum(st[:, 8:9], xnsb[par], axis=AX.X)
            c.vedge(v)
            v.tensor_scalar(st[:, 9:10], st[:, 8:9], 1.0 / D, 0.0, ALU.mult, ALU.add)
            v.wait_ge(sst.h, 4 * tt + 3)
            v.tensor_scalar(st[:, 11:12], st[:, 10:11], 1.0 / D, 0.0, ALU.mult, ALU.add)
            v.tensor_tensor(st[:, 12:13], st[:, 9:10], st[:, 9:10], ALU.mult)
            c.vedge(v)
            v.tensor_tensor(st[:, 13:14], st[:, 11:12], st[:, 12:13], ALU.subtract).then_inc(vv.h, 1)
            vv.inc()
            v.wait_ge(sst.h, 4 * tt + 4)
            v.reciprocal(st[:, 15:16], st[:, 14:15])
            c.vedge(v)
            v.tensor_scalar(xnnsb[par], xnsb[par], st[:, 9:10], st[:, 15:16],
                            ALU.subtract, ALU.mult).then_inc(vst.h, 1)
            vst.inc()
            if tt >= 2:
                v.wait_ge(dst2[par].h, 48 * (tt // 2))
            for k in range(3):
                v.wait_ge(tst.h, 3 * tt + k + 1)
                bank = c.psum[2 * k].bitcast(F32)
                bank2 = c.psum[2 * k + 1].bitcast(F32)
                sg = stage[k * 2 + par]
                if k == 2:
                    for ki in range(KD):
                        b = (bank if ki < 4 else bank2)
                        v.tensor_scalar(sg[:, ki * 128:(ki + 1) * 128],
                                        b[:, (ki % 4) * 128:(ki % 4 + 1) * 128],
                                        c.pcol("n1g", ki), c.pcol("n1b", ki),
                                        ALU.mult, ALU.add)
                else:
                    v.tensor_copy(sg[:, 0:512], bank[:, 0:512])
                    v.tensor_copy(sg[:, 512:1024], bank2[:, 0:512])
                c.vmark(v, est.h, 1)
                est.inc()

    def fscal(s):
        for tt in range(NTT):
            par = tt % 2
            s.wait_ge(ld2[par].h, 16 * (tt // 2 + 1))
            if tt >= 1:
                s.wait_ge(vv.h, 2 * tt - 1)
            s.activation(x2, xin[par], AF.Square, accum_out=st[:, 2:3]).then_inc(sst.h, 1)
            sst.inc()
            s.wait_ge(vv.h, 2 * tt + 1)
            s.activation(st[:, 6:7], st[:, 5:6], AF.Sqrt,
                         bias=c.pcol("c_eps5"), scale=1.0).then_inc(sst.h, 1)
            sst.inc()
            s.wait_ge(vst.h, 2 * tt + 1)
            if tt >= 1:
                s.wait_ge(vv.h, 2 * tt)
            s.activation(x2, xnsb[par], AF.Square, accum_out=st[:, 10:11]).then_inc(sst.h, 1)
            sst.inc()
            s.wait_ge(vv.h, 2 * tt + 2)
            s.activation(st[:, 14:15], st[:, 13:14], AF.Sqrt,
                         bias=c.pcol("c_eps6"), scale=1.0).then_inc(sst.h, 1)
            sst.inc()

    def ften(t):
        for tt in range(NTT):
            par = tt % 2
            srcs = [xin[par], xnsb[par], xnnsb[par]]
            waits = [(ld2[par], 16 * (tt // 2 + 1)), (vst, 2 * tt + 1), (vst, 2 * tt + 2)]
            for k in range(3):
                sem, val = waits[k]
                t.wait_ge(sem.h, val)
                if tt >= 1:
                    t.wait_ge(est.h, 3 * (tt - 1) + k + 1)
                bank = c.psum[2 * k].bitcast(F32)
                bank2 = c.psum[2 * k + 1].bitcast(F32)
                for ki in range(KD):
                    b = (bank if ki < 4 else bank2)
                    t.transpose(b[:, (ki % 4) * 128:(ki % 4 + 1) * 128],
                                srcs[k][:, ki * 128:(ki + 1) * 128], c.sb["identf"])
                t.drain().then_inc(tst.h, 1)
                tst.inc()

    phase(c, {"sync": fsync, "vector": fvec, "scalar": fscal, "tensor": ften})
    c.pop()



def emit_gemm(c, phname, xkey, xt, jobs):
    Kt = xt
    xreg = c.region(128, xt * T, BF16, phname + "x")
    wbuf = [c.region(128, Kt * 512, BF16, phname + "w") for _ in range(2)]
    stage = [c.region(128, 512, F32, phname + "s") for _ in range(8)]
    addb = [c.region(128, 512, F32, phname + "a") for _ in range(4)]
    xld = c.sem(phname + "xld", sw=True)
    wld2 = [c.sem(phname + "wld0", sw=True), c.sem(phname + "wld1", sw=True)]
    wfree = c.sem(phname + "wfree")
    mmS = c.sem(phname + "mm")
    ev = c.sem(phname + "ev")
    od2 = [c.sem(phname + "od0"), c.sem(phname + "od1")]
    ald = c.sem(phname + "ald")

    units = []
    for j, job in enumerate(jobs):
        for ns in range(job["N"] // 512):
            units.append((j, ns))
    nsteps_per_unit = 2 * 4
    has_add = any(job.get("add_from") for job in jobs)

    def fgps(g):
        for ki in range(xt):
            g.dma_start(out=xreg[:, ki * T:(ki + 1) * T],
                        in_=c.d[xkey][ki]).then_inc(xld.h, 16)
            xld.inc(16)
        for u, (j, ns) in enumerate(units):
            job = jobs[j]
            if u >= 2:
                g.wait_ge(wfree.h, u - 1)
            W = c.d[job["wkey"]]
            n0 = job.get("noff", 0) + ns * 512
            for ki in range(Kt):
                g.dma_start(out=wbuf[u % 2][:, ki * 512:(ki + 1) * 512],
                            in_=W[ki * 128:(ki + 1) * 128, n0:n0 + 512]).then_inc(wld2[u % 2].h, 16)
                wld2[u % 2].inc(16)

    def fsync(sync):
        s = 0
        for u, (j, ns) in enumerate(units):
            job = jobs[j]
            outk = job["out"]
            for tg in range(2):
                for nt4 in range(4):
                    nt_g = ns * 4 + nt4
                    if job.get("add_from"):
                        if s >= 1:
                            sync.wait_ge(ev.h, 4 * s)
                        for tb4 in range(4):
                            tb = tg * 4 + tb4
                            sync.dma_start(
                                out=addb[tb4],
                                in_=c.d[job["add_from"]][nt_g][:, tb * 512:(tb + 1) * 512]
                            ).then_inc(ald.h, 16)
                            ald.inc(16)
                    sync.wait_ge(ev.h, 4 * (s + 1))
                    for tb4 in range(4):
                        tb = tg * 4 + tb4
                        sync.dma_start(
                            out=c.d[outk][nt_g][:, tb * 512:(tb + 1) * 512],
                            in_=stage[(s % 2) * 4 + tb4]).then_inc(od2[s % 2].h, 16)
                        od2[s % 2].inc(16)
                    s += 1
        sync.wait_ge(od2[0].h, od2[0].n)
        sync.wait_ge(od2[1].h, od2[1].n)

    def ften(t):
        t.wait_ge(xld.h, 16 * xt)
        s = 0
        for u, (j, ns) in enumerate(units):
            job = jobs[j]
            t.wait_ge(wld2[u % 2].h, 16 * Kt * (u // 2 + 1))
            for tg in range(2):
                for nt4 in range(4):
                    if s >= 2:
                        t.wait_ge(ev.h, 4 * (s - 1))
                    banks = [c.psum[(s % 2) * 4 + i] for i in range(4)]
                    for ki in range(Kt):
                        lhs = wbuf[u % 2][:, ki * 512 + nt4 * 128: ki * 512 + (nt4 + 1) * 128]
                        for tb4 in range(4):
                            tb = tg * 4 + tb4
                            t.matmul(banks[tb4][:, 0:512],
                                     lhs,
                                     xreg[:, ki * T + tb * 512: ki * T + (tb + 1) * 512],
                                     start=(ki == 0), stop=(ki == Kt - 1),
                                     skip_group_check=True)
                    t.drain().then_inc(mmS.h, 1)
                    mmS.inc()
                    s += 1
            t.drain().then_inc(wfree.h, 1)
            wfree.inc()

    assert len({bool(job.get("add_from")) for job in jobs}) == 1

    def fvec(v):
        s = 0
        nald = 0
        for u, (j, ns) in enumerate(units):
            job = jobs[j]
            for tg in range(2):
                for nt4 in range(4):
                    nt_g = ns * 4 + nt4
                    v.wait_ge(mmS.h, s + 1)
                    if s >= 2:
                        v.wait_ge(od2[s % 2].h, 64 * (s // 2))
                    if job.get("add_from"):
                        nald += 64
                        v.wait_ge(ald.h, nald)
                    for tb4 in range(4):
                        sg = stage[(s % 2) * 4 + tb4]
                        ps = c.psum[(s % 2) * 4 + tb4][:, 0:512]
                        if job.get("scale") == "temp":
                            v.tensor_scalar(sg, ps, c.sb["temp"][:, 2:3],
                                            c.pcol(job["bias"], job.get("bias_off", 0) + nt_g),
                                            ALU.mult, ALU.add)
                        elif job.get("bias"):
                            v.tensor_scalar(sg, ps,
                                            c.pcol(job["bias"], job.get("bias_off", 0) + nt_g),
                                            0.0, ALU.add, ALU.add)
                        else:
                            v.tensor_copy(sg, ps)
                        if job.get("add_from"):
                            v.tensor_tensor(sg, sg, addb[tb4], ALU.add)
                    c.vmark(v, ev.h, 4)
                    ev.inc(4)
                    s += 1

    phase(c, {"sync": fsync, "gpsimd": fgps, "vector": fvec,
              "tensor": ften})


def ph_qkv0(c):
    c.push()
    emit_gemm(c, "qkv0", "sc_xn2F", KD, [
        dict(wkey="Wq", N=1024, out="sc_qF", bias="bq"),
        dict(wkey="Wk", N=1024, out="sc_kF", bias="bk"),
        dict(wkey="Wv", N=1024, out="sc_vF", bias="bv"),
    ])
    c.pop()


def ph_qkv1(c):
    c.push()
    emit_gemm(c, "qkv1", "sc_xn1F", KD, [
        dict(wkey="Watt", N=1024, out="sc_q1F", bias="battqs",
             scale="temp", noff=0),
        dict(wkey="Watt", N=1024, out="sc_k1F", bias="batt", bias_off=8, noff=1024),
        dict(wkey="Watt", N=1024, out="sc_v1F", bias="batt", bias_off=16, noff=2048),
    ])
    c.pop()


def ph_proj1(c):
    c.push()
    emit_gemm(c, "proj1", "sc_y1F", KD, [
        dict(wkey="Wproj", N=1024, out="sc_accF", bias="bproj",
             add_from="sc_xl1F"),
    ])
    c.pop()


def ph_lora(c):
    c.push()
    xreg = c.region(128, KD * T, BF16, "lorax")
    w1 = c.region(128, KD * 48, BF16, "w1")
    w2 = c.region(48, 1024, BF16, "w2")
    lt = c.region(48, T, BF16, "lt")
    stage = [c.region(128, 512, F32, "lstg") for _ in range(2)]
    xld = c.sem("lrxld", sw=True)
    mm1 = c.sem("lrmm1")
    tev = c.sem("lrtev")
    mm2 = c.sem("lrmm2")
    ev2 = c.sem("lrev2")
    od2 = [c.sem("lrod0"), c.sem("lrod1")]

    def fgps(g):
        for ki in range(KD):
            g.dma_start(out=xreg[:, ki * T:(ki + 1) * T],
                        in_=c.d["sc_xn2F"][ki]).then_inc(xld.h, 16)
            xld.inc(16)
            g.dma_start(out=w1[:, ki * 48:(ki + 1) * 48],
                        in_=c.d["thw1"][ki * 128:(ki + 1) * 128, :]).then_inc(xld.h, 16)
            xld.inc(16)
        g.dma_start(out=w2, in_=c.d["thw2"]).then_inc(xld.h, 16)
        xld.inc(16)

    def ften(t):
        t.wait_ge(xld.h, xld.n)
        for tb in range(NTB):
            for ki in range(KD):
                t.matmul(c.psum[tb][0:48, 0:512], w1[:, ki * 48:(ki + 1) * 48],
                         xreg[:, ki * T + tb * 512: ki * T + (tb + 1) * 512],
                         start=(ki == 0), stop=(ki == KD - 1))
            t.drain().then_inc(mm1.h, 1)
            mm1.inc()
        s = 0
        t.wait_ge(tev.h, NTB)
        for nt in range(8):
            for tb in range(NTB):
                if s >= 4:
                    t.wait_ge(ev2.h, s - 3)
                t.matmul(c.psum[s % 4][:, 0:512], w2[:, nt * 128:(nt + 1) * 128],
                         lt[:, tb * 512:(tb + 1) * 512],
                         start=True, stop=True).then_inc(mm2.h, 1)
                mm2.inc()
                s += 1

    def fscal(sc):
        for tb in range(NTB):
            sc.wait_ge(mm1.h, tb + 1)
            sc.activation(lt[:, tb * 512:(tb + 1) * 512], c.psum[tb][0:48, 0:512],
                          AF.Tanh).then_inc(tev.h, 1)
            tev.inc()
        s = 0
        for nt in range(8):
            for tb in range(NTB):
                sc.wait_ge(mm2.h, s + 1)
                if s >= 2:
                    sc.wait_ge(od2[s % 2].h, 16 * (s // 2))
                sc.activation(stage[s % 2], c.psum[s % 4][:, 0:512],
                              AF.Copy).then_inc(ev2.h, 1)
                ev2.inc()
                s += 1

    def fsync(sync):
        s = 0
        for nt in range(8):
            for tb in range(NTB):
                sync.wait_ge(ev2.h, s + 1)
                sync.dma_start(out=c.d["sc_loraF"][nt][:, tb * 512:(tb + 1) * 512],
                               in_=stage[s % 2]).then_inc(od2[s % 2].h, 16)
                od2[s % 2].inc(16)
                s += 1
        sync.wait_ge(od2[0].h, od2[0].n)
        sync.wait_ge(od2[1].h, od2[1].n)

    phase(c, {"sync": fsync, "gpsimd": fgps, "scalar": fscal, "tensor": ften})
    c.pop()



def ph_wo(c):
    c.push()
    ab2 = [c.region(128, KD * 512, BF16, "wox") for _ in range(2)]
    wo = c.region(128, KD * 1024, BF16, "wow")
    x1sb = [c.region(128, KD * 512, F32, "wox1") for _ in range(2)]
    h2sb = [c.region(128, KD * 512, F32, "woh2") for _ in range(2)]
    xnb = [c.region(128, KD * 512, F32, "woxn") for _ in range(2)]
    x1c = c.region(128, KD * 512, BF16, "wox1c")
    x2c = c.region(128, KD * 512, BF16, "wox2c")
    srow = c.region(1, 16, F32, "worow")
    r1 = c.region(1, 512, F32, "wor1")
    r2 = c.region(1, 512, F32, "wor2")
    rm = c.region(1, 512, F32, "worm")
    rv = c.region(1, 512, F32, "worv")
    tmp = c.region(128, 512, F32, "wotmp")
    ld = c.sem("wold", sw=True)
    al = c.sem("woal")
    mmA = c.sem("womA")
    evA = c.sem("woevA")
    vad = c.sem("wovad")
    vcst = c.sem("wovcst")
    ssq = c.sem("wossq")
    smm = c.sem("wosmm")
    stv = c.sem("wostv")
    bc = c.sem("wobc")
    nrm = c.sem("wonrm")
    od2 = [c.sem("wood0"), c.sem("wood1")]

    xl2 = [c.sem("woxl0", sw=True), c.sem("woxl1", sw=True)]

    def fgps(g):
        for ki in range(KD):
            g.dma_start(out=wo[:, ki * 1024:(ki + 1) * 1024],
                        in_=c.d["Wo"][ki * 128:(ki + 1) * 128, :]).then_inc(ld.h, 16)
            ld.inc(16)
        for tb in range(NTB):
            if tb >= 2:
                g.wait_ge(mmA.h, 2 * (tb - 1))
            for ki in range(KD):
                g.dma_start(out=ab2[tb % 2][:, ki * 512:(ki + 1) * 512],
                            in_=c.d["sc_attnF"][ki][:, tb * 512:(tb + 1) * 512]).then_inc(xl2[tb % 2].h, 16)
                xl2[tb % 2].inc(16)

    def fsync(sync):
        for tb in range(NTB):
            par = tb % 2
            if tb >= 2:
                sync.wait_ge(vad.h, tb - 1)
            for nt in range(KD):
                sync.dma_start(out=xnb[par][:, nt * 512:(nt + 1) * 512],
                               in_=c.d["sc_xnF"][nt][:, tb * 512:(tb + 1) * 512]).then_inc(al.h, 16)
                al.inc(16)
            sync.wait_ge(vad.h, tb + 1)
            for nt in range(KD):
                sync.dma_start(out=c.d["sc_x1F"][nt][:, tb * 512:(tb + 1) * 512],
                               in_=x1sb[par][:, nt * 512:(nt + 1) * 512]).then_inc(od2[par].h, 16)
                od2[par].inc(16)
            sync.wait_ge(nrm.h, tb + 1)
            for nt in range(KD):
                sync.dma_start(out=c.d["sc_h2F"][nt][:, tb * 512:(tb + 1) * 512],
                               in_=h2sb[par][:, nt * 512:(nt + 1) * 512]).then_inc(od2[par].h, 16)
                od2[par].inc(16)
        sync.wait_ge(od2[0].h, od2[0].n)
        sync.wait_ge(od2[1].h, od2[1].n)

    def ften(t):
        t.wait_ge(ld.h, 16 * KD)
        for tb in range(NTB):
            t.wait_ge(xl2[tb % 2].h, 16 * KD * (tb // 2 + 1))
            for grp in range(2):
                if tb > 0 or grp > 0:
                    t.wait_ge(evA.h, 8 * tb + 4 * grp)
                for nt4 in range(4):
                    nt = grp * 4 + nt4
                    for ki in range(KD):
                        t.matmul(c.psum[nt4][:, 0:512],
                                 wo[:, ki * 1024 + nt * 128: ki * 1024 + (nt + 1) * 128],
                                 ab2[tb % 2][:, ki * 512:(ki + 1) * 512],
                                 start=(ki == 0), stop=(ki == KD - 1))
                t.drain().then_inc(mmA.h, 1)
                mmA.inc()
            t.wait_ge(vcst.h, tb + 1)
            t.wait_ge(ssq.h, tb + 1)
            if tb > 0:
                t.wait_ge(stv.h, tb)
            for ki in range(KD):
                t.matmul(c.psum[4][0:1, 0:512], c.sb["onesb"][:, 0:1],
                         x1c[:, ki * 512:(ki + 1) * 512],
                         start=(ki == 0), stop=(ki == KD - 1))
            for ki in range(KD):
                t.matmul(c.psum[5][0:1, 0:512], c.sb["onesb"][:, 0:1],
                         x2c[:, ki * 512:(ki + 1) * 512],
                         start=(ki == 0), stop=(ki == KD - 1))
            t.drain().then_inc(smm.h, 1)
            smm.inc()
            t.wait_ge(stv.h, tb + 1)
            if tb > 0:
                t.wait_ge(nrm.h, tb)
            t.matmul(c.psum[6][:, 0:512], c.sb["onesrow"][0:1, 0:128], r1[0:1, 0:512],
                     start=True, stop=True)
            t.matmul(c.psum[7][:, 0:512], c.sb["onesrow"][0:1, 0:128], r2[0:1, 0:512],
                     start=True, stop=True).then_inc(bc.h, 1)
            bc.inc()

    def fscal(s):
        for tb in range(NTB):
            par = tb % 2
            s.wait_ge(vad.h, tb + 1)
            for nt in range(KD):
                s.activation(x2c[:, nt * 512:(nt + 1) * 512],
                             x1sb[par][:, nt * 512:(nt + 1) * 512], AF.Square)
            c.smark(s, ssq.h, 1)
            ssq.inc()
            s.wait_ge(c.sem("wovv").h, tb + 1)
            s.activation(rv[0:1, 0:512], rv[0:1, 0:512], AF.Sqrt,
                         bias=c.pcol("c_eps6")[0:1, 0:1], scale=1.0).then_inc(c.sem("wosq2").h, 1)
            c.sem("wosq2").inc()

    vv = c.sem("wovv")
    sq2 = c.sem("wosq2")

    def fvec(v):
        for tb in range(NTB):
            par = tb % 2
            for grp in range(2):
                v.wait_ge(mmA.h, 2 * tb + grp + 1)
                if tb >= 2 and grp == 0:
                    v.wait_ge(od2[par].h, 256 * (tb // 2))
                    v.wait_ge(nrm.h, tb - 1)
                for nt4 in range(4):
                    nt = grp * 4 + nt4
                    v.tensor_scalar(x1sb[par][:, nt * 512:(nt + 1) * 512],
                                    c.psum[nt4][:, 0:512], c.pcol("bo", nt), 0.0,
                                    ALU.add, ALU.add)
                c.vmark(v, evA.h, 4)
                evA.inc(4)
            v.wait_ge(al.h, 128 * (tb + 1))
            for nt in range(KD):
                v.tensor_tensor(x1sb[par][:, nt * 512:(nt + 1) * 512],
                                x1sb[par][:, nt * 512:(nt + 1) * 512],
                                xnb[par][:, nt * 512:(nt + 1) * 512], ALU.add)
            c.vmark(v, vad.h, 1)
            vad.inc()
            for nt in range(KD):
                v.tensor_copy(x1c[:, nt * 512:(nt + 1) * 512],
                              x1sb[par][:, nt * 512:(nt + 1) * 512])
            c.vmark(v, vcst.h, 1)
            vcst.inc()
            v.wait_ge(smm.h, tb + 1)
            v.tensor_scalar(rm[0:1, 0:512], c.psum[4][0:1, 0:512], 1.0 / D, 0.0,
                            ALU.mult, ALU.add)
            v.tensor_scalar(rv[0:1, 0:512], c.psum[5][0:1, 0:512], 1.0 / D, 0.0,
                            ALU.mult, ALU.add)
            v.tensor_tensor(tmp[0:1, 0:512], rm[0:1, 0:512], rm[0:1, 0:512], ALU.mult)
            v.tensor_tensor(rv[0:1, 0:512], rv[0:1, 0:512], tmp[0:1, 0:512],
                            ALU.subtract).then_inc(vv.h, 1)
            vv.inc()
            v.wait_ge(sq2.h, tb + 1)
            v.reciprocal(r1[0:1, 0:512], rv[0:1, 0:512])
            v.tensor_tensor(r2[0:1, 0:512], rm[0:1, 0:512], r1[0:1, 0:512], ALU.mult)
            v.tensor_scalar(r2[0:1, 0:512], r2[0:1, 0:512], -1.0, 0.0,
                            ALU.mult, ALU.add).then_inc(stv.h, 1)
            stv.inc()
            v.wait_ge(bc.h, tb + 1)
            for nt in range(KD):
                v.tensor_tensor(tmp, x1sb[par][:, nt * 512:(nt + 1) * 512],
                                c.psum[6][:, 0:512], ALU.mult)
                v.tensor_tensor(tmp, tmp, c.psum[7][:, 0:512], ALU.add)
                v.tensor_scalar(h2sb[par][:, nt * 512:(nt + 1) * 512], tmp,
                                c.pcol("n2g", nt), c.pcol("n2b", nt), ALU.mult, ALU.add)
            c.vmark(v, nrm.h, 1)
            nrm.inc()

    phase(c, {"sync": fsync, "gpsimd": fgps, "vector": fvec, "scalar": fscal,
              "tensor": ften})
    c.pop()



def ph_comb1(c):
    c.push()
    xl = [c.region(128, KD * 512, F32, "cb_xl") for _ in range(2)]
    xn1 = [c.region(128, KD * 512, F32, "cb_xn") for _ in range(2)]
    ina = [c.region(128, KD * 512, F32, "cb_a") for _ in range(2)]
    inb = [c.region(128, KD * 512, F32, "cb_b") for _ in range(2)]
    x1c = c.region(128, KD * 512, BF16, "cb_c")
    x2c = c.region(128, KD * 512, BF16, "cb_s")
    rm = c.region(1, 512, F32, "cb_rm")
    rv = c.region(1, 512, F32, "cb_rv")
    r1 = c.region(1, 512, F32, "cb_r1")
    r2 = c.region(1, 512, F32, "cb_r2")
    tmp = c.region(128, 512, F32, "cb_tmp")
    al = c.sem("cbal")
    vad = c.sem("cbvad")
    vcst = c.sem("cbvcst")
    ssq = c.sem("cbssq")
    smm = c.sem("cbsmm")
    vv = c.sem("cbvv")
    sq2 = c.sem("cbsq2")
    stv = c.sem("cbstv")
    bc = c.sem("cbbc")
    nrm = c.sem("cbnrm")
    od2 = [c.sem("cbod0"), c.sem("cbod1")]

    def fsync(sync):
        for tb in range(NTB):
            par = tb % 2
            if tb >= 2:
                sync.wait_ge(nrm.h, tb - 1)
                sync.wait_ge(od2[par].h, 256 * (tb // 2))
            for nt in range(KD):
                sl = slice(nt * 512, (nt + 1) * 512)
                tsl = slice(tb * 512, (tb + 1) * 512)
                sync.dma_start(out=xl[par][:, sl], in_=c.d["sc_xF"][nt][:, tsl]).then_inc(al.h, 16)
                sync.dma_start(out=ina[par][:, sl], in_=c.d["sc_A0F"][nt][:, tsl]).then_inc(al.h, 16)
                sync.dma_start(out=inb[par][:, sl], in_=c.d["sc_F0F"][nt][:, tsl]).then_inc(al.h, 16)
                al.inc(48)
            sync.wait_ge(vad.h, tb + 1)
            for nt in range(KD):
                sync.dma_start(out=c.d["sc_xl1F"][nt][:, tb * 512:(tb + 1) * 512],
                               in_=xl[par][:, nt * 512:(nt + 1) * 512]).then_inc(od2[par].h, 16)
                od2[par].inc(16)
            sync.wait_ge(nrm.h, tb + 1)
            for nt in range(KD):
                sync.dma_start(out=c.d["sc_xn1F"][nt][:, tb * 512:(tb + 1) * 512],
                               in_=xn1[par][:, nt * 512:(nt + 1) * 512]).then_inc(od2[par].h, 16)
                od2[par].inc(16)
        sync.wait_ge(od2[0].h, od2[0].n)
        sync.wait_ge(od2[1].h, od2[1].n)

    def ften(t):
        for tb in range(NTB):
            t.wait_ge(vcst.h, tb + 1)
            t.wait_ge(ssq.h, tb + 1)
            if tb > 0:
                t.wait_ge(stv.h, tb)
            for ki in range(KD):
                t.matmul(c.psum[4][0:1, 0:512], c.sb["onesb"][:, 0:1],
                         x1c[:, ki * 512:(ki + 1) * 512],
                         start=(ki == 0), stop=(ki == KD - 1))
            for ki in range(KD):
                t.matmul(c.psum[5][0:1, 0:512], c.sb["onesb"][:, 0:1],
                         x2c[:, ki * 512:(ki + 1) * 512],
                         start=(ki == 0), stop=(ki == KD - 1))
            t.drain().then_inc(smm.h, 1)
            smm.inc()
            t.wait_ge(stv.h, tb + 1)
            if tb > 0:
                t.wait_ge(nrm.h, tb)
            t.matmul(c.psum[6][:, 0:512], c.sb["onesrow"][0:1, 0:128], r1[0:1, 0:512],
                     start=True, stop=True)
            t.matmul(c.psum[7][:, 0:512], c.sb["onesrow"][0:1, 0:128], r2[0:1, 0:512],
                     start=True, stop=True).then_inc(bc.h, 1)
            bc.inc()

    def fscal(s):
        for tb in range(NTB):
            par = tb % 2
            s.wait_ge(vad.h, tb + 1)
            for nt in range(KD):
                s.activation(x2c[:, nt * 512:(nt + 1) * 512],
                             xl[par][:, nt * 512:(nt + 1) * 512], AF.Square)
            c.smark(s, ssq.h, 1)
            ssq.inc()
            s.wait_ge(vv.h, tb + 1)
            s.activation(rv[0:1, 0:512], rv[0:1, 0:512], AF.Sqrt,
                         bias=c.pcol("c_eps5")[0:1, 0:1], scale=1.0).then_inc(sq2.h, 1)
            sq2.inc()

    def fvec(v):
        for tb in range(NTB):
            par = tb % 2
            v.wait_ge(al.h, 384 * (tb + 1))
            if tb >= 2:
                v.wait_ge(od2[par].h, 256 * (tb // 2))
                v.wait_ge(nrm.h, tb - 1)
            for nt in range(KD):
                sl = slice(nt * 512, (nt + 1) * 512)
                v.tensor_tensor(xl[par][:, sl], xl[par][:, sl], ina[par][:, sl], ALU.add)
                v.tensor_tensor(xl[par][:, sl], xl[par][:, sl], inb[par][:, sl], ALU.add)
            c.vmark(v, vad.h, 1)
            vad.inc()
            for nt in range(KD):
                v.tensor_copy(x1c[:, nt * 512:(nt + 1) * 512],
                              xl[par][:, nt * 512:(nt + 1) * 512])
            c.vmark(v, vcst.h, 1)
            vcst.inc()
            v.wait_ge(smm.h, tb + 1)
            v.tensor_scalar(rm[0:1, 0:512], c.psum[4][0:1, 0:512], 1.0 / D, 0.0,
                            ALU.mult, ALU.add)
            v.tensor_scalar(rv[0:1, 0:512], c.psum[5][0:1, 0:512], 1.0 / D, 0.0,
                            ALU.mult, ALU.add)
            v.tensor_tensor(tmp[0:1, 0:512], rm[0:1, 0:512], rm[0:1, 0:512], ALU.mult)
            v.tensor_tensor(rv[0:1, 0:512], rv[0:1, 0:512], tmp[0:1, 0:512],
                            ALU.subtract).then_inc(vv.h, 1)
            vv.inc()
            v.wait_ge(sq2.h, tb + 1)
            v.reciprocal(r1[0:1, 0:512], rv[0:1, 0:512])
            v.tensor_tensor(r2[0:1, 0:512], rm[0:1, 0:512], r1[0:1, 0:512], ALU.mult)
            v.tensor_scalar(r2[0:1, 0:512], r2[0:1, 0:512], -1.0, 0.0,
                            ALU.mult, ALU.add).then_inc(stv.h, 1)
            stv.inc()
            v.wait_ge(bc.h, tb + 1)
            for nt in range(KD):
                sl = slice(nt * 512, (nt + 1) * 512)
                v.tensor_tensor(tmp, xl[par][:, sl], c.psum[6][:, 0:512], ALU.mult)
                v.tensor_tensor(xn1[par][:, sl], tmp, c.psum[7][:, 0:512], ALU.add)
            c.vmark(v, nrm.h, 1)
            nrm.inc()

    phase(c, {"sync": fsync, "vector": fvec, "scalar": fscal, "tensor": ften})
    c.pop()



def ph_ffn0b(c):
    c.push()
    xreg = c.region(128, KD * T, BF16, "f0x")
    w1b = [c.region(128, KD * 512, BF16, "f0w1") for _ in range(2)]
    w2b = [c.region(128, 1024, BF16, "f0w2") for _ in range(2)]
    gsb = c.region(128, 32 * 512, BF16, "f0g")
    gx = [c.region(128, 512, F32, "f0gx") for _ in range(2)]
    gt = [c.region(128, 512, F32, "f0gt") for _ in range(2)]
    stage = [c.region(128, 512, F32, "f0s") for _ in range(8)]
    xld = c.sem("f0xld", sw=True)
    w1ld2 = [c.sem("f0w1ld0", sw=True), c.sem("f0w1ld1", sw=True)]
    w1fr = c.sem("f0w1fr")
    w2ld2 = [c.sem("f0w2ld0", sw=True), c.sem("f0w2ld1", sw=True)]
    w2fr = c.sem("f0w2fr")
    mmA = c.sem("f0mmA")
    gdn = c.sem("f0gdn")
    sgd = c.sem("f0sgd")
    mmB = c.sem("f0mmB")
    evB = c.sem("f0evB")
    vad = c.sem("f0vad")
    ald = c.sem("f0ald")
    od2 = [c.sem("f0od0"), c.sem("f0od1")]

    def fgps(g):
        for ki in range(KD):
            g.dma_start(out=xreg[:, ki * T:(ki + 1) * T],
                        in_=c.d["sc_h2F"][ki]).then_inc(xld.h, 16)
            xld.inc(16)
        for tb in range(NTB):
            for ns in range(8):
                u = tb * 8 + ns
                if u >= 2:
                    g.wait_ge(w1fr.h, u - 1)
                for ki in range(KD):
                    g.dma_start(out=w1b[u % 2][:, ki * 512:(ki + 1) * 512],
                                in_=c.d["Wf1"][ki * 128:(ki + 1) * 128,
                                               ns * 512:(ns + 1) * 512]).then_inc(w1ld2[u % 2].h, 16)
                    w1ld2[u % 2].inc(16)
            for ki in range(32):
                u = tb * 32 + ki
                if u >= 2:
                    g.wait_ge(w2fr.h, u - 1)
                g.dma_start(out=w2b[u % 2],
                            in_=c.d["Wf2"][ki * 128:(ki + 1) * 128, :]).then_inc(w2ld2[u % 2].h, 16)
                w2ld2[u % 2].inc(16)

    def ften(t):
        t.wait_ge(xld.h, 16 * KD)
        sA = 0
        for tb in range(NTB):
            if tb > 0:
                t.wait_ge(evB.h, 8 * tb)
            for ns in range(8):
                u = tb * 8 + ns
                t.wait_ge(w1ld2[u % 2].h, 16 * KD * (u // 2 + 1))
                for nt4 in range(4):
                    if sA >= 4:
                        t.wait_ge(gdn.h, sA - 3)
                    bank = c.psum[sA % 4]
                    for ki in range(KD):
                        t.matmul(bank[:, 0:512],
                                 w1b[u % 2][:, ki * 512 + nt4 * 128: ki * 512 + (nt4 + 1) * 128],
                                 xreg[:, ki * T + tb * 512: ki * T + (tb + 1) * 512],
                                 start=(ki == 0), stop=(ki == KD - 1))
                    t.drain().then_inc(mmA.h, 1)
                    mmA.inc()
                    sA += 1
                t.drain().then_inc(w1fr.h, 1)
                w1fr.inc()
            t.wait_ge(gdn.h, 32 * (tb + 1))
            if tb > 0:
                t.wait_ge(evB.h, 8 * tb)
            for ki in range(32):
                u = tb * 32 + ki
                t.wait_ge(w2ld2[u % 2].h, 16 * (u // 2 + 1))
                for mt in range(KD):
                    t.matmul(c.psum[mt][:, 0:512],
                             w2b[u % 2][:, mt * 128:(mt + 1) * 128],
                             gsb[:, ki * 512:(ki + 1) * 512],
                             start=(ki == 0), stop=(ki == 31), skip_group_check=True)
                t.drain().then_inc(w2fr.h, 1)
                w2fr.inc()
            t.drain().then_inc(mmB.h, 1)
            mmB.inc()

    vgx = c.sem("f0vgx")

    def fscal(s):
        sA = 0
        for tb in range(NTB):
            for ns in range(8):
                for nt4 in range(4):
                    s.wait_ge(vgx.h, sA + 1)
                    gxp, gtp = gx[sA % 2], gt[sA % 2]
                    s.activation(gtp, gxp, AF.Square).then_inc(sgd.h, 1)
                    sgd.inc()
                    s.wait_ge(c.sem("f0vu").h, sA + 1)
                    s.activation(gtp, gtp, AF.Tanh, scale=GCONST).then_inc(c.sem("f0st").h, 1)
                    c.sem("f0st").inc()
                    sA += 1

    vu = c.sem("f0vu")
    st_ = c.sem("f0st")

    addb4 = [c.region(128, 512, F32, "f0a4") for _ in range(8)]

    def fvec(v):
        sA = 0
        for tb in range(NTB):
            if tb > 0:
                v.wait_ge(mmB.h, tb)
            for ns in range(8):
                for nt4 in range(4):
                    nt_g = ns * 4 + nt4
                    gxp, gtp = gx[sA % 2], gt[sA % 2]
                    v.wait_ge(mmA.h, sA + 1)
                    if sA >= 2:
                        v.wait_ge(gdn.h, sA - 1)
                    v.tensor_scalar(gxp, c.psum[sA % 4][:, 0:512],
                                    c.pcol("bf1", nt_g), 0.0, ALU.add,
                                    ALU.add).then_inc(vgx.h, 1)
                    vgx.inc()
                    v.wait_ge(sgd.h, sA + 1)
                    v.tensor_tensor(gtp, gtp, gxp, ALU.mult)
                    v.scalar_tensor_tensor(gtp, gtp, 0.044715, gxp, ALU.mult,
                                           ALU.add).then_inc(vu.h, 1)
                    vu.inc()
                    v.wait_ge(st_.h, sA + 1)
                    v.tensor_scalar(gtp, gtp, 1.0, 0.5, ALU.add, ALU.mult)
                    v.tensor_tensor(gsb[:, nt_g * 512:(nt_g + 1) * 512], gtp, gxp,
                                    ALU.mult).then_inc(gdn.h, 1)
                    gdn.inc()
                    sA += 1
            v.wait_ge(mmB.h, tb + 1)
            if tb >= 1:
                v.wait_ge(od2[(tb - 1) % 2].h, 128 * ((tb - 1) // 2 + 1))
            v.wait_ge(ald.h, 128 * (tb + 1))
            for mt in range(KD):
                v.tensor_scalar(stage[mt], c.psum[mt][:, 0:512],
                                c.pcol("bf2", mt), 0.0, ALU.add, ALU.add)
                v.tensor_tensor(stage[mt], stage[mt], addb4[mt], ALU.add)
            c.vmark(v, evB.h, 8)
            evB.inc(8)
            c.vmark(v, vad.h, 1)
            vad.inc()

    def fsync(sync):
        for tb in range(NTB):
            if tb >= 1:
                sync.wait_ge(vad.h, tb)
            for mt in range(KD):
                sync.dma_start(out=addb4[mt],
                               in_=c.d["sc_x1F"][mt][:, tb * 512:(tb + 1) * 512]).then_inc(ald.h, 16)
                ald.inc(16)
            sync.wait_ge(vad.h, tb + 1)
            for mt in range(KD):
                sync.dma_start(out=c.d["sc_A0F"][mt][:, tb * 512:(tb + 1) * 512],
                               in_=stage[mt]).then_inc(od2[tb % 2].h, 16)
                od2[tb % 2].inc(16)
        sync.wait_ge(od2[0].h, od2[0].n)
        sync.wait_ge(od2[1].h, od2[1].n)

    phase(c, {"sync": fsync, "gpsimd": fgps, "vector": fvec, "scalar": fscal,
              "tensor": ften})
    c.pop()



def ph_ffnint(c):
    c.push()
    xreg = c.region(128, KD * T, BF16, "fix")
    wub = [c.region(128, KD * 512, BF16, "fiu") for _ in range(2)]
    wdb = [c.region(128, 1024, BF16, "fid") for _ in range(2)]
    hsb = c.region(128, 24 * 512, BF16, "fih")
    sgb = [c.region(128, 512, BF16, "fisg") for _ in range(2)]
    stage = [c.region(128, 512, F32, "fis") for _ in range(8)]
    xld = c.sem("fixld", sw=True)
    uld2 = [c.sem("fiuld0", sw=True), c.sem("fiuld1", sw=True)]
    ufr = c.sem("fiufr")
    dld2 = [c.sem("fidld0", sw=True), c.sem("fidld1", sw=True)]
    dfr = c.sem("fidfr")
    mmG = c.sem("fimmG")
    sil = c.sem("fisil")
    hdn = c.sem("fihdn")
    mmD = c.sem("fimmD")
    evD = c.sem("fievD")
    od2 = [c.sem("fiod0"), c.sem("fiod1")]

    def fgps(g):
        for ki in range(KD):
            g.dma_start(out=xreg[:, ki * T:(ki + 1) * T],
                        in_=c.d["sc_xnF"][ki]).then_inc(xld.h, 16)
            xld.inc(16)
        for tb in range(NTB):
            for ng in range(6):
                for kind in range(2):
                    u = tb * 12 + ng * 2 + kind
                    if u >= 2:
                        g.wait_ge(ufr.h, u - 1)
                    W = c.d["wgt"] if kind == 0 else c.d["wut"]
                    for ki in range(KD):
                        g.dma_start(out=wub[u % 2][:, ki * 512:(ki + 1) * 512],
                                    in_=W[ki * 128:(ki + 1) * 128,
                                          ng * 512:(ng + 1) * 512]).then_inc(uld2[u % 2].h, 16)
                        uld2[u % 2].inc(16)
            for ki in range(24):
                u = tb * 24 + ki
                if u >= 2:
                    g.wait_ge(dfr.h, u - 1)
                g.dma_start(out=wdb[u % 2],
                            in_=c.d["wdt"][ki * 128:(ki + 1) * 128, :]).then_inc(dld2[u % 2].h, 16)
                dld2[u % 2].inc(16)

    def ften(t):
        t.wait_ge(xld.h, 16 * KD)
        for tb in range(NTB):
            if tb > 0:
                t.wait_ge(evD.h, 8 * tb)
            for ng in range(6):
                ug = tb * 12 + ng * 2
                if ng > 0 or tb > 0:
                    t.wait_ge(hdn.h, 24 * tb + 4 * ng)
                t.wait_ge(uld2[ug % 2].h, 16 * KD * (ug // 2 + 1))
                for nt4 in range(4):
                    for ki in range(KD):
                        t.matmul(c.psum[nt4][:, 0:512],
                                 wub[ug % 2][:, ki * 512 + nt4 * 128: ki * 512 + (nt4 + 1) * 128],
                                 xreg[:, ki * T + tb * 512: ki * T + (tb + 1) * 512],
                                 start=(ki == 0), stop=(ki == KD - 1))
                t.drain().then_inc(ufr.h, 1)
                ufr.inc()
                t.wait_ge(uld2[(ug + 1) % 2].h, 16 * KD * ((ug + 1) // 2 + 1))
                for nt4 in range(4):
                    for ki in range(KD):
                        t.matmul(c.psum[4 + nt4][:, 0:512],
                                 wub[(ug + 1) % 2][:, ki * 512 + nt4 * 128: ki * 512 + (nt4 + 1) * 128],
                                 xreg[:, ki * T + tb * 512: ki * T + (tb + 1) * 512],
                                 start=(ki == 0), stop=(ki == KD - 1))
                t.drain().then_inc(ufr.h, 1)
                ufr.inc()
                t.drain().then_inc(mmG.h, 1)
                mmG.inc()
            t.wait_ge(hdn.h, 24 * (tb + 1))
            t.wait_ge(evD.h, 8 * tb)
            for ki in range(24):
                u = tb * 24 + ki
                t.wait_ge(dld2[u % 2].h, 16 * (u // 2 + 1))
                for mt in range(KD):
                    t.matmul(c.psum[mt][:, 0:512],
                             wdb[u % 2][:, mt * 128:(mt + 1) * 128],
                             hsb[:, ki * 512:(ki + 1) * 512],
                             start=(ki == 0), stop=(ki == 23), skip_group_check=True)
                t.drain().then_inc(dfr.h, 1)
                dfr.inc()
            t.drain().then_inc(mmD.h, 1)
            mmD.inc()

    def fscal(s):
        sg = 0
        for tb in range(NTB):
            for ng in range(6):
                s.wait_ge(mmG.h, tb * 6 + ng + 1)
                for nt4 in range(4):
                    if sg >= 2:
                        s.wait_ge(hdn.h, sg - 1)
                    s.activation(sgb[sg % 2], c.psum[nt4][:, 0:512], AF.Silu,
                                 scale=1.0 / 127.0).then_inc(sil.h, 1)
                    sil.inc()
                    sg += 1


    def fvec(v):
        sg = 0
        for tb in range(NTB):
            if tb > 0:
                v.wait_ge(mmD.h, tb)
            for ng in range(6):
                for nt4 in range(4):
                    v.wait_ge(sil.h, sg + 1)
                    slot = ng * 4 + nt4
                    v.tensor_tensor(hsb[:, slot * 512:(slot + 1) * 512],
                                    sgb[sg % 2], c.psum[4 + nt4][:, 0:512],
                                    ALU.mult).then_inc(hdn.h, 1)
                    hdn.inc()
                    sg += 1
            v.wait_ge(mmD.h, tb + 1)
            if tb >= 1:
                v.wait_ge(od2[(tb - 1) % 2].h, 128 * ((tb - 1) // 2 + 1))
            for mt in range(KD):
                v.tensor_scalar(stage[mt], c.psum[mt][:, 0:512],
                                c.pcol("c_i127sq"), 0.0, ALU.mult,
                                ALU.add).then_inc(evD.h, 1)
                evD.inc()

    def fsync(sync):
        for tb in range(NTB):
            sync.wait_ge(evD.h, 8 * (tb + 1))
            for mt in range(KD):
                sync.dma_start(out=c.d["sc_F0F"][mt][:, tb * 512:(tb + 1) * 512],
                               in_=stage[mt]).then_inc(od2[tb % 2].h, 16)
                od2[tb % 2].inc(16)
        sync.wait_ge(od2[0].h, od2[0].n)
        sync.wait_ge(od2[1].h, od2[1].n)

    phase(c, {"sync": fsync, "gpsimd": fgps, "vector": fvec, "scalar": fscal,
              "tensor": ften})
    c.pop()



def ph_router(c):
    c.push()
    xf = [c.region(128, KD * 512, F32, "roxf") for _ in range(2)]
    xhb = [c.region(128, KD * 512, BF16, "roxh") for _ in range(2)]
    xhf = [c.region(128, KD * 512, F32, "roxhf") for _ in range(2)]
    xlb = [c.region(128, KD * 512, BF16, "roxl") for _ in range(2)]
    gw = c.region(128, 2 * KD * 8, BF16, "rog")
    lg = c.region(8, T, F32, "rolg")
    rtm = c.region(128, 8, F32, "rotm")
    eq1 = c.region(128, 8, F32, "roe1")
    eq2 = c.region(128, 8, F32, "roe2")
    r2m = c.region(128, 8, F32, "ror2")
    rwk = c.region(128, 16, F32, "rowk")
    rst = [c.region(8, 128, F32, "rost") for _ in range(2)]
    ld = c.sem("rold", sw=True)
    xl2 = [c.sem("roxl0", sw=True), c.sem("roxl1", sw=True)]
    mm = c.sem("romm")
    ev = c.sem("roev")
    tr1 = c.sem("rotr1")
    cpd = c.sem("rocpd")
    vm = c.sem("rovm")
    se = c.sem("rose")
    sf = c.sem("rosf")
    tr2 = c.sem("rotr2")
    ev2 = c.sem("roev2")
    od2 = [c.sem("rood0"), c.sem("rood1")]

    def fgps(g):
        for ki in range(KD):
            g.dma_start(out=gw[:, ki * 8:(ki + 1) * 8],
                        in_=c.d["gwt"][ki * 128:(ki + 1) * 128, :]).then_inc(ld.h, 16)
            ld.inc(16)
            g.dma_start(out=gw[:, KD * 8 + ki * 8:KD * 8 + (ki + 1) * 8],
                        in_=c.d["gwt_lo"][ki * 128:(ki + 1) * 128, :]).then_inc(ld.h, 16)
            ld.inc(16)
        for tb in range(NTB):
            if tb >= 2:
                g.wait_ge(c.sem("rovxd").h, tb - 1)
            for ki in range(KD):
                g.dma_start(out=xf[tb % 2][:, ki * 512:(ki + 1) * 512],
                            in_=c.d["sc_xn1F"][ki][:, tb * 512:(tb + 1) * 512]).then_inc(xl2[tb % 2].h, 16)
                xl2[tb % 2].inc(16)

    def ften(t):
        vxd = c.sem("rovxd")
        t.wait_ge(ld.h, 16 * 2 * KD)
        for tb in range(NTB):
            t.wait_ge(vxd.h, tb + 1)
            if tb >= 2:
                t.wait_ge(ev.h, tb - 1)
            for ki in range(KD):
                for gsl, xsl in ((0, xhb), (0, xlb), (KD * 8, xhb)):
                    t.matmul(c.psum[tb % 2][0:8, 0:512],
                             gw[:, gsl + ki * 8:gsl + (ki + 1) * 8],
                             xsl[tb % 2][:, ki * 512:(ki + 1) * 512],
                             start=(ki == 0 and gsl == 0 and xsl is xhb),
                             stop=(ki == KD - 1 and gsl == KD * 8))
            t.drain().then_inc(mm.h, 1)
            mm.inc()
        for tt in range(NTT):
            t.wait_ge(ev.h, tt // 4 + 1)
            if tt >= 2:
                t.wait_ge(cpd.h, tt - 1)
            t.transpose(c.psum[2 + tt % 2][:, 0:8], lg[0:8, tt * 128:(tt + 1) * 128],
                        c.sb["identf"][0:8, 0:8])
            t.drain().then_inc(tr1.h, 1)
            tr1.inc()
            t.wait_ge(sf.h, tt + 1)
            if tt >= 2:
                t.wait_ge(ev2.h, tt - 1)
            t.transpose(c.psum[4 + tt % 2][0:8, 0:128], rtm, c.sb["identf"])
            t.drain().then_inc(tr2.h, 1)
            tr2.inc()

    def fscal(s):
        for tb in range(NTB):
            s.wait_ge(mm.h, tb + 1)
            s.activation(lg[0:8, tb * 512:(tb + 1) * 512], c.psum[tb % 2][0:8, 0:512],
                         AF.Copy).then_inc(ev.h, 1)
            ev.inc()
        for tt in range(NTT):
            s.wait_ge(vm.h, tt + 1)
            s.activation(rtm, rtm, AF.Exp, bias=rwk[:, 1:2],
                         accum_out=rwk[:, 2:3]).then_inc(se.h, 1)
            se.inc()
            s.wait_ge(tr2.h, tt + 1)
            if tt >= 2:
                s.wait_ge(od2[tt % 2].h, 16 * (tt // 2))
            s.activation(rst[tt % 2], c.psum[4 + tt % 2][0:8, 0:128],
                         AF.Copy).then_inc(ev2.h, 1)
            ev2.inc()

    def fvec(v):
        vxd = c.sem("rovxd")
        for tb in range(NTB):
            v.wait_ge(xl2[tb % 2].h, 16 * KD * (tb // 2 + 1))
            if tb >= 2:
                v.wait_ge(mm.h, tb - 1)
            v.tensor_copy(xhb[tb % 2], xf[tb % 2])
            v.tensor_copy(xhf[tb % 2], xhb[tb % 2])
            v.tensor_tensor(xlb[tb % 2], xf[tb % 2], xhf[tb % 2], ALU.subtract)
            c.vmark(v, vxd.h, 1)
            vxd.inc()
        for tt in range(NTT):
            v.wait_ge(tr1.h, tt + 1)
            if tt >= 1:
                v.wait_ge(tr2.h, tt)
            v.tensor_copy(rtm, c.psum[2 + tt % 2][:, 0:8]).then_inc(cpd.h, 1)
            cpd.inc()
            v.reduce_max(rwk[:, 0:1], rtm, axis=AX.X)
            v.tensor_scalar(rwk[:, 1:2], rwk[:, 0:1], -1.0, 0.0, ALU.mult,
                            ALU.add).then_inc(vm.h, 1)
            vm.inc()
            v.wait_ge(se.h, tt + 1)
            v.reciprocal(rwk[:, 3:4], rwk[:, 2:3])
            v.tensor_scalar(rtm, rtm, rwk[:, 3:4], 0.0, ALU.mult, ALU.add)
            v.reduce_max(rwk[:, 4:5], rtm, axis=AX.X)
            v.tensor_scalar(eq1, rtm, rwk[:, 4:5], 0.0, ALU.is_equal, ALU.add)
            v.scalar_tensor_tensor(r2m, eq1, -1e30, rtm, ALU.mult, ALU.add)
            v.reduce_max(rwk[:, 6:7], r2m, axis=AX.X)
            v.tensor_scalar(eq2, r2m, rwk[:, 6:7], 0.0, ALU.is_equal, ALU.add)
            v.tensor_tensor(rwk[:, 7:8], rwk[:, 4:5], rwk[:, 6:7], ALU.add)
            v.tensor_scalar(rwk[:, 7:8], rwk[:, 7:8], 1e-9, 0.0, ALU.add, ALU.add)
            v.reciprocal(rwk[:, 8:9], rwk[:, 7:8])
            v.tensor_tensor(rwk[:, 9:10], rwk[:, 4:5], rwk[:, 8:9], ALU.mult)
            v.tensor_tensor(rwk[:, 10:11], rwk[:, 6:7], rwk[:, 8:9], ALU.mult)
            v.tensor_tensor(rwk[:, 11:12], rwk[:, 9:10], rwk[:, 10:11], ALU.add)
            v.tensor_scalar(rwk[:, 11:12], rwk[:, 11:12], 1e-9, 0.0, ALU.add, ALU.add)
            v.reciprocal(rwk[:, 12:13], rwk[:, 11:12])
            v.tensor_tensor(rwk[:, 13:14], rwk[:, 9:10], rwk[:, 12:13], ALU.mult)
            v.tensor_tensor(rwk[:, 14:15], rwk[:, 10:11], rwk[:, 12:13], ALU.mult)
            v.tensor_scalar(rtm, eq1, rwk[:, 13:14], 0.0, ALU.mult, ALU.add)
            v.scalar_tensor_tensor(rtm, eq2, rwk[:, 14:15], rtm, ALU.mult,
                                   ALU.add).then_inc(sf.h, 1)
            sf.inc()

    def fsync(sync):
        for tt in range(NTT):
            sync.wait_ge(ev2.h, tt + 1)
            sync.dma_start(out=c.d["sc_r"][:, tt * 128:(tt + 1) * 128],
                           in_=rst[tt % 2]).then_inc(od2[tt % 2].h, 16)
            od2[tt % 2].inc(16)
        sync.wait_ge(od2[0].h, od2[0].n)
        sync.wait_ge(od2[1].h, od2[1].n)

    phase(c, {"sync": fsync, "gpsimd": fgps, "vector": fvec, "scalar": fscal,
              "tensor": ften})
    c.pop()



def ph_moe(c):
    c.push()
    gureg = c.region(128, KD * 2048, BF16, "mogu")
    dnreg = c.region(128, KD * 1024, BF16, "modn")
    xb = [c.region(128, KD * 512, BF16, "mox") for _ in range(2)]
    hsb = c.region(128, KD * 512, BF16, "moh")
    sgb = [c.region(128, 512, BF16, "mosg") for _ in range(2)]
    rb = [c.region(128, 512, F32, "morb") for _ in range(2)]
    fprev = [c.region(128, 512, F32, "mofp") for _ in range(2)]
    stage = [c.region(128, 512, F32, "mos") for _ in range(8)]
    wld = c.sem("mowld", sw=True)
    xl2 = [c.sem("moxl0", sw=True), c.sem("moxl1", sw=True)]
    rbl2 = [c.sem("morbl0"), c.sem("morbl1")]
    fpl = c.sem("mofpl")
    mmG = c.sem("mommG")
    sil = c.sem("mosil")
    hdn = c.sem("mohdn")
    mmD = c.sem("mommD")
    acc = c.sem("moacc")
    od2 = [c.sem("mood0"), c.sem("mood1")]

    NE = 8

    def fgps(g):
        for e in range(NE):
            if e >= 1:
                g.wait_ge(mmD.h, e * NTB)
            for ki in range(KD):
                g.dma_start(out=gureg[:, ki * 2048:(ki + 1) * 2048],
                            in_=c.d["gu"][e, ki * 128:(ki + 1) * 128, :]).then_inc(wld.h, 16)
                wld.inc(16)
                g.dma_start(out=dnreg[:, ki * 1024:(ki + 1) * 1024],
                            in_=c.d["dn"][e, ki * 128:(ki + 1) * 128, :]).then_inc(wld.h, 16)
                wld.inc(16)
            for tb in range(NTB):
                et = e * NTB + tb
                if et >= 2:
                    g.wait_ge(mmD.h, et - 1)
                for ki in range(KD):
                    g.dma_start(out=xb[tb % 2][:, ki * 512:(ki + 1) * 512],
                                in_=c.d["sc_xn1F"][ki][:, tb * 512:(tb + 1) * 512]).then_inc(xl2[tb % 2].h, 16)
                    xl2[tb % 2].inc(16)

    def fsync(sync):
        for e in range(NE):
            for tb in range(NTB):
                et = e * NTB + tb
                if et >= 2:
                    sync.wait_ge(acc.h, 8 * (et - 1))
                sync.dma_start(out=rb[tb % 2],
                               in_=c.d["sc_r"][e:e + 1, tb * 512:(tb + 1) * 512]
                               .broadcast_to([128, 512])).then_inc(rbl2[tb % 2].h, 16)
                rbl2[tb % 2].inc(16)
                for mt in range(KD):
                    if e > 0:
                        if et * 8 + mt >= 2:
                            sync.wait_ge(acc.h, 8 * et + mt - 1)
                        sync.dma_start(out=fprev[mt % 2],
                                       in_=c.d["sc_F1F"][mt][:, tb * 512:(tb + 1) * 512]).then_inc(fpl.h, 16)
                        fpl.inc(16)
                    sync.wait_ge(acc.h, 8 * et + mt + 1)
                    sync.dma_start(out=c.d["sc_F1F"][mt][:, tb * 512:(tb + 1) * 512],
                                   in_=stage[mt]).then_inc(od2[et % 2].h, 16)
                    od2[et % 2].inc(16)
        sync.wait_ge(od2[0].h, od2[0].n)
        sync.wait_ge(od2[1].h, od2[1].n)

    def ften(t):
        for e in range(NE):
            t.wait_ge(wld.h, 16 * 2 * KD * (e + 1))
            for tb in range(NTB):
                et = e * NTB + tb
                t.wait_ge(xl2[tb % 2].h, 16 * KD * (et // 2 + 1))
                for grp in range(2):
                    if et > 0 or grp > 0:
                        t.wait_ge(hdn.h, 8 * et + 4 * grp)
                    if grp == 0 and et > 0:
                        t.wait_ge(acc.h, 8 * et)
                    for nt4 in range(4):
                        nt = grp * 4 + nt4
                        for ki in range(KD):
                            t.matmul(c.psum[nt4][:, 0:512],
                                     gureg[:, ki * 2048 + nt * 128: ki * 2048 + (nt + 1) * 128],
                                     xb[tb % 2][:, ki * 512:(ki + 1) * 512],
                                     start=(ki == 0), stop=(ki == KD - 1))
                        for ki in range(KD):
                            t.matmul(c.psum[4 + nt4][:, 0:512],
                                     gureg[:, ki * 2048 + 1024 + nt * 128: ki * 2048 + 1024 + (nt + 1) * 128],
                                     xb[tb % 2][:, ki * 512:(ki + 1) * 512],
                                     start=(ki == 0), stop=(ki == KD - 1))
                    t.drain().then_inc(mmG.h, 1)
                    mmG.inc()
                t.wait_ge(hdn.h, 8 * (et + 1))
                for ki in range(KD):
                    for mt in range(KD):
                        t.matmul(c.psum[mt][:, 0:512],
                                 dnreg[:, ki * 1024 + mt * 128:(ki * 1024 + (mt + 1) * 128)],
                                 hsb[:, ki * 512:(ki + 1) * 512],
                                 start=(ki == 0), stop=(ki == KD - 1),
                                 skip_group_check=True)
                t.drain().then_inc(mmD.h, 1)
                mmD.inc()

    def fscal(s):
        sg = 0
        for e in range(NE):
            for tb in range(NTB):
                et = e * NTB + tb
                for grp in range(2):
                    s.wait_ge(mmG.h, 2 * et + grp + 1)
                    for nt4 in range(4):
                        if sg >= 2:
                            s.wait_ge(hdn.h, sg - 1)
                        s.activation(sgb[sg % 2], c.psum[nt4][:, 0:512],
                                     AF.Silu).then_inc(sil.h, 1)
                        sil.inc()
                        sg += 1

    def fvec(v):
        sg = 0
        for e in range(NE):
            for tb in range(NTB):
                et = e * NTB + tb
                if et > 0:
                    v.wait_ge(mmD.h, et)
                for grp in range(2):
                    for nt4 in range(4):
                        nt = grp * 4 + nt4
                        v.wait_ge(sil.h, sg + 1)
                        v.tensor_tensor(hsb[:, nt * 512:(nt + 1) * 512],
                                        sgb[sg % 2], c.psum[4 + nt4][:, 0:512],
                                        ALU.mult).then_inc(hdn.h, 1)
                        hdn.inc()
                        sg += 1
                v.wait_ge(mmD.h, et + 1)
                v.wait_ge(rbl2[tb % 2].h, 16 * (et // 2 + 1))
                if et >= 1:
                    v.wait_ge(od2[(et - 1) % 2].h, 128 * ((et - 1) // 2 + 1))
                for mt in range(KD):
                    v.tensor_tensor(stage[mt], c.psum[mt][:, 0:512], rb[tb % 2],
                                    ALU.mult)
                    if e > 0:
                        v.wait_ge(fpl.h, 16 * ((e - 1) * NTB * KD + tb * KD + mt + 1))
                        v.tensor_tensor(stage[mt], stage[mt], fprev[mt % 2],
                                        ALU.add)
                    c.vmark(v, acc.h, 1)
                    acc.inc()

    phase(c, {"sync": fsync, "gpsimd": fgps, "vector": fvec, "scalar": fscal,
              "tensor": ften})
    c.pop()



def ph_out(c):
    c.push()
    osb = [c.region(128, KD * 512, F32, "oo") for _ in range(2)]
    inb = [c.region(128, KD * 512, F32, "oi") for _ in range(2)]
    ost = [c.region(128, 4 * 1024, F32, "os") for _ in range(2)]
    al = c.sem("ooal")
    vad = c.sem("oovad")
    trd = c.sem("ootrd")
    evt = c.sem("ooevt")
    od2 = [c.sem("oood0"), c.sem("oood1")]

    def fsync(sync):
        for tb in range(NTB):
            par = tb % 2
            if tb >= 2:
                sync.wait_ge(vad.h, tb - 1)
            for nt in range(KD):
                sl = slice(nt * 512, (nt + 1) * 512)
                tsl = slice(tb * 512, (tb + 1) * 512)
                sync.dma_start(out=osb[par][:, sl], in_=c.d["sc_accF"][nt][:, tsl]).then_inc(al.h, 16)
                sync.dma_start(out=inb[par][:, sl], in_=c.d["sc_F1F"][nt][:, tsl]).then_inc(al.h, 16)
                al.inc(32)
            sync.wait_ge(evt.h, 8 * (tb + 1))
            for tsub in range(4):
                sync.dma_start(out=c.d["out"][tb * 512 + tsub * 128: tb * 512 + (tsub + 1) * 128, :],
                               in_=ost[par][:, tsub * 1024:(tsub + 1) * 1024]).then_inc(od2[par].h, 16)
                od2[par].inc(16)
        sync.wait_ge(od2[0].h, od2[0].n)
        sync.wait_ge(od2[1].h, od2[1].n)

    def fvec(v):
        for tb in range(NTB):
            par = tb % 2
            v.wait_ge(al.h, 256 * (tb + 1))
            for nt in range(KD):
                sl = slice(nt * 512, (nt + 1) * 512)
                v.tensor_tensor(osb[par][:, sl], osb[par][:, sl], inb[par][:, sl], ALU.add)
            c.vmark(v, vad.h, 1)
            vad.inc()
            if tb >= 2:
                v.wait_ge(od2[par].h, 64 * (tb // 2))
            for mt in range(KD):
                v.wait_ge(trd.h, 8 * tb + mt + 1)
                bank = c.psum[mt % 2]
                for tsub in range(4):
                    v.tensor_copy(ost[par][:, tsub * 1024 + mt * 128: tsub * 1024 + (mt + 1) * 128],
                                  bank[:, tsub * 128:(tsub + 1) * 128])
                c.vmark(v, evt.h, 1)
                evt.inc()

    def ften(t):
        for tb in range(NTB):
            par = tb % 2
            t.wait_ge(vad.h, tb + 1)
            for mt in range(KD):
                if 8 * tb + mt >= 2:
                    t.wait_ge(evt.h, 8 * tb + mt - 1)
                bank = c.psum[mt % 2]
                for tsub in range(4):
                    t.transpose(bank[:, tsub * 128:(tsub + 1) * 128],
                                osb[par][:, mt * 512 + tsub * 128: mt * 512 + (tsub + 1) * 128],
                                c.sb["identf"])
                t.drain().then_inc(trd.h, 1)
                trd.inc()

    phase(c, {"sync": fsync, "vector": fvec, "tensor": ften})
    c.pop()



def ph_lin(c):
    BL = 256
    NBL = T // BL
    NC = BL // 64
    HALFPI = float(np.pi / 2)
    c.push()
    P = [c.region(128, KD * BL, F32, f"liP{i}") for i in range(9)]
    Pint = c.region(128, KD * BL, I32, "liPi")
    PintF = Pint.bitcast(F32)
    QH = c.region(128, KD * BL, BF16, "liQH")
    QC = c.region(128, KD * BL, BF16, "liQC")
    KH = [c.region(128, KD * BL, BF16, f"liKH{t}") for t in range(4)]
    vb = c.region(128, KD * BL, BF16, "livb")
    VT = c.region(128, 8 * NC * 64, BF16, "liVT")
    KTs = c.region(128, 64, BF16, "liKT")
    KTsw = c.region(128, 64, BF16, "liKTw")
    GM = [c.region(128, NC * 64, BF16, f"liGM{i}") for i in range(8)]
    SS = [[c.region(128, 64, F32, "liS") for _ in range(2)] for _ in range(8)]
    SSb = [[c.region(128, 64, BF16, "liSb") for _ in range(2)] for _ in range(8)]
    SWr = [[c.region(128, 64, F32, "liSW") for _ in range(2)] for _ in range(8)]
    ABe = c.region(128, 32, F32, "liABe")
    ABw = c.region(128, 64, F32, "liABw")
    AB = c.region(128, 192, F32, "liAB")
    Tw = c.region(128, 64, F32, "liTw")
    yac = [[c.region(128, 256, F32, "liyac") for _ in range(8)] for _ in range(2)]
    lil = c.sem("lil")
    ve = c.sem("live")
    se = c.sem("lise")
    vready = c.sem("livr")
    vbc = c.sem("livbc")
    vte = c.sem("livte")
    tvt = c.sem("litvt")
    tG = c.sem("litG")
    gev = c.sem("ligev")
    ktt = c.sem("liktt")
    ktv = c.sem("liktv")
    tpe = c.sem("litpe")
    vtr = c.sem("livtr")
    yev = c.sem("liyev")
    lod2 = [c.sem("lilod0"), c.sem("lilod1")]
    sinit = c.sem("lisinit")
    SWMAP = [1, 0, 3, 2]
    rst = c.sb["scanrst"][:, 0:KD * BL]

    def tau_sub(v, tau):
        off = 7 + 16 * tau
        v.tensor_tensor(P[2].rearrange("p (a b) -> p a b", b=64),
                        P[3].rearrange("p (a b) -> p a b", b=64),
                        P[3][:, off::64].unsqueeze(2).broadcast_to([128, KD * BL // 64, 64]),
                        ALU.subtract)
        v.tensor_scalar(P[2], P[2], 80.0, 0.0, ALU.min, ALU.add)

    def build_script():
        sc = []
        A = sc.append
        A(("v", lambda v: v.tensor_copy(vb, P[0])))

        def th_clip(v):
            for ki in range(KD):
                sl = slice(ki * BL, (ki + 1) * BL)
                v.tensor_scalar(P[1][:, sl], P[1][:, sl], c.pcol("thb", ki), HALFPI,
                                ALU.add, ALU.min)
            v.tensor_scalar(P[1], P[1], -HALFPI, 0.0, ALU.max, ALU.add)
        A(("v", th_clip))
        A(("s", lambda s: s.activation(P[2], P[1], AF.Square)))

        def lam_scans(v):
            for ki in range(KD):
                sl = slice(ki * BL, (ki + 1) * BL)
                v.tensor_scalar(P[2][:, sl], P[2][:, sl], c.pcol("eta", ki),
                                c.pcol("lamb", ki), ALU.mult, ALU.add)
            v.tensor_tensor_scan(P[3], rst, P[2], 0.0, ALU.mult, ALU.add)
            v.tensor_tensor_scan(P[0], rst, P[1], 0.0, ALU.mult, ALU.add)
            v.tensor_copy(ABe.rearrange("p (a b) -> p a b", b=4),
                          P[0][:, 63::64].rearrange("p (a b) -> p a b", b=4))
            v.tensor_scalar(P[2], P[0], INV2PI, 0.5, ALU.mult, ALU.add)
            v.tensor_copy(Pint, P[2])
            v.tensor_copy(P[4], Pint)
            v.tensor_tensor(PintF, P[4], P[2], ALU.is_gt)
            v.tensor_tensor(P[4], P[4], PintF, ALU.subtract)
            v.scalar_tensor_tensor(P[4], P[4], -TWOPI, P[0], ALU.mult, ALU.add)
        A(("v", lam_scans))
        A(("s", lambda s: s.activation(P[1], P[4], AF.Sin)))

        def trig2(v):
            v.tensor_scalar(P[4], P[0], 1.0, HALFPI, ALU.mult, ALU.add)
            v.tensor_scalar(P[2], P[4], INV2PI, 0.5, ALU.mult, ALU.add)
            v.tensor_copy(Pint, P[2])
            v.tensor_copy(P[0], Pint)
            v.tensor_tensor(PintF, P[0], P[2], ALU.is_gt)
            v.tensor_tensor(P[0], P[0], PintF, ALU.subtract)
            v.scalar_tensor_tensor(P[0], P[0], -TWOPI, P[4], ALU.mult, ALU.add)
        A(("v", trig2))
        A(("s", lambda s: s.activation(P[4], P[0], AF.Sin)))

        def sins_phik(v):
            v.tensor_scalar(P[0], P[1], c.pcol("sgnri"), 0.0, ALU.mult, ALU.add)
            v.tensor_scalar(P[2], P[7], 0.0, 0.0, ALU.min, ALU.add)
        A(("v", sins_phik))
        A(("s", lambda s: s.activation(P[2], P[2], AF.Exp)))

        def phik_fin(v):
            v.scalar_tensor_tensor(P[1], P[7], 0.0, P[2], ALU.max, ALU.add)
            v.tensor_scalar(P[2], P[8], 0.0, 0.0, ALU.min, ALU.add)
        A(("v", phik_fin))
        A(("s", lambda s: s.activation(P[2], P[2], AF.Exp)))

        def rot(v):
            v.scalar_tensor_tensor(P[7], P[8], 0.0, P[2], ALU.max, ALU.add)
            v.tensor_tensor(P[2], P[5], P[4], ALU.mult)
            v.tensor_tensor(P[6], P[6], P[0], ALU.mult)
            v.tensor_tensor(P[5], P[2], P[6], ALU.add)
            v.tensor_tensor(P[2], P[1], P[4], ALU.mult)
            v.tensor_tensor(P[7], P[7], P[0], ALU.mult)
            v.tensor_tensor(P[1], P[2], P[7], ALU.add)
            v.tensor_tensor(P[2].rearrange("p (a b) -> p a b", b=16),
                            P[3].rearrange("p (a b) -> p a b", b=16),
                            P[3][:, 7::16].unsqueeze(2).broadcast_to([128, KD * BL // 16, 16]),
                            ALU.subtract)
        A(("v", rot))
        A(("s", lambda s: s.activation(P[4], P[2], AF.Exp, scale=-1.0)))
        A(("v", lambda v: (v.tensor_tensor(QH, P[5], P[4], ALU.mult),
                           tau_sub(v, 0))))
        for tau in range(4):
            A(("s", lambda s: s.activation(P[4], P[2], AF.Exp)))
            if tau < 3:
                A(("v", lambda v, tau=tau: (v.tensor_tensor(KH[tau], P[1], P[4], ALU.mult),
                                            tau_sub(v, tau + 1))))
            else:
                A(("v", lambda v: v.tensor_tensor(KH[3], P[1], P[4], ALU.mult)))
        A(("s", lambda s: s.activation(P[4], P[3], AF.Exp, scale=-1.0)))

        def qc_ends(v):
            v.tensor_tensor(QC, P[5], P[4], ALU.mult)
            v.tensor_scalar(ABw[:, 32:64], ABe, INV2PI, 0.5, ALU.mult, ALU.add)
            v.tensor_copy(Pint[:, 0:32], ABw[:, 32:64])
            v.tensor_copy(ABw[:, 0:32], Pint[:, 0:32])
            v.tensor_tensor(AB[:, 0:32], ABw[:, 0:32], ABw[:, 32:64], ALU.is_gt)
            v.tensor_tensor(ABw[:, 0:32], ABw[:, 0:32], AB[:, 0:32], ALU.subtract)
            v.scalar_tensor_tensor(ABw[:, 0:32], ABw[:, 0:32], -TWOPI, ABe,
                                   ALU.mult, ALU.add)
        A(("v", qc_ends))
        A(("s", lambda s: s.activation(AB[:, 160:192], ABw[:, 0:32], AF.Sin)))

        def cose_in(v):
            v.tensor_scalar(ABw[:, 32:64], ABw[:, 0:32], 1.0, HALFPI, ALU.mult, ALU.add)
            v.tensor_scalar(AB[:, 0:32], ABw[:, 32:64], INV2PI, 0.5, ALU.mult, ALU.add)
            v.tensor_copy(Pint[:, 0:32], AB[:, 0:32])
            v.tensor_copy(AB[:, 96:128], Pint[:, 0:32])
            v.tensor_tensor(AB[:, 64:96], AB[:, 96:128], AB[:, 0:32], ALU.is_gt)
            v.tensor_tensor(AB[:, 96:128], AB[:, 96:128], AB[:, 64:96], ALU.subtract)
            v.scalar_tensor_tensor(ABw[:, 0:32], AB[:, 96:128], -TWOPI, ABw[:, 32:64],
                                   ALU.mult, ALU.add)
        A(("v", cose_in))
        A(("s", lambda s: s.activation(AB[:, 128:160], ABw[:, 0:32], AF.Sin)))

        def lend(v):
            v.tensor_copy(ABw[:, 0:32].rearrange("p (a b) -> p a b", b=4),
                          P[3][:, 63::64].rearrange("p (a b) -> p a b", b=4))
            v.tensor_copy(ABw[:, 32:64].rearrange("p (a b) -> p a b", b=4),
                          P[3][:, 55::64].rearrange("p (a b) -> p a b", b=4))
            v.tensor_tensor(ABw[:, 32:64], ABw[:, 0:32], ABw[:, 32:64], ALU.subtract)
        A(("v", lend))
        A(("s", lambda s: (s.activation(ABw[:, 0:32], ABw[:, 0:32], AF.Exp, scale=-1.0),
                           s.activation(ABw[:, 32:64], ABw[:, 32:64], AF.Exp, scale=-1.0))))

        def abfin(v):
            sinE, cosE = AB[:, 160:192], AB[:, 128:160]
            v.tensor_tensor(AB[:, 0:32], ABw[:, 0:32], cosE, ALU.mult)
            v.tensor_tensor(AB[:, 96:128], ABw[:, 32:64], cosE, ALU.mult)
            v.tensor_tensor(ABw[:, 0:32], ABw[:, 0:32], sinE, ALU.mult)
            v.tensor_tensor(ABw[:, 32:64], ABw[:, 32:64], sinE, ALU.mult)
            v.tensor_scalar(AB[:, 32:64], ABw[:, 0:32], c.pcol("sgnri"), -1.0,
                            ALU.mult, ALU.mult)
            v.tensor_scalar(AB[:, 64:96], ABw[:, 0:32], c.pcol("sgnri"), 1.0,
                            ALU.mult, ALU.mult)
            v.tensor_scalar(AB[:, 160:192], ABw[:, 32:64], c.pcol("sgnri"), -1.0,
                            ALU.mult, ALU.mult)
            v.tensor_scalar(AB[:, 128:160], ABw[:, 32:64], c.pcol("sgnri"), 1.0,
                            ALU.mult, ALU.mult)
        A(("v", abfin))
        return sc

    script = build_script()
    NV = sum(1 for k, _ in script if k == "v")
    NS = sum(1 for k, _ in script if k == "s")

    def fsync(sync):
        for bi in range(NBL):
            bc = slice(bi * BL, (bi + 1) * BL)
            if bi >= 1:
                sync.wait_ge(vready.h, bi)
            for (reg, key) in [(P[0], "sc_vF"), (P[1], "sc_loraF"),
                               (P[5], "sc_qF"), (P[7], "sc_kF")]:
                sync.dma_start(out=reg.rearrange("p (a t) -> p a t", a=KD),
                               in_=c.d[key].rearrange("a p t -> p a t")[:, :, bc]).then_inc(lil.h, 16)
                lil.inc(16)
            for (reg, key) in [(P[6], "sc_qF"), (P[8], "sc_kF")]:
                for w in range(4):
                    sw = SWMAP[w]
                    sync.dma_start(
                        out=reg[32 * w:32 * (w + 1), :].rearrange("p (a t) -> p a t", a=KD),
                        in_=c.d[key].rearrange("a p t -> p a t")[32 * sw:32 * (sw + 1), :, bc]
                    ).then_inc(lil.h, 16)
                    lil.inc(16)
            for hp in range(8):
                sync.wait_ge(yev.h, 8 * bi + hp + 1)
                sync.dma_start(out=c.d["sc_attnF"][hp][:, bc],
                               in_=yac[bi % 2][hp]).then_inc(lod2[bi % 2].h, 16)
                lod2[bi % 2].inc(16)
        sync.wait_ge(lod2[0].h, lod2[0].n)
        sync.wait_ge(lod2[1].h, lod2[1].n)

    def fvec(v):
        v.memset(c.psum[0][:, 0:512], 0.0)
        v.memset(c.psum[1][:, 0:512], 0.0)
        for hp in range(8):
            for par in range(2):
                v.memset(SS[hp][par], 0.0)
                v.memset(SSb[hp][par], 0.0)
                v.memset(SWr[hp][par], 0.0)
        c.vmark(v, sinit.h, 1)
        sinit.inc()
        for bi in range(NBL):
            ut0 = bi * 32
            if bi >= 1:
                v.wait_ge(tpe.h, 32 * bi)
            v.wait_ge(lil.h, 192 * (bi + 1))
            ns_done = 0
            first = True
            for kind, fn in script:
                if kind == "v":
                    if ns_done:
                        v.wait_ge(se.h, bi * NS + ns_done)
                    fn(v)
                    c.vmark(v, ve.h, 1)
                    ve.inc()
                    if first:
                        c.vmark(v, vbc.h, 1)
                        vbc.inc()
                        first = False
                else:
                    ns_done += 1
            c.vmark(v, vready.h, 1)
            vready.inc()
            v.wait_ge(tvt.h, bi + 1)
            for i in range(8):
                bank = c.psum[6 + i // 4]
                v.tensor_copy(VT.bitcast(I32)[:, i * 128:(i + 1) * 128],
                              bank.bitcast(I32)[:, (i % 4) * 128:((i % 4) + 1) * 128])
            c.vmark(v, vte.h, 1)
            vte.inc()
            for hp in range(8):
                v.wait_ge(tG.h, 8 * bi + hp + 1)
                v.tensor_tensor(GM[hp].rearrange("p (a b) -> p a b", b=64),
                                c.psum[hp % 2][:, 0:NC * 64].rearrange("p (a b) -> p a b", b=64),
                                c.sb["maskG"][:, 0:64].unsqueeze(1).broadcast_to([128, NC, 64]),
                                ALU.mult)
                c.vmark(v, gev.h, 1)
                gev.inc()
            for cci in range(NC):
                gcc = bi * NC + cci
                par = gcc % 2
                for hp in range(8):
                    ib = [2, 6][hp % 2]
                    yb = c.psum[4 + hp % 2][:, (hp // 2) * 64:(hp // 2 + 1) * 64]
                    ut = ut0 + cci * 8 + hp
                    v.wait_ge(ktt.h, ut + 1)
                    v.tensor_copy(KTs.bitcast(I32), c.psum[hp % 2].bitcast(I32)[:, 384:416])
                    v.tensor_copy(KTsw.bitcast(I32)[:, 0:16],
                                  c.psum[hp % 2].bitcast(I32)[:, 400:416])
                    v.tensor_copy(KTsw.bitcast(I32)[:, 16:32],
                                  c.psum[hp % 2].bitcast(I32)[:, 384:400])
                    c.vmark(v, ktv.h, 1)
                    ktv.inc()
                    v.wait_ge(tpe.h, ut + 1)
                    inc_ap = c.psum[ib][:, (hp // 2) * 64:(hp // 2 + 1) * 64]
                    incw_ap = c.psum[ib + 1][:, (hp // 2) * 64:(hp // 2 + 1) * 64]
                    col = hp * 4 + cci
                    a2 = AB[:, col:col + 1]
                    b2n = AB[:, 32 + col:32 + col + 1]
                    b2p = AB[:, 64 + col:64 + col + 1]
                    a3 = AB[:, 96 + col:96 + col + 1]
                    b3p = AB[:, 128 + col:128 + col + 1]
                    b3n = AB[:, 160 + col:160 + col + 1]
                    nxt = 1 - par
                    v.tensor_scalar(Tw, SWr[hp][par], b2n, 0.0, ALU.mult, ALU.add)
                    v.scalar_tensor_tensor(Tw, SS[hp][par], a2, Tw, ALU.mult, ALU.add)
                    v.scalar_tensor_tensor(Tw, inc_ap, a3, Tw, ALU.mult, ALU.add)
                    v.scalar_tensor_tensor(SS[hp][nxt], incw_ap, b3n, Tw, ALU.mult, ALU.add)
                    v.tensor_copy(SSb[hp][nxt], SS[hp][nxt])
                    v.tensor_scalar(Tw, SS[hp][par], b2p, 0.0, ALU.mult, ALU.add)
                    v.scalar_tensor_tensor(Tw, SWr[hp][par], a2, Tw, ALU.mult, ALU.add)
                    v.scalar_tensor_tensor(Tw, incw_ap, a3, Tw, ALU.mult, ALU.add)
                    v.scalar_tensor_tensor(SWr[hp][nxt], inc_ap, b3p, Tw, ALU.mult, ALU.add)
                    if cci == 0 and bi >= 2:
                        v.wait_ge(lod2[bi % 2].h, 128 * (bi // 2))
                    v.tensor_copy(yac[bi % 2][hp][:, cci * 64:(cci + 1) * 64], yb)
                    if cci == NC - 1:
                        c.vmark(v, yev.h, 1)
                        yev.inc()
                    c.vmark(v, vtr.h, 1)
                    vtr.inc()

    def fscal(s):
        for bi in range(NBL):
            nv_done = 0
            for kind, fn in script:
                if kind == "v":
                    nv_done += 1
                else:
                    s.wait_ge(ve.h, bi * NV + nv_done)
                    fn(s)
                    c.smark(s, se.h, 1)
                    se.inc()

    def ften(t):
        t.wait_ge(sinit.h, 1)
        for bi in range(NBL):
            ut0 = bi * 32
            t.wait_ge(vbc.h, bi + 1)
            if bi >= 1:
                t.wait_ge(vtr.h, 32 * bi)
                t.wait_ge(vte.h, bi)
            for hp in range(8):
                for cci in range(NC):
                    i = hp * NC + cci
                    bank = c.psum[6 + hp // 4].bitcast(BF16)
                    scol = (i % 16) * 64
                    for h2 in range(2):
                        h = 2 * hp + h2
                        src = vb[64 * (h % 2):64 * (h % 2) + 64,
                                 (h // 2) * BL + cci * 64:(h // 2) * BL + (cci + 1) * 64]
                        idt = c.sb["identb"][64 * (h % 2):64 * (h % 2) + 64,
                                             64 * (h % 2):64 * (h % 2) + 64]
                        t.transpose(bank[64 * h2:64 * (h2 + 1), scol:scol + 64], src, idt)
            t.drain().then_inc(tvt.h, 1)
            tvt.inc()
            t.wait_ge(vready.h, bi + 1)
            if bi >= 1:
                t.wait_ge(ktv.h, 32 * bi)
            for hp in range(8):
                if 8 * bi + hp >= 2:
                    t.wait_ge(gev.h, 8 * bi + hp - 1)
                bank = c.psum[hp % 2]
                for cci in range(NC):
                    for h2 in range(2):
                        rq = slice(64 * h2, 64 * (h2 + 1))
                        c0 = hp * BL + cci * 64
                        for tau in range(4):
                            t.matmul(bank[64 * h2:64 * h2 + 16 * tau + 8,
                                          cci * 64 + 16 * tau: cci * 64 + 16 * tau + 8],
                                     KH[tau][rq, c0:c0 + 16 * tau + 8],
                                     QH[rq, c0 + 16 * tau:c0 + 16 * tau + 8],
                                     start=True, stop=True, skip_group_check=True)
                            t.matmul(bank[64 * h2:64 * h2 + 16 * (tau + 1),
                                          cci * 64 + 16 * tau + 8: cci * 64 + 16 * (tau + 1)],
                                     KH[tau][rq, c0:c0 + 16 * (tau + 1)],
                                     QH[rq, c0 + 16 * tau + 8:c0 + 16 * (tau + 1)],
                                     start=True, stop=True, skip_group_check=True)
                t.drain().then_inc(tG.h, 1)
                tG.inc()
            t.wait_ge(vte.h, bi + 1)
            t.wait_ge(gev.h, 8 * (bi + 1))
            for cci in range(NC):
                gcc = bi * NC + cci
                par = gcc % 2
                for hp in range(8):
                    ib = [2, 6][hp % 2]
                    ys = slice((hp // 2) * 64, (hp // 2 + 1) * 64)
                    yb = c.psum[4 + hp % 2]
                    ut = ut0 + cci * 8 + hp
                    if ut >= 2:
                        t.wait_ge(vtr.h, ut - 1)
                        t.wait_ge(ktv.h, ut - 1)
                    vslot = VT[:, (hp * NC + cci) * 64:(hp * NC + cci + 1) * 64]
                    for h2 in range(2):
                        rq = slice(64 * h2, 64 * (h2 + 1))
                        ccols = slice(hp * BL + cci * 64, hp * BL + (cci + 1) * 64)
                        idt = c.sb["identb"][rq, rq]
                        t.transpose(c.psum[hp % 2].bitcast(BF16)[rq, 768:832],
                                    KH[3][rq, ccols], idt)
                        t.matmul(yb[64 * h2:64 * (h2 + 1), ys],
                                 vslot[rq, :], GM[hp][rq, cci * 64:(cci + 1) * 64],
                                 start=True, stop=False, skip_group_check=True)
                        t.matmul(yb[64 * h2:64 * (h2 + 1), ys],
                                 SSb[hp][par][rq, :], QC[rq, ccols],
                                 start=False, stop=True, skip_group_check=True)
                    t.drain().then_inc(ktt.h, 1)
                    ktt.inc()
                    t.wait_ge(ktv.h, ut + 1)
                    for h2 in range(2):
                        rq = slice(64 * h2, 64 * (h2 + 1))
                        t.matmul(c.psum[ib][rq, ys],
                                 KTs[rq, :], vslot[rq, :],
                                 start=True, stop=True, skip_group_check=True)
                        t.matmul(c.psum[ib + 1][rq, ys],
                                 KTsw[rq, :], vslot[rq, :],
                                 start=True, stop=True, skip_group_check=True)
                    t.drain().then_inc(tpe.h, 1)
                    tpe.inc()

    phase(c, {"sync": fsync, "vector": fvec, "scalar": fscal, "tensor": ften})
    c.pop()



def ph_flash(c):
    c.push()
    GW = 512
    KBG = GW // 128
    qh = [c.region(64, T, BF16, "flq") for _ in range(2)]
    kh = [c.region(64, T, BF16, "flk") for _ in range(2)]
    vh = [c.region(64, T, F32, "flv") for _ in range(2)]
    vhb = [c.region(64, T, BF16, "flvb") for _ in range(2)]
    va = [c.region(128, 32 * 128, BF16, "flva") for _ in range(2)]
    Pb = [c.region(128, 2 * GW, BF16, "flPb") for _ in range(2)]
    Pt = [c.region(128, 2 * GW, BF16, "flPt") for _ in range(2)]
    c.sem("flvax")
    rw = c.region(128, 52, F32, "flrw")
    accY = c.region(128, 66, F32, "flacc")
    ysb = c.region(128, 64, F32, "flysb")
    yts = [c.region(64, 128, F32, "flyts") for _ in range(2)]
    hl2 = [c.sem("flhl0", sw=True), c.sem("flhl1", sw=True)]
    vtp = c.sem("flvtp")
    vae = c.sem("flvae")
    zs = c.sem("flzs")
    dv = c.sem("fldv")
    ex = c.sem("flex")
    trs = c.sem("fltrs")
    pte = c.sem("flpte")
    av = c.sem("flav")
    fy = c.sem("flfy")
    fyt = c.sem("flfyt")
    fye = c.sem("flfye")
    od2 = [c.sem("flod0"), c.sem("flod1")]

    sched = []
    for h in range(16):
        for qt in range(FL_NQT or NTT):
            nkb = qt + 1
            ngrp = (nkb + 7) // 8
            sched.append((h, qt, nkb, ngrp))
    GTOT = sum(s[3] for s in sched)

    def fgps(g):
        for h in range(FL_NH):
            if h >= 2:
                g.wait_ge(av.h, sum(s[3] for s in sched if s[0] < h - 1))
            src_q = c.d["sc_q1F"][h // 2][64 * (h % 2):64 * (h % 2) + 64, :]
            src_k = c.d["sc_k1F"][h // 2][64 * (h % 2):64 * (h % 2) + 64, :]
            src_v = c.d["sc_v1F"][h // 2][64 * (h % 2):64 * (h % 2) + 64, :]
            g.dma_start(out=qh[h % 2][:], in_=src_q).then_inc(hl2[h % 2].h, 16)
            g.dma_start(out=kh[h % 2][:], in_=src_k).then_inc(hl2[h % 2].h, 16)
            g.dma_start(out=vh[h % 2][:], in_=src_v).then_inc(hl2[h % 2].h, 16)
            hl2[h % 2].inc(48)

    def ften(t):
        gi = 0
        hq = 0
        for h in range(FL_NH):
            par = h % 2
            eg = EGH[h]
            t.wait_ge(hl2[h % 2].h, 48 * (h // 2 + 1))
            t.wait_ge(c.sem("flvc").h, h + 1)
            if h >= 1:
                t.wait_ge(pte.h, sum(s[3] for s in sched if s[0] < h))
            for tt in range(NTT):
                if tt % 8 == 0:
                    W = h * 4 + tt // 8
                    if W >= 1:
                        t.wait_ge(c.sem("flvax").h, W)
                slot = tt % 16
                bank = c.psum[2 + slot % 2]
                t.matmul(bank[:, (slot // 2) * 64:(slot // 2 + 1) * 64],
                         vhb[par][:, tt * 128:(tt + 1) * 128],
                         c.sb["identb"][0:64, 0:64],
                         start=True, stop=True, skip_group_check=True)
                if tt % 8 == 7 or tt == NTT - 1:
                    t.drain().then_inc(vtp.h, 1)
                    vtp.inc()
            for qt in range(FL_NQT or NTT):
                nkb = qt + 1
                ngrp = (nkb + 7) // 8
                t.wait_ge(vae.h, h + 1)
                for sg in range(ngrp):
                    nt_ = min(8, nkb - sg * 8)
                    nA = min(4, nt_)
                    nB = nt_ - nA
                    if gi >= 1:
                        t.wait_ge(ex.h, gi)
                    t.matmul(c.psum[0][:, 0:nA * 128],
                             qh[par][:, qt * 128:(qt + 1) * 128],
                             kh[par][:, sg * 1024: sg * 1024 + nA * 128],
                             start=True, stop=True, skip_group_check=True)
                    if nB:
                        t.matmul(c.psum[1][:, 0:nB * 128],
                                 qh[par][:, qt * 128:(qt + 1) * 128],
                                 kh[par][:, sg * 1024 + 512: sg * 1024 + 512 + nB * 128],
                                 start=True, stop=True, skip_group_check=True)
                    t.drain().then_inc(zs.h, 1)
                    zs.inc()
                    t.wait_ge(ex.h, gi + 1)
                    if gi >= 2:
                        t.wait_ge(pte.h, gi - 1)
                    ptb = c.psum[2 + gi % 2].bitcast(BF16)
                    for kb in range(nt_):
                        t.transpose(ptb[:, kb * 128:(kb + 1) * 128],
                                    Pb[gi % 2][:, kb * 128:(kb + 1) * 128],
                                    c.sb["identb"])
                    t.drain().then_inc(trs.h, 1)
                    trs.inc()
                    t.wait_ge(pte.h, gi + 1)
                    if gi >= 1:
                        t.wait_ge(c.sem("flavc").h, gi)
                    for kb in range(nt_):
                        kbg = sg * 8 + kb
                        t.matmul(c.psum[4][:, 0:66],
                                 Pt[gi % 2][:, kb * 128:(kb + 1) * 128],
                                 va[par][:, kbg * 128: kbg * 128 + 66],
                                 start=(kb == 0), stop=(kb == nt_ - 1),
                                 skip_group_check=True)
                    t.drain().then_inc(av.h, 1)
                    av.inc()
                    gi += 1
                t.wait_ge(fy.h, hq + 1)
                if hq >= 1:
                    t.wait_ge(fye.h, hq)
                t.transpose(c.psum[5][0:64, 0:128], ysb, c.sb["identf"])
                t.drain().then_inc(fyt.h, 1)
                fyt.inc()
                hq += 1

    def fvec(v):
        gi = 0
        hq = 0
        vc = c.sem("flvc")
        for h in range(FL_NH):
            par = h % 2
            eg = EGH[h]
            sl = ALIBI_SLOPES[h]
            v.wait_ge(hl2[h % 2].h, 48 * (h // 2 + 1))
            if h >= 2:
                v.wait_ge(av.h, sum(s[3] for s in sched if s[0] < h - 1))
            v.tensor_copy(vhb[par][:], vh[par][:])
            c.vmark(v, vc.h, 1)
            vc.inc()
            vax = c.sem("flvax")
            for tt in range(NTT):
                v.wait_ge(vtp.h, h * 4 + tt // 8 + 1)
                slot = tt % 16
                bank = c.psum[2 + slot % 2]
                rr = tt % eg if eg > 1 else 0
                ucol = c.sb["uflash"][:, h * 4 + rr: h * 4 + rr + 1]
                v.tensor_scalar(va[par][:, tt * 128:tt * 128 + 64],
                                bank[:, (slot // 2) * 64:(slot // 2 + 1) * 64],
                                ucol, 0.0, ALU.mult, ALU.add)
                v.tensor_scalar(va[par][:, tt * 128 + 64:tt * 128 + 65],
                                ucol, 1.0, 0.0, ALU.mult, ALU.add)
                if tt % 8 == 7 or tt == NTT - 1:
                    c.vmark(v, vax.h, 1)
                    vax.inc()
            c.vmark(v, vae.h, 1)
            vae.inc()
            for qt in range(FL_NQT or NTT):
                nkb = qt + 1
                ngrp = (nkb + 7) // 8
                for sg in range(ngrp):
                    nt_ = min(8, nkb - sg * 8)
                    nA = min(4, nt_)
                    nB = nt_ - nA
                    v.wait_ge(zs.h, gi + 1)
                    if sg == ngrp - 1:
                        lb = c.psum[1] if nB else c.psum[0]
                        lc = (nB - 1 if nB else nA - 1) * 128
                        v.tensor_tensor(lb[:, lc:lc + 128], lb[:, lc:lc + 128],
                                        c.sb["maskF"], ALU.add)
                    rb = 24 * (gi % 2)
                    v.reduce_max(rw[:, rb:rb + nA].unsqueeze(2),
                                 c.psum[0][:, 0:nA * 128].rearrange(
                                     "p (a b) -> p a b", b=128), axis=AX.X)
                    if nB:
                        v.reduce_max(rw[:, rb + nA:rb + nt_].unsqueeze(2),
                                     c.psum[1][:, 0:nB * 128].rearrange(
                                         "p (a b) -> p a b", b=128), axis=AX.X)
                    v.scalar_tensor_tensor(rw[:, rb:rb + nt_], rw[:, rb:rb + nt_],
                                           float(sl * sg * 1024),
                                           c.sb["shflash"][:, h * 8:h * 8 + nt_],
                                           ALU.add, ALU.add)
                    v.reduce_max(rw[:, rb + 8:rb + 9], rw[:, rb:rb + nt_], axis=AX.X)
                    if sg == 0:
                        v.tensor_copy(rw[:, 49:50], rw[:, rb + 8:rb + 9])
                        v.tensor_copy(rw[:, 48:49], rw[:, rb + 8:rb + 9])
                    else:
                        v.tensor_tensor(rw[:, 49:50], rw[:, 48:49], rw[:, rb + 8:rb + 9],
                                        ALU.max)
                        v.tensor_tensor(rw[:, rb + 11:rb + 12], rw[:, 48:49],
                                        rw[:, 49:50], ALU.subtract)
                        v.tensor_copy(rw[:, 48:49], rw[:, 49:50])
                    nch = (nt_ + eg - 1) // eg
                    for e in range(nch):
                        rho = sg * 1024 + e * eg * 128 + eg * 64
                        v.tensor_scalar(rw[:, rb + 12 + e:rb + 13 + e], rw[:, 49:50],
                                        -1.0, float(sl * rho), ALU.mult, ALU.add)
                    c.vmark(v, dv.h, 1)
                    dv.inc()
                    v.wait_ge(trs.h, gi + 1)
                    v.tensor_copy(Pt[gi % 2].bitcast(I32)[:, 0:nt_ * 64],
                                  c.psum[2 + gi % 2].bitcast(I32)[:, 0:nt_ * 64])
                    c.vmark(v, pte.h, 1)
                    pte.inc()
                    avc = c.sem("flavc")
                    v.wait_ge(av.h, gi + 1)
                    if sg == 0:
                        v.tensor_copy(accY, c.psum[4][:, 0:66])
                    else:
                        v.wait_ge(c.sem("flre").h, gi + 1)
                        v.scalar_tensor_tensor(accY, accY, rw[:, rb + 22:rb + 23],
                                               c.psum[4][:, 0:66],
                                               ALU.mult, ALU.add)
                    c.vmark(v, avc.h, 1)
                    avc.inc()
                    gi += 1
                v.reciprocal(rw[:, 50:51], accY[:, 64:65])
                v.tensor_scalar(ysb, accY[:, 0:64], rw[:, 50:51], 0.0,
                                ALU.mult, ALU.add)
                c.vmark(v, fy.h, 1)
                fy.inc()
                v.wait_ge(fyt.h, hq + 1)
                if hq >= 2:
                    v.wait_ge(od2[hq % 2].h, 16 * (hq // 2))
                v.tensor_copy(yts[hq % 2][:], c.psum[5][0:64, 0:128])
                c.vmark(v, fye.h, 1)
                fye.inc()
                hq += 1

    def fscal(s):
        gi = 0
        re = c.sem("flre")
        for h in range(FL_NH):
            eg = EGH[h]
            for qt in range(FL_NQT or NTT):
                nkb = qt + 1
                ngrp = (nkb + 7) // 8
                for sg in range(ngrp):
                    nt_ = min(8, nkb - sg * 8)
                    nA = min(4, nt_)
                    rb = 24 * (gi % 2)
                    s.wait_ge(dv.h, gi + 1)
                    if sg >= 1:
                        s.activation(rw[:, rb + 22:rb + 23], rw[:, rb + 11:rb + 12],
                                     AF.Exp)
                    c.smark(s, re.h, 1)
                    re.inc()
                    nch = (nt_ + eg - 1) // eg
                    for e in range(nch):
                        ce0 = e * eg * 128
                        wch = min(eg * 128, nt_ * 128 - ce0)
                        bk, bc = (0, ce0) if ce0 < nA * 128 else (1, ce0 - nA * 128)
                        s.activation(Pb[gi % 2][:, ce0:ce0 + wch],
                                     c.psum[bk][:, bc:bc + wch],
                                     AF.Exp, bias=rw[:, rb + 12 + e:rb + 13 + e])
                    c.smark(s, ex.h, 1)
                    ex.inc()
                    gi += 1

    def fsync(sync):
        hq = 0
        for h in range(FL_NH):
            for qt in range(FL_NQT or NTT):
                sync.wait_ge(fye.h, hq + 1)
                dest = c.d["sc_y1F"][h // 2][64 * (h % 2):64 * (h % 2) + 64,
                                             qt * 128:(qt + 1) * 128]
                sync.dma_start(out=dest, in_=yts[hq % 2][:]).then_inc(od2[hq % 2].h, 16)
                od2[hq % 2].inc(16)
                hq += 1
        sync.wait_ge(od2[0].h, od2[0].n)
        sync.wait_ge(od2[1].h, od2[1].n)

    phase(c, {"sync": fsync, "gpsimd": fgps, "vector": fvec, "scalar": fscal,
              "tensor": ften})
    c.pop()



PHASES = None


def build(com, pcols, dev_until=None, dev_external=False):
    nc = bass.Bass("TRN2", target_bir_lowering=False)
    es = ExitStack()
    phases = [ph_setup, ph_ln0, ph_qkv0, ph_lora, ph_lin, ph_wo, ph_ffn0b,
              ph_ffnint, ph_comb1, ph_qkv1, ph_flash, ph_proj1, ph_router,
              ph_moe, ph_out]
    with es:
        c = Ctx(nc, es, dev_external)
        c.pcols = pcols
        declare_tensors(c, com, dev_external)
        setup_regions(c)
        c.block = es.enter_context(nc.Block())
        for ph in phases:
            ph(c)
            if dev_until is not None and ph.__name__ == dev_until:
                break
    return nc


def kernel(**inputs):
    com, pcols, xs = prep_inputs(inputs)
    nc = build(com, pcols)
    in_maps = [dict(com, x=xs[i]) for i in range(8)]
    res = run_bass_kernel_spmd(nc, in_maps, core_ids=list(range(8)))
    return np.stack([np.asarray(r["out"]) for r in res.results], axis=0)
